# version 03
# baseline (speedup 1.0000x reference)
import math
from contextlib import ExitStack

import numpy as np
import ml_dtypes

import concourse.bass as bass
import concourse.mybir as mybir
from concourse.bass_utils import run_bass_kernel_spmd

F32 = mybir.dt.float32
BF16 = mybir.dt.bfloat16
I32 = mybir.dt.int32
AF = mybir.ActivationFunctionType
ALU = mybir.AluOpType
AX = mybir.AxisListType
BF = ml_dtypes.bfloat16

S = 4096
D = 2048
DK = 128
NH = 8
F = 8192
FH = F // 128
GP = 128 // FH
TOWN = S // 2
NT = S // 128
NTO = TOWN // 128
ND = D // 128
NQT = S // 128
NQG = S // 512
EPS_LN = 1e-5
EPS_F32 = float(np.finfo(np.float32).eps)
SQ4 = 128.0 ** (-0.25)
ISQ128 = 1.0 / math.sqrt(128.0)
ISQ64 = 1.0 / math.sqrt(float(FH))
TWO_PI = 2.0 * math.pi
NEG_BIG = -1.0e30

ENGS = ("sync", "vector", "scalar", "tensor", "gpsimd")


class Sem:
    __slots__ = ("h", "n", "name")

    def __init__(self, h, name):
        self.h, self.n, self.name = h, 0, name


_CHAIN_ENGS = ("vector", "scalar", "gpsimd")
_NO_CHAIN_METHODS = {"dma_start", "nop", "engine_nop"}


def _ap_name(a):
    t = getattr(a, "tensor", None)
    return getattr(t, "name", None) if t is not None else None


def _rw(method, args, kw):
    writes, reads = set(), set()
    aps = [a for a in args if _ap_name(a)]
    if method in ("memset", "iota"):
        if aps:
            writes.add(_ap_name(aps[0]))
    else:
        if aps:
            writes.add(_ap_name(aps[0]))
        for a in aps[1:]:
            reads.add(_ap_name(a))
    for k, v in kw.items():
        n = _ap_name(v)
        if n is None:
            continue
        if k == "accum_out":
            writes.add(n)
        else:
            reads.add(n)
    return writes, reads


class Rec:

    chain_sems = {}

    def __init__(self):
        self.q = {e: [] for e in ENGS}
        self.lw = {e: {} for e in _CHAIN_ENGS}
        self.lr = {e: {} for e in _CHAIN_ENGS}
        self.fence = {e: 0 for e in _CHAIN_ENGS}

    def wait(self, eng, sem, val):
        if val > 0:
            self.q[eng].append(("w", sem.h, int(val)))

    def op(self, eng, method, *args, sem=None, inc=1, **kw):
        chain = None
        if eng in _CHAIN_ENGS and method not in _NO_CHAIN_METHODS:
            cs = Rec.chain_sems[eng]
            writes, reads = _rw(method, args, kw)
            need = 0
            for n in reads:
                need = max(need, self.lw[eng].get(n, 0))
            for n in writes:
                need = max(need, self.lw[eng].get(n, 0),
                           self.lr[eng].get(n, 0))
            if need > self.fence[eng]:
                self.q[eng].append(("w", cs.h, int(need)))
                self.fence[eng] = need
            cs.n += 1
            chain = cs
            for n in writes:
                self.lw[eng][n] = cs.n
            for n in reads:
                self.lr[eng][n] = max(self.lr[eng].get(n, 0), cs.n)
        if sem is not None and sem is chain:
            sem = None
        if sem is not None:
            sem.n += inc
        self.q[eng].append(("o", method, args, kw,
                            (sem.h if sem else None), inc,
                            (chain.h if chain else None)))
        return chain.n if chain is not None else (
            sem.n if sem is not None else None)

    def call(self, eng, fn):
        self.q[eng].append(("c", fn))

    def flush(self, nc):
        if not any(self.q[e] for e in ENGS):
            return
        with nc.Block() as block:
            for name in ENGS:
                items = self.q[name]
                if not items:
                    continue

                def body(eng, items=items):
                    for it in items:
                        if it[0] == "w":
                            eng.wait_ge(it[1], it[2])
                        elif it[0] == "c":
                            it[1](eng)
                        else:
                            _, method, args, kw, semh, inc, chainh = it
                            ins = getattr(eng, method)(*args, **kw)
                            if semh is not None:
                                ins.then_inc(semh, inc)
                            if chainh is not None:
                                ins.then_inc(chainh, 1)

                getattr(block, name)(body)


def bcol(t, col=0):
    return t[:, col:col + 1]


def bcast(t, n, col=0):
    return bass.AP(t, col, [[t.shape[1], 128], [0, n]])


def build_program(upto=None, dbg=(), b4_units=None, b4_pa_only=False, b4_groups=None, b4_noav=False, b4_nokq=False, b4_nob=False, b4_bstage=0, b4_nobm=False, b4_nobp=False):
    nc = bass.Bass("TRN2", target_bir_lowering=False)
    dbg = set(dbg)

    x_full = nc.dram_tensor("x_full", [S, D], F32, kind="ExternalInput")
    x_own = nc.dram_tensor("x_own", [TOWN, D], F32, kind="ExternalInput")
    x_own_tr = nc.dram_tensor("x_own_tr", [D, TOWN], F32, kind="ExternalInput")
    wq_t = nc.dram_tensor("wq_t", [D, NH * DK], F32, kind="ExternalInput")
    wq_bh = nc.dram_tensor("wq_bh", [128, NH], F32, kind="ExternalInput")
    rms_gc = nc.dram_tensor("rms_gc", [128, ND], F32, kind="ExternalInput")
    wo_t = nc.dram_tensor("wo_t", [D, D], F32, kind="ExternalInput")
    wo_bc = nc.dram_tensor("wo_bc", [128, ND], F32, kind="ExternalInput")
    w1_t = nc.dram_tensor("w1_t", [D, F], F32, kind="ExternalInput")
    w3_t = nc.dram_tensor("w3_t", [D, F], F32, kind="ExternalInput")
    w2_s = nc.dram_tensor("w2_s", [128, (128 // GP) * D], F32, kind="ExternalInput")
    h128_i = nc.dram_tensor("h128_i", [128, 128], BF16, kind="ExternalInput")
    bd_i = nc.dram_tensor("bd_i", [128, 128], BF16, kind="ExternalInput")
    identb_i = nc.dram_tensor("identb_i", [128, 128], BF16, kind="ExternalInput")
    identf_i = nc.dram_tensor("identf_i", [128, 128], F32, kind="ExternalInput")
    onesf_i = nc.dram_tensor("onesf_i", [1, 128], F32, kind="ExternalInput")
    maskq_i = nc.dram_tensor("maskq_i", [4 * 128, 512], F32, kind="ExternalInput")
    maskt_i = nc.dram_tensor("maskt_i", [4 * 128, 512], F32, kind="ExternalInput")
    coloff_i = nc.dram_tensor("coloff_i", [1, 1], I32, kind="ExternalInput")

    out_t = nc.dram_tensor("out_t", [D, TOWN], F32, kind="ExternalOutput")

    xnT_sp = nc.dram_tensor("xnT_sp", [128, ND * TOWN], BF16)
    qr_dram = nc.dram_tensor("qr_dram", [NH, 128, S], BF16)
    v_dram = nc.dram_tensor("v_dram", [NH, 128, S], BF16)
    s1_dram = nc.dram_tensor("s1_dram", [FH, 128, TOWN], BF16)
    w2f_dram = nc.dram_tensor("w2f_dram", [ND, FH, 128, 128], BF16)
    wob_dram = nc.dram_tensor("wob_dram", [D, D], BF16)
    o_cc_in = nc.dram_tensor("o_cc_in", [NH * 128, S], BF16)
    o_cc_out = nc.dram_tensor("o_cc_out", [2 * NH * 128, S], BF16)

    dbg_outs = {}

    def dbg_out(name, shape, dtype=F32):
        dbg_outs[name] = nc.dram_tensor(name, shape, dtype, kind="ExternalOutput")
        return dbg_outs[name]

    with ExitStack() as ctx:
        e = ctx.enter_context

        def sem(name):
            return Sem(e(nc.semaphore(name)), name)

        sL = sem("sL")
        sSp = sem("sSp")
        sT = sem("sT")
        sT2 = sem("sT2")
        sCC = sem("sCC")
        sOut = sem("sOut")
        Rec.chain_sems = {"vector": sem("cV"), "scalar": sem("cA"),
                          "gpsimd": sem("cG")}
        sV = sV2 = Rec.chain_sems["vector"]
        sA = sA2 = Rec.chain_sems["scalar"]
        sG = Rec.chain_sems["gpsimd"]
        sLx = [sem("sLx0"), sem("sLx1")]
        sLu = [sem("sLu0"), sem("sLu1")]
        sLxn = sem("sLxn")
        sLb6 = sem("sLb6")
        sLog = sem("sLog")
        sLwo6 = sem("sLwo6")
        sLw26 = sem("sLw26")
        sLwf = sem("sLwf")
        sLw2m = sem("sLw2m")
        sLwom = sem("sLwom")
        sSpO = [sem("sSpO0"), sem("sSpO1")]
        sSpS1 = sem("sSpS1")
        sSpW2 = sem("sSpW2")
        sSpWo = sem("sSpWo")

        h128b = e(nc.sbuf_tensor("h128b", [128, 128], BF16))
        bdb = e(nc.sbuf_tensor("bdb", [128, 128], BF16))
        identb = e(nc.sbuf_tensor("identb", [128, 128], BF16))
        identf = e(nc.sbuf_tensor("identf", [128, 128], F32))
        onesf = e(nc.sbuf_tensor("onesf", [1, 128], F32))
        rmsg = e(nc.sbuf_tensor("rmsg", [128, ND], F32))
        wqbias = e(nc.sbuf_tensor("wqbias", [128, NH], F32))
        wobias = e(nc.sbuf_tensor("wobias", [128, ND], F32))
        coloff = e(nc.sbuf_tensor("coloff", [1, 1], I32))
        cIsq128 = e(nc.sbuf_tensor("cIsq128", [128, 1], F32))
        cIsq64 = e(nc.sbuf_tensor("cIsq64", [128, 1], F32))

        r = Rec()
        for dst, src in ((h128b, h128_i), (bdb, bd_i), (identb, identb_i),
                         (identf, identf_i), (onesf, onesf_i), (rmsg, rms_gc),
                         (wqbias, wq_bh), (wobias, wo_bc), (coloff, coloff_i)):
            r.op("sync", "dma_start", dst[:], src[:], sem=sL, inc=16)
        r.op("vector", "memset", cIsq128[:], ISQ128)
        r.op("vector", "memset", cIsq64[:], ISQ64)
        r.wait("sync", sL, sL.n)
        r.flush(nc)

        with ExitStack() as cEarly:
            eE = cEarly.enter_context
            cosT = eE(nc.sbuf_tensor("cosT", [128, S], F32))
            sinT = eE(nc.sbuf_tensor("sinT", [128, S], F32))

            with ExitStack() as c1:
                e1 = c1.enter_context
                jcol = e1(nc.sbuf_tensor("jcol", [128, 1], F32))
                ifcol = e1(nc.sbuf_tensor("ifcol", [128, 1], F32))
                angw = e1(nc.sbuf_tensor("angw", [128, S], F32))
                uw = e1(nc.sbuf_tensor("uw", [128, S], F32))
                kacc = e1(nc.sbuf_tensor("kacc", [128, S], F32))
                cmpw = e1(nc.sbuf_tensor("cmpw", [128, S], F32))
                cNegLn = e1(nc.sbuf_tensor("cNegLn", [128, 1], F32))

                r = Rec()
                r.op("gpsimd", "iota", jcol[:, 0:1], [[1, 1]],
                     channel_multiplier=1, allow_small_or_imprecise_dtypes=True)
                r.op("gpsimd", "iota", angw[:, :], [[1, S]],
                     channel_multiplier=0, allow_small_or_imprecise_dtypes=True,
                     sem=sG)
                r.op("vector", "memset", cNegLn[:], -math.log(10000.0) / 64.0)
                r.wait("vector", sG, sG.n)
                w = r.op("vector", "tensor_single_scalar", cmpw[:, 0:1],
                         jcol[:], float(FH), ALU.is_ge, sem=sV)
                r.wait("vector", sV, w)
                r.op("vector", "scalar_tensor_tensor", jcol[:], cmpw[:, 0:1],
                     -float(FH), jcol[:], ALU.mult, ALU.add, sem=sV)
                r.wait("scalar", sV, sV.n)
                r.op("scalar", "activation", ifcol[:], jcol[:], AF.Exp,
                     scale=bcol(cNegLn), sem=sA)
                r.wait("vector", sA, sA.n)
                r.op("vector", "tensor_tensor", angw[:], angw[:],
                     bcast(ifcol, S), ALU.mult)
                r.op("vector", "tensor_single_scalar", angw[0:64, :],
                     angw[0:64, :], math.pi / 2.0, ALU.add)
                r.op("vector", "tensor_single_scalar", uw[:], angw[:],
                     1.0 / TWO_PI, ALU.mult)
                r.op("vector", "tensor_single_scalar", uw[:], uw[:], 0.5,
                     ALU.add)
                r.op("vector", "memset", kacc[:], 0.0)
                for b in (512, 256, 128, 64, 32, 16, 8, 4, 2, 1):
                    r.op("vector", "tensor_single_scalar", cmpw[:], uw[:],
                         float(b), ALU.is_ge)
                    r.op("vector", "scalar_tensor_tensor", uw[:], cmpw[:],
                         -float(b), uw[:], ALU.mult, ALU.add)
                    r.op("vector", "scalar_tensor_tensor", kacc[:], cmpw[:],
                         float(b), kacc[:], ALU.mult, ALU.add)
                r.op("vector", "scalar_tensor_tensor", angw[:], kacc[:],
                     -TWO_PI, angw[:], ALU.mult, ALU.add)
                r.op("vector", "tensor_single_scalar", angw[:], angw[:],
                     3.141592, ALU.min)
                r.op("vector", "tensor_single_scalar", angw[:], angw[:],
                     -3.141592, ALU.max, sem=sV)
                r.wait("scalar", sV, sV.n)
                r.op("scalar", "activation", uw[:], angw[:], AF.Sin, sem=sA)
                r.wait("vector", sA, sA.n)
                sc_ = r.op("vector", "tensor_single_scalar", uw[:], uw[:],
                           SQ4, ALU.mult, sem=sV)
                r.wait("vector", sV, sc_)
                r.op("vector", "tensor_copy", cosT[0:64, :], uw[0:64, :])
                r.op("vector", "tensor_copy", cosT[64:128, :], uw[0:64, :])
                r.op("vector", "tensor_copy", sinT[0:64, :], uw[64:128, :])
                r.op("vector", "tensor_copy", sinT[64:128, :], uw[64:128, :],
                     sem=sV)
                r.flush(nc)

            if "tables" in dbg:
                o = dbg_out("dbg_tables", [128, S])
                r = Rec()
                r.op("sync", "dma_start", o[:], cosT[:], sem=sOut, inc=16)
                r.wait("sync", sOut, sOut.n)
                r.flush(nc)
            if upto == "B1":
                return nc, dbg_outs

            xnrsT = [eE(nc.sbuf_tensor(f"xnrsT{k}", [128, S], BF16))
                     for k in range(ND)]

            with ExitStack() as c2:
                e2 = c2.enter_context
                xstg = [e2(nc.sbuf_tensor(f"xstg{i}", [128, D], F32))
                        for i in range(2)]
                sqdump = e2(nc.sbuf_tensor("sqdump", [128, D], BF16))
                xnb = e2(nc.sbuf_tensor("xnb", [128, D], BF16))
                xnstrip = e2(nc.sbuf_tensor("xnstrip", [128, D], BF16))
                s1c = e2(nc.sbuf_tensor("s1c", [128, 1], F32))
                s2c = e2(nc.sbuf_tensor("s2c", [128, 1], F32))
                meanc = e2(nc.sbuf_tensor("meanc", [128, 1], F32))
                varc = e2(nc.sbuf_tensor("varc", [128, 1], F32))
                r1c = e2(nc.sbuf_tensor("r1c", [128, 1], F32))
                uc = e2(nc.sbuf_tensor("uc", [128, 1], F32))
                rstdc = e2(nc.sbuf_tensor("rstdc", [128, 1], F32))
                gc = e2(nc.sbuf_tensor("gc", [128, 1], F32))
                cc = e2(nc.sbuf_tensor("cc", [128, 1], F32))
                bcl = e2(nc.sbuf_tensor("bcl", [128, 1], F32))
                trp = [e2(nc.psum_tensor(f"trp{i}", [128, 128], BF16))
                       for i in range(2)]

                r = Rec()
                ntot = NT + NTO
                norm_done = []
                ev_done = []
                trp_guard = [0, 0]
                spill_guard = 0
                for i in range(ntot):
                    own = i >= NT
                    ti = i - NT if own else i
                    src = x_own if own else x_full
                    slot = xstg[i % 2]
                    if i >= 2:
                        r.wait("sync", sA, norm_done[i - 2])
                    lsem = sLx[i % 2]
                    r.op("sync", "dma_start", slot[:],
                         src[ti * 128:(ti + 1) * 128, :], sem=lsem, inc=16)
                    ld = lsem.n
                    r.wait("vector", lsem, ld)
                    vv = r.op("vector", "reduce_sum", s1c[:], slot[:], AX.X,
                              sem=sV)
                    r.wait("scalar", lsem, ld)
                    if i >= 1:
                        r.wait("scalar", sV2, ev_done[i - 1])
                    sq = r.op("scalar", "activation", sqdump[:], slot[:],
                              AF.Square, accum_out=s2c[:, 0:1], sem=sA)
                    r.wait("vector", sA, sq)

                    def vchain(method, *args):
                        w = r.op("vector", method, *args, sem=sV)
                        r.wait("vector", sV, w)
                        return w

                    r.wait("vector", sV, vv)
                    vchain("tensor_single_scalar", meanc[:], s1c[:],
                           1.0 / D, ALU.mult)
                    vchain("tensor_single_scalar", varc[:], s2c[:],
                           1.0 / D, ALU.mult)
                    vchain("tensor_tensor", uc[:], meanc[:], meanc[:],
                           ALU.mult)
                    vchain("tensor_tensor", varc[:], varc[:], uc[:],
                           ALU.subtract)
                    vchain("tensor_single_scalar", r1c[:], varc[:],
                           EPS_LN, ALU.add)
                    vchain("reciprocal", r1c[:], r1c[:])
                    if own:
                        cols = vchain("tensor_copy", uc[:], r1c[:])
                    else:
                        vchain("tensor_tensor", uc[:], varc[:], r1c[:],
                               ALU.mult)
                        vchain("tensor_single_scalar", uc[:], uc[:],
                               EPS_F32, ALU.add)
                        cols = vchain("reciprocal", uc[:], uc[:])
                    r.wait("scalar", sV, cols)
                    r.op("scalar", "activation", rstdc[:], r1c[:], AF.Sqrt)
                    sqr = r.op("scalar", "activation", gc[:], uc[:], AF.Sqrt,
                               sem=sA)
                    r.wait("vector", sA, sqr)
                    if own:
                        ccv = r.op("vector", "tensor_copy", cc[:], rstdc[:],
                                   sem=sV)
                    else:
                        ccv = r.op("vector", "tensor_tensor", cc[:], rstdc[:],
                                   gc[:], ALU.mult, sem=sV)
                    r.wait("vector", sV, ccv)
                    cb = r.op("vector", "scalar_tensor_tensor", bcl[:],
                              meanc[:], -1.0, cc[:], ALU.mult, ALU.mult,
                              sem=sV)
                    r.wait("scalar", sV, cb)
                    nd_ = r.op("scalar", "activation", xnb[:], slot[:],
                               AF.Identity, bias=bcol(bcl), scale=bcol(cc),
                               sem=sA)
                    norm_done.append(nd_)
                    for k in range(ND):
                        ps = k % 2
                        r.wait("tensor", sA, nd_)
                        r.wait("tensor", sV2, trp_guard[ps])
                        tr = r.op("tensor", "transpose", trp[ps][:],
                                  xnb[:, k * 128:(k + 1) * 128], identb[:],
                                  sem=sT)
                        r.wait("vector", sT, tr)
                        if own and k == 0:
                            r.wait("vector", sSp, spill_guard)
                        if own:
                            evv = r.op("vector", "tensor_copy",
                                       xnstrip[:, k * 128:(k + 1) * 128],
                                       trp[ps][:], sem=sV2)
                        else:
                            evv = r.op("vector", "tensor_copy",
                                       xnrsT[k][:, ti * 128:(ti + 1) * 128],
                                       trp[ps][:], sem=sV2)
                        trp_guard[ps] = evv
                    ev_done.append(sV2.n)
                    if own:
                        r.wait("sync", sV2, ev_done[i])
                        r.op("sync", "dma_start",
                             bass.AP(xnT_sp, ti * 128,
                                     [[ND * TOWN, 128], [TOWN, ND], [1, 128]]),
                             bass.AP(xnstrip, 0,
                                     [[D, 128], [128, ND], [1, 128]]),
                             sem=sSp, inc=16)
                        spill_guard = sSp.n
                r.wait("sync", sSp, sSp.n)
                r.wait("sync", sLx[0], sLx[0].n)
                r.wait("sync", sLx[1], sLx[1].n)
                r.flush(nc)
            b2_spills = sSp.n

            if "xnrst" in dbg:
                o = dbg_out("dbg_xnrst", [ND * 128, S], BF16)
                o2 = dbg_out("dbg_xnt", [128, ND * TOWN], BF16)
                r = Rec()
                for k in range(ND):
                    r.op("sync", "dma_start", o[k * 128:(k + 1) * 128, :],
                         xnrsT[k][:], sem=sOut, inc=16)
                r.op("sync", "dma_start", o2[:], xnT_sp[:], sem=sOut,
                     inc=16)
                r.wait("sync", sOut, sOut.n)
                r.flush(nc)
            if upto == "B2":
                return nc, dbg_outs

            with ExitStack() as c3:
                e3 = c3.enter_context
                wqstg = e3(nc.sbuf_tensor("wqstg", [128, D // 2], F32))
                wqb = e3(nc.sbuf_tensor("wqb", [128, D], BF16))
                qT = e3(nc.sbuf_tensor("qT", [128, S], F32))
                t1w = e3(nc.sbuf_tensor("t1w", [128, 512], F32))
                t2w = e3(nc.sbuf_tensor("t2w", [128, 512], F32))
                qrb = e3(nc.sbuf_tensor("qrb", [128, S], BF16))
                vbst = e3(nc.sbuf_tensor("vbst", [128, S], BF16))
                qps = [e3(nc.psum_tensor(f"qps{i}", [128, 512], F32))
                       for i in range(2)]

                r = Rec()
                evac_hist = []
                vcopy_hist = []
                spill_hist = []
                for m in range(NH):
                    for hf in range(2):
                        if m >= 1 or hf == 1:
                            r.wait("sync", sV, sV.n)
                        r.op("sync", "dma_start", wqstg[:],
                             bass.AP(wq_t, m * 128 + hf * (ND // 2) * 128 * NH * DK,
                                     [[NH * DK, 128], [128 * NH * DK, ND // 2],
                                      [1, 128]]),
                             sem=sL, inc=16)
                        wq_ld = sL.n
                        r.wait("vector", sL, wq_ld)
                        if hf == 0 and m >= 1:
                            r.wait("vector", sT, sT.n)
                            r.wait("vector", sSp, spill_hist[m - 1])
                        for k2 in range(ND // 2):
                            k = hf * (ND // 2) + k2
                            r.op("vector", "tensor_tensor",
                                 wqb[:, k * 128:(k + 1) * 128],
                                 wqstg[:, k2 * 128:(k2 + 1) * 128],
                                 bcast(rmsg, 128, k), ALU.mult,
                                 sem=(sV if k2 == ND // 2 - 1 else None))
                    wq_cv = sV.n
                    for tp in range(8):
                        slot = qps[tp % 2]
                        r.wait("tensor", sV, wq_cv)
                        if tp >= 2:
                            r.wait("tensor", sA, evac_hist[-2])
                        for k in range(ND):
                            r.op("tensor", "matmul", slot[:],
                                 wqb[:, k * 128:(k + 1) * 128],
                                 xnrsT[k][:, tp * 512:(tp + 1) * 512],
                                 start=(k == 0), stop=(k == ND - 1),
                                 sem=(sT if k == ND - 1 else None))
                        r.wait("scalar", sT, sT.n)
                        if tp == 0 and m >= 1:
                            r.wait("scalar", sV2, vcopy_hist[m - 1])
                        ev = r.op("scalar", "activation",
                                  qT[:, tp * 512:(tp + 1) * 512], slot[:],
                                  AF.Identity, bias=wqbias[:, m:m + 1], sem=sA)
                        evac_hist.append(ev)
                    q_ev = evac_hist[-1]
                    r.wait("vector", sA, q_ev)
                    if m >= 1:
                        r.wait("vector", sSp, spill_hist[m - 1])
                    for ch in range(8):
                        a, b = ch * 512, (ch + 1) * 512
                        r.op("vector", "tensor_tensor", t1w[:],
                             qT[:, a:b], cosT[:, a:b], ALU.mult)
                        r.op("vector", "tensor_tensor", t2w[0:64, :],
                             qT[64:128, a:b], sinT[64:128, a:b], ALU.mult)
                        r.op("vector", "tensor_tensor", t2w[64:128, :],
                             qT[0:64, a:b], sinT[0:64, a:b], ALU.mult)
                        r.op("vector", "tensor_tensor", qrb[0:64, a:b],
                             t1w[0:64, :], t2w[0:64, :], ALU.subtract)
                        r.op("vector", "tensor_tensor", qrb[64:128, a:b],
                             t1w[64:128, :], t2w[64:128, :], ALU.add)
                    vc = r.op("vector", "tensor_copy", vbst[:], qT[:], sem=sV2)
                    vcopy_hist.append(vc)
                    r.wait("sync", sV2, vc)
                    r.op("sync", "dma_start", qr_dram[m], qrb[:], sem=sSp,
                         inc=16)
                    r.op("sync", "dma_start", v_dram[m], vbst[:], sem=sSp,
                         inc=16)
                    spill_hist.append(sSp.n)
                r.wait("sync", sSp, sSp.n)
                r.flush(nc)
            b3_spills = sSp.n

            if "qproj" in dbg:
                oq = dbg_out("dbg_qr", [NH * 128, S], BF16)
                ov = dbg_out("dbg_v", [NH * 128, S], BF16)
                r = Rec()
                for m in range(NH):
                    r.op("sync", "dma_start", oq[m * 128:(m + 1) * 128, :],
                         qr_dram[m], sem=sOut, inc=16)
                    r.op("sync", "dma_start", ov[m * 128:(m + 1) * 128, :],
                         v_dram[m], sem=sOut, inc=16)
                r.wait("sync", sOut, sOut.n)
                r.flush(nc)
            if upto == "B3":
                return nc, dbg_outs

        with ExitStack() as c4:
            e4 = c4.enter_context
            maskq = e4(nc.sbuf_tensor("maskq", [128, 4 * 512], F32))
            maskt = e4(nc.sbuf_tensor("maskt", [128, 4 * 512], F32))
            qrs = e4(nc.sbuf_tensor("qrs", [128, S], BF16))
            vfs = e4(nc.sbuf_tensor("vfs", [128, S], BF16))
            vsb = e4(nc.sbuf_tensor("vsb", [128, S], BF16))
            srow = e4(nc.sbuf_tensor("srow", [128, S], F32))
            edump = vfs
            negm = e4(nc.sbuf_tensor("negm", [128, 1], F32))
            ssum = e4(nc.sbuf_tensor("ssum", [128, 1], F32))
            lnsc = e4(nc.sbuf_tensor("lnsc", [128, 1], F32))
            nmadj = e4(nc.sbuf_tensor("nmadj", [128, 1], F32))
            nmrow = e4(nc.sbuf_tensor("nmrow", [1, NQT * 128], F32))
            bplain = e4(nc.sbuf_tensor("bplain", [128, 512], F32))
            bmask = [e4(nc.sbuf_tensor(f"bmask{i}", [128, 512], F32))
                     for i in range(4)]
            sadj = [e4(nc.sbuf_tensor(f"sadj{i}", [128, 512], F32))
                    for i in range(2)]
            pTb = [e4(nc.sbuf_tensor(f"pTb{i}", [128, 512], BF16))
                   for i in range(3)]
            oTg = [e4(nc.sbuf_tensor(f"oTg{i}", [128, 512], BF16))
                   for i in range(2)]
            xnto = e4(nc.sbuf_tensor("xnto", [128, ND * TOWN], BF16))
            wstg = e4(nc.sbuf_tensor("wstg", [128, D], F32))
            w1b = [e4(nc.sbuf_tensor(f"w1b{i}", [128, D], BF16))
                   for i in range(2)]
            w3b = [e4(nc.sbuf_tensor(f"w3b{i}", [128, D], BF16))
                   for i in range(2)]
            stmp = e4(nc.sbuf_tensor("stmp", [128, 512], F32))
            hb = e4(nc.sbuf_tensor("hb", [128, 512], BF16))
            s1strip = e4(nc.sbuf_tensor("s1strip", [128, TOWN], BF16))
            w2stg = e4(nc.sbuf_tensor("w2stg", [128, D], F32))
            w2sb = e4(nc.sbuf_tensor("w2sb", [128, D], BF16))
            w2fout = e4(nc.sbuf_tensor("w2fout", [128, D], BF16))
            qkps = [e4(nc.psum_tensor(f"qkps{i}", [128, 512], F32))
                    for i in range(2)]
            stps = [e4(nc.psum_tensor(f"stps{i}", [128, 512], F32))
                    for i in range(2)]
            bps = e4(nc.psum_tensor("bps", [128, 512], F32))
            avps = e4(nc.psum_tensor("avps", [128, 512], F32))
            w2fps = e4(nc.psum_tensor("w2fps", [128, 512], F32))
            nmps = w2fps[0:1, 0:128]
            vtp = [qkps[i][:, 0:64].bitcast(BF16) for i in range(2)]

            bankg = {}

            r = Rec()

            def gwait(eng, name):
                g_ = bankg.get(name)
                if g_ is not None:
                    r.wait(eng, g_[0], g_[1])

            def gset(name, semv):
                bankg[name] = semv

            r.op("sync", "dma_start",
                 bass.AP(maskq, 0, [[4 * 512, 128], [512, 4], [1, 512]]),
                 bass.AP(maskq_i, 0, [[512, 128], [128 * 512, 4], [1, 512]]),
                 sem=sL, inc=16)
            r.op("sync", "dma_start",
                 bass.AP(maskt, 0, [[4 * 512, 128], [512, 4], [1, 512]]),
                 bass.AP(maskt_i, 0, [[512, 128], [128 * 512, 4], [1, 512]]),
                 sem=sL, inc=16)
            mask_ld = sL.n
            r.wait("sync", sSp, b2_spills)
            r.op("sync", "dma_start", xnto[:], xnT_sp[:], sem=sLxn, inc=16)
            xn_ld = sLxn.n

            unit_done = []
            exp1_hist = []
            prev_unit_av = 0
            oTg_free = [0, 0]
            silu_last = 0
            mult_last = 0
            h128_last = 0
            wcv_hist = [0, 0]
            wstg_free = 0
            w2cv = 0
            w2stg_free = 0
            wo_cv = 0
            s1_sp = 0

            def emit_unit(h):
                nonlocal prev_unit_av
                qsl = qrs
                if h == 0:
                    r.wait("sync", sSp, b3_spills)
                if h >= 2:
                    r.wait("sync", sA2, unit_done[h - 2])
                usem = sLu[h % 2]
                if h >= 1:
                    r.wait("sync", sT2, prev_unit_av)
                    r.wait("sync", sA, exp1_hist[-1])
                r.op("sync", "dma_start", qsl[:], qr_dram[h], sem=usem,
                     inc=16)
                r.op("sync", "dma_start", vfs[:], v_dram[h], sem=usem,
                     inc=16)
                u_ld = usem.n
                for j in range(NT):
                    ps = j % 2
                    r.wait("tensor", usem, u_ld)
                    gwait("tensor", f"q{ps}")
                    tr = r.op("tensor", "transpose", vtp[ps],
                              vfs[:, j * 128:(j + 1) * 128], identb[:],
                              sem=sT)
                    r.wait("vector", sT, tr)
                    if j == 0:
                        r.wait("vector", sT2, prev_unit_av)
                    evv = r.op("vector", "tensor_copy",
                               vsb[:, j * 128:(j + 1) * 128], vtp[ps],
                               sem=sV2)
                    gset(f"q{ps}", (sV2, evv))
                nm_tr = nm_cp = 0
                for t in range(NQT):
                    G = t // 4 + 1
                    rr = t % 4
                    prev_exp1 = exp1_hist[-1] if exp1_hist else 0
                    for j in range(G):
                        ps = j % 2
                        r.wait("tensor", usem, u_ld)
                        gwait("tensor", f"q{ps}")
                        mm = r.op("tensor", "matmul", qkps[ps][:],
                                  qsl[:, t * 128:(t + 1) * 128],
                                  qsl[:, j * 512:(j + 1) * 512], start=True,
                                  stop=True, sem=sT)
                        if j == G - 1:
                            r.wait("vector", sT, mm)
                            r.wait("vector", sL, mask_ld)
                            r.wait("vector", sA, prev_exp1)
                            evv = r.op("vector", "tensor_tensor",
                                       srow[:, j * 512:(j + 1) * 512],
                                       qkps[ps][:],
                                       maskq[:, rr * 512:(rr + 1) * 512],
                                       ALU.add, sem=sV)
                            gset(f"q{ps}", (sV, evv))
                        else:
                            r.wait("scalar", sT, mm)
                            r.wait("scalar", sA, prev_exp1)
                            evv = r.op("scalar", "copy",
                                       srow[:, j * 512:(j + 1) * 512],
                                       qkps[ps][:], sem=sA2)
                            gset(f"q{ps}", (sA2, evv))
                    r.wait("vector", sV, sV.n)
                    r.wait("vector", sA2, sA2.n)
                    r.wait("vector", sA, prev_exp1)
                    rd = r.op("vector", "tensor_reduce", negm[:],
                              srow[:, 0:G * 512], AX.X, ALU.max, negate=True,
                              sem=sV)
                    r.wait("scalar", sV, rd)
                    ex1 = r.op("scalar", "activation", edump[:, 0:G * 512],
                               srow[:, 0:G * 512], AF.Exp, bias=bcol(negm),
                               accum_out=ssum[:, 0:1], sem=sA)
                    r.wait("scalar", sA, ex1)
                    lnv = r.op("scalar", "activation", lnsc[:], ssum[:],
                               AF.Ln, sem=sA)
                    exp1_hist.append(lnv)
                    r.wait("vector", sA, lnv)
                    if nm_tr:
                        r.wait("vector", sT, nm_tr)
                    r.op("vector", "tensor_tensor", nmadj[:, 0:1],
                         negm[:], lnsc[:], ALU.subtract, sem=sV)
                    r.wait("tensor", sV, sV.n)
                    gwait("tensor", "w2")
                    nm_tr = r.op("tensor", "transpose", nmps,
                                 nmadj[:, 0:1], identf[:], sem=sT)
                    r.wait("scalar", sT, nm_tr)
                    nm_cp = r.op("scalar", "copy",
                                 nmrow[0:1, t * 128:(t + 1) * 128], nmps,
                                 sem=sA)
                    gset("w2", (sA, nm_cp))
                for g in range(NQG):
                    nk = 4 * (g + 1)
                    gav = []
                    gwait("tensor", "b")
                    for rr2 in range(4):
                        j = 4 * g + rr2
                        r.op("tensor", "matmul",
                             bps[:, rr2 * 128:(rr2 + 1) * 128],
                             onesf[0:1, :],
                             nmrow[0:1, j * 128:(j + 1) * 128],
                             start=True, stop=True,
                             sem=(sT if rr2 == 3 else None))
                    bmm = sT.n
                    r.wait("scalar", sT, bmm)
                    bev_a = r.op("scalar", "copy", bplain[:], bps[:], sem=sA)
                    r.wait("vector", sT, bmm)
                    r.wait("vector", sL, mask_ld)
                    r.wait("vector", sA, bev_a)
                    for rr2 in range(4):
                        r.op("vector", "tensor_tensor", bmask[rr2][:],
                             bps[:], maskt[:, rr2 * 512:(rr2 + 1) * 512],
                             ALU.add, sem=(sV if rr2 == 3 else None))
                    bev_v = sV.n
                    gset("b", (sV, bev_v))
                    sub_hist = []
                    exp_hist = []
                    for jk in range(nk):
                        p2, p3 = jk % 2, jk % 3
                        if jk >= 2:
                            r.wait("tensor", sV, sub_hist[jk - 2])
                        qkmm = r.op("tensor", "matmul", stps[p2][:],
                                    qsl[:, jk * 128:(jk + 1) * 128],
                                    qsl[:, g * 512:(g + 1) * 512],
                                    start=True, stop=True, sem=sT)
                        if jk >= 1:
                            r.wait("tensor", sA, exp_hist[jk - 1])
                            av = r.op("tensor", "matmul", avps[:],
                                      vsb[:, (jk - 1) * 128:jk * 128],
                                      pTb[(jk - 1) % 3][:],
                                      start=(jk - 1 == 0), stop=False,
                                      skip_group_check=True, sem=sT2)
                            gav.append(av)
                        r.wait("vector", sT, qkmm)
                        if jk < 4 * g:
                            r.wait("vector", sA, bev_a)
                            bsrc = bplain
                        else:
                            r.wait("vector", sV, bev_v)
                            bsrc = bmask[jk - 4 * g]
                        if jk >= 2:
                            r.wait("vector", sA, exp_hist[jk - 2])
                        sb = r.op("vector", "tensor_tensor", sadj[p2][:],
                                  stps[p2][:], bsrc[:], ALU.add, sem=sV)
                        sub_hist.append(sb)
                        r.wait("scalar", sV, sb)
                        if jk >= 3:
                            r.wait("scalar", sT2, gav[jk - 3])
                        ex = r.op("scalar", "activation", pTb[p3][:],
                                  sadj[p2][:], AF.Exp, sem=sA)
                        exp_hist.append(ex)
                    r.wait("tensor", sA, exp_hist[nk - 1])
                    av = r.op("tensor", "matmul", avps[:],
                              vsb[:, (nk - 1) * 128:nk * 128],
                              pTb[(nk - 1) % 3][:],
                              start=(nk == 1), stop=True,
                              skip_group_check=True, sem=sT2)
                    gav.append(av)
                    r.wait("scalar", sT2, av)
                    gs = g % 2
                    r.wait("scalar", sSpO[gs], oTg_free[gs])
                    oe = r.op("scalar", "activation", oTg[gs][:], avps[:],
                              AF.Copy, sem=sA2)
                    r.wait("tensor", sA2, oe)
                    r.wait("sync", sA2, oe)
                    r.op("sync", "dma_start",
                         o_cc_in[h * 128:(h + 1) * 128,
                                 g * 512:(g + 1) * 512],
                         oTg[gs][:], sem=sSpO[gs], inc=16)
                    oTg_free[gs] = sSpO[gs].n
                unit_done.append(sA2.n)
                prev_unit_av = sT2.n

            def emit_ffn(hi):
                nonlocal silu_last, mult_last, h128_last, wstg_free, s1_sp
                sl_ = ml = hm = 0
                wslot = hi % 2
                r.wait("sync", sV, wstg_free)
                r.op("sync", "dma_start", wstg[:],
                     bass.AP(w1_t, hi * 128, [[F, 128], [128 * F, ND],
                                              [1, 128]]), sem=sLwf, inc=16)
                w_ld = sLwf.n
                r.wait("vector", sLwf, w_ld)
                if h128_last:
                    r.wait("vector", sT, h128_last)
                cv1 = r.op("vector", "tensor_copy", w1b[wslot][:], wstg[:],
                           sem=sV)
                r.wait("sync", sV, cv1)
                r.op("sync", "dma_start", wstg[:],
                     bass.AP(w3_t, hi * 128, [[F, 128], [128 * F, ND],
                                              [1, 128]]), sem=sLwf, inc=16)
                w_ld = sLwf.n
                r.wait("vector", sLwf, w_ld)
                cv = r.op("vector", "tensor_copy", w3b[wslot][:], wstg[:],
                          sem=sV)
                wcv_hist[wslot] = cv
                wstg_free = cv
                for tp in range(4):
                    cols = slice(tp * 512, (tp + 1) * 512)
                    if hi == 0 and tp == 0:
                        r.wait("tensor", sLxn, xn_ld)
                    r.wait("tensor", sV, cv)
                    gwait("tensor", "q0")
                    if silu_last:
                        r.wait("tensor", sA, silu_last)
                    for k in range(ND):
                        r.op("tensor", "matmul", qkps[0][:],
                             w1b[wslot][:, k * 128:(k + 1) * 128],
                             xnto[:, k * TOWN + tp * 512:
                                   k * TOWN + (tp + 1) * 512],
                             start=(k == 0), stop=(k == ND - 1),
                             sem=(sT if k == ND - 1 else None))
                    up_done = sT.n
                    gwait("tensor", "q1")
                    if mult_last:
                        r.wait("tensor", sV2, mult_last)
                    for k in range(ND):
                        r.op("tensor", "matmul", qkps[1][:],
                             w3b[wslot][:, k * 128:(k + 1) * 128],
                             xnto[:, k * TOWN + tp * 512:
                                   k * TOWN + (tp + 1) * 512],
                             start=(k == 0), stop=(k == ND - 1),
                             sem=(sT if k == ND - 1 else None))
                    lin_done = sT.n
                    r.wait("scalar", sT, up_done)
                    if mult_last:
                        r.wait("scalar", sV2, mult_last)
                    sl_ = r.op("scalar", "activation", stmp[:], qkps[0][:],
                               AF.Silu, sem=sA)
                    silu_last = sl_
                    gset("q0", (sA, sl_))
                    r.wait("vector", sA, sl_)
                    r.wait("vector", sT, lin_done)
                    if h128_last:
                        r.wait("vector", sT, h128_last)
                    ml = r.op("vector", "tensor_tensor", hb[:], stmp[:],
                              qkps[1][:], ALU.mult, sem=sV2)
                    mult_last = ml
                    gset("q1", (sV2, ml))
                    r.wait("tensor", sV2, ml)
                    gwait("tensor", "b")
                    hm = r.op("tensor", "matmul", bps[:], h128b[:], hb[:],
                              start=True, stop=True, sem=sT)
                    h128_last = hm
                    r.wait("scalar", sT, hm)
                    if tp == 0:
                        r.wait("scalar", sSpS1, s1_sp)
                    s1e = r.op("scalar", "activation", s1strip[:, cols],
                               bps[:], AF.Copy, scale=bcol(cIsq128), sem=sA)
                    gset("b", (sA, s1e))
                r.wait("sync", sA, sA.n)
                r.op("sync", "dma_start", s1_dram[hi], s1strip[:],
                     sem=sSpS1, inc=16)
                s1_sp = sSpS1.n

            def emit_w2f(lam):
                nonlocal w2cv, w2stg_free
                r.wait("sync", sV, w2stg_free)
                r.wait("sync", sG, wo_cv)
                r.op("sync", "dma_start", w2stg[:],
                     bass.AP(w2_s, lam * D, [[(128 // GP) * D, 128], [1, D]]),
                     sem=sLw2m, inc=16)
                r.wait("vector", sLw2m, sLw2m.n)
                r.wait("vector", sT, sT.n)
                w2cv = r.op("vector", "tensor_copy", w2sb[:], w2stg[:],
                            sem=sV)
                w2stg_free = w2cv
                w2ev = 0
                for q4 in range(4):
                    r.wait("tensor", sV, w2cv)
                    gwait("tensor", "w2")
                    r.op("tensor", "matmul", w2fps[:], bdb[:],
                         w2sb[:, q4 * 512:(q4 + 1) * 512], start=True,
                         stop=True, sem=sT)
                    r.wait("scalar", sT, sT.n)
                    if q4 == 0:
                        r.wait("scalar", sSpW2, sSpW2.n)
                        r.wait("scalar", sSpWo, sSpWo.n)
                    w2ev = r.op("scalar", "activation",
                                w2fout[:, q4 * 512:(q4 + 1) * 512],
                                w2fps[:], AF.Copy, scale=bcol(cIsq64),
                                sem=sA)
                    gset("w2", (sA, w2ev))
                r.wait("sync", sA, w2ev)
                for gi in range(GP):
                    ap_sb2 = bass.AP(w2fout, gi * FH * D,
                                     [[D, FH], [128, ND], [1, 128]])
                    ap_dr2 = bass.AP(w2f_dram, lam * GP * 128 + gi * 128,
                                     [[128 * 128, FH], [FH * 128 * 128, ND],
                                      [1, 128]])
                    r.op("sync", "dma_start", ap_dr2, ap_sb2, sem=sSpW2,
                         inc=16)

            def emit_wo(k):
                nonlocal wo_cv, w2stg_free
                r.wait("sync", sV, w2stg_free)
                r.wait("sync", sG, wo_cv)
                r.op("sync", "dma_start", w2stg[:],
                     wo_t[k * 128:(k + 1) * 128, :], sem=sLwom, inc=16)
                r.wait("gpsimd", sLwom, sLwom.n)
                r.wait("gpsimd", sSpW2, sSpW2.n)
                r.wait("gpsimd", sSpWo, sSpWo.n)
                r.wait("gpsimd", sA, sA.n)
                wo_cv = r.op("gpsimd", "tensor_copy", w2fout[:], w2stg[:],
                             sem=sG)
                w2stg_free = 0
                r.wait("sync", sG, wo_cv)
                r.op("sync", "dma_start", wob_dram[k * 128:(k + 1) * 128, :],
                     w2fout[:], sem=sSpWo, inc=16)

            for h in range(NH):
                emit_unit(h)
                for hi in range(8 * h, 8 * h + 8):
                    emit_ffn(hi)
                for lam in range(8 * h, 8 * h + 8):
                    emit_w2f(lam)
                for k in (2 * h, 2 * h + 1):
                    emit_wo(k)

            r.wait("gpsimd", sSpO[0], sSpO[0].n)
            r.wait("gpsimd", sSpO[1], sSpO[1].n)

            def do_cc(gp):
                gp.collective_compute(
                    "AllGather", ALU.bypass,
                    replica_groups=[[0, 1], [2, 3], [4, 5], [6, 7]],
                    ins=[o_cc_in.ap().opt()], outs=[o_cc_out.ap().opt()],
                ).then_inc(sCC.h, 1)
            sCC.n += 1
            r.call("gpsimd", do_cc)
            for s_ in (sSp, sSpS1, sSpW2, sSpWo, sL, sLxn, sLwf, sLw2m,
                       sLwom, sLu[0], sLu[1], sSpO[0], sSpO[1]):
                r.wait("sync", s_, s_.n)
            r.flush(nc)
        b5_spill_state = [(s_, s_.n) for s_ in
                          (sSp, sSpS1, sSpW2, sSpWo, sSpO[0], sSpO[1])]

        if "attn" in dbg:
            o = dbg_out("dbg_oT", [NH * 128, S], BF16)
            r = Rec()
            r.wait("sync", sSp, sSp.n)
            r.op("sync", "dma_start", o[:], o_cc_in[:], sem=sOut, inc=16)
            r.wait("sync", sOut, sOut.n)
            r.flush(nc)
        if upto == "B4":
            return nc, dbg_outs

        if "ffnup" in dbg:
            o = dbg_out("dbg_s1", [FH * 128, TOWN], BF16)
            o2 = dbg_out("dbg_w2f", [ND * FH * 128, 128], BF16)
            r = Rec()
            r.wait("sync", sSp, sSp.n)
            for hi in range(FH):
                r.op("sync", "dma_start", o[hi * 128:(hi + 1) * 128, :],
                     s1_dram[hi], sem=sOut, inc=16)
            r.op("sync", "dma_start", o2.ap(),
                 w2f_dram.ap().rearrange("m h p d -> (m h p) d"), sem=sOut,
                 inc=16)
            r.wait("sync", sOut, sOut.n)
            r.flush(nc)
        if upto == "B5":
            return nc, dbg_outs

        with ExitStack() as c6:
            e6 = c6.enter_context
            acc = e6(nc.sbuf_tensor("acc", [128, ND * 512], F32))
            ogst = e6(nc.sbuf_tensor("ogst", [128, ND * 512], BF16))
            s1t = e6(nc.sbuf_tensor("s1t", [128, FH * 512], BF16))
            w2fst = e6(nc.sbuf_tensor("w2fst", [128, FH * 128], BF16))
            wost = e6(nc.sbuf_tensor("wost", [128, D], BF16))
            xnst = e6(nc.sbuf_tensor("xnst", [128, ND * 512], BF16))
            ops_ = [e6(nc.psum_tensor(f"ops{i}", [128, 512], F32))
                    for i in range(2)]
            dps = [e6(nc.psum_tensor(f"dps{i}", [128, 512], F32))
                   for i in range(2)]

            r = Rec()
            for s_, v_ in b5_spill_state:
                r.wait("sync", s_, v_)
            out_hist = 0
            oev_hist = []
            dev_hist = []
            for t4 in range(TOWN // 512):
                if t4 > 0:
                    r.wait("sync", sV, sV.n)
                    r.wait("sync", sV2, sV2.n)
                    r.wait("sync", sOut, out_hist)
                r.op("sync", "dma_start",
                     bass.AP(acc, 0, [[ND * 512, 128], [512, ND], [1, 512]]),
                     bass.AP(x_own_tr, t4 * 512,
                             [[TOWN, 128], [128 * TOWN, ND], [1, 512]]),
                     sem=sLb6, inc=16)
                r.op("sync", "dma_start",
                     bass.AP(xnst, 0, [[ND * 512, 128], [512, ND], [1, 512]]),
                     bass.AP(xnT_sp, t4 * 512,
                             [[ND * TOWN, 128], [TOWN, ND], [1, 512]]),
                     sem=sLb6, inc=16)
                r.op("sync", "dma_start",
                     bass.AP(s1t, 0, [[FH * 512, 128], [512, FH], [1, 512]]),
                     bass.AP(s1_dram, t4 * 512,
                             [[TOWN, 128], [128 * TOWN, FH], [1, 512]]),
                     sem=sLb6, inc=16)
                base_ld = sLb6.n

                def og_load(gp, t4=t4):
                    with gp.register(f"rgo{t4}") as rg:
                        gp.reg_load(rg, coloff[0:1, 0:1])
                        if t4:
                            gp.reg_add(rg, rg, t4 * 512)
                        gp.dma_start(
                            bass.AP(ogst, 0, [[ND * 512, 128], [512, ND],
                                              [1, 512]]),
                            bass.AP(o_cc_out, rg,
                                    [[S, 128], [128 * S, ND], [1, 512]])
                        ).then_inc(sLog.h, 16)
                if t4 > 0:
                    r.wait("gpsimd", sT, sT.n)
                r.wait("gpsimd", sCC, sCC.n)
                r.call("gpsimd", og_load)
                sLog.n += 16
                og_ld = sLog.n

                r.wait("vector", sLb6, base_ld)
                for k in range(ND):
                    r.op("vector", "tensor_tensor",
                         acc[:, k * 512:(k + 1) * 512],
                         acc[:, k * 512:(k + 1) * 512],
                         bcast(wobias, 512, k), ALU.add)
                init_done = r.op("vector", "tensor_tensor", acc[:], acc[:],
                                 xnst[:], ALU.add, sem=sV)

                wo_lds = []
                for m in range(ND):
                    r.wait("sync", sT, sT.n)
                    r.op("sync", "dma_start", wost[:],
                         bass.AP(wob_dram, m * 128,
                                 [[D, 128], [128 * D, ND], [1, 128]]),
                         sem=sLwo6, inc=16)
                    wo_lds.append(sLwo6.n)
                    slot = ops_[m % 2]
                    r.wait("tensor", sLwo6, wo_lds[m])
                    r.wait("tensor", sLog, og_ld)
                    if m >= 2:
                        r.wait("tensor", sV, oev_hist[m - 2])
                    for k in range(ND):
                        r.op("tensor", "matmul", slot[:],
                             wost[:, k * 128:(k + 1) * 128],
                             ogst[:, k * 512:(k + 1) * 512],
                             start=(k == 0), stop=(k == ND - 1),
                             sem=(sT if k == ND - 1 else None))
                    r.wait("vector", sT, sT.n)
                    if m == 0:
                        r.wait("vector", sV, init_done)
                    ov_ = r.op("vector", "tensor_tensor",
                               acc[:, m * 512:(m + 1) * 512], slot[:],
                               acc[:, m * 512:(m + 1) * 512], ALU.add, sem=sV)
                    oev_hist.append(ov_)
                for m in range(ND):
                    r.wait("sync", sT2, sT2.n)
                    r.op("sync", "dma_start",
                         bass.AP(w2fst, 0, [[FH * 128, 128], [128, FH],
                                            [1, 128]]),
                         bass.AP(w2f_dram, m * FH * 128 * 128,
                                 [[128, 128], [128 * 128, FH], [1, 128]]),
                         sem=sLw26, inc=16)
                    w2_ld = sLw26.n
                    slot = dps[m % 2]
                    r.wait("tensor", sLw26, w2_ld)
                    if m >= 2:
                        r.wait("tensor", sV2, dev_hist[m - 2])
                    for hi in range(FH):
                        r.op("tensor", "matmul", slot[:],
                             w2fst[:, hi * 128:(hi + 1) * 128],
                             s1t[:, hi * 512:(hi + 1) * 512],
                             start=(hi == 0), stop=(hi == FH - 1),
                             sem=(sT2 if hi == FH - 1 else None))
                    r.wait("vector", sT2, sT2.n)
                    dv_ = r.op("vector", "tensor_tensor",
                               acc[:, m * 512:(m + 1) * 512], slot[:],
                               acc[:, m * 512:(m + 1) * 512], ALU.add,
                               sem=sV2)
                    dev_hist.append(dv_)
                oev_hist = []
                dev_hist = []
                r.wait("sync", sV, sV.n)
                r.wait("sync", sV2, sV2.n)
                r.op("sync", "dma_start",
                     bass.AP(out_t, t4 * 512,
                             [[TOWN, 128], [128 * TOWN, ND], [1, 512]]),
                     bass.AP(acc, 0, [[ND * 512, 128], [512, ND], [1, 512]]),
                     sem=sOut, inc=16)
                out_hist = sOut.n
            r.wait("sync", sOut, sOut.n)
            r.flush(nc)

    return nc, dbg_outs


def _hadamard(n):
    h = np.array([[1.0]])
    while h.shape[0] < n:
        h = np.block([[h, h], [h, -h]])
    return h


def make_core_inputs(inputs, core):
    X = np.asarray(inputs["L0_attn_X"])
    Wq = np.asarray(inputs["L0_attn_Wq_w"])
    Wqb = np.asarray(inputs["L0_attn_Wq_b"])
    Wo = np.asarray(inputs["L0_attn_Wo_w"])
    Wob = np.asarray(inputs["L0_attn_Wo_b"])
    g = np.asarray(inputs["L0_attn_rms_w"])
    W1 = np.asarray(inputs["L0_ffn_W1"])
    W2 = np.asarray(inputs["L0_ffn_W2"])
    W3 = np.asarray(inputs["L0_ffn_W3"])

    s, half = core // 2, core % 2
    ho = half * NH
    xs = np.ascontiguousarray(X[s], dtype=np.float32)
    x_own = np.ascontiguousarray(xs[half * TOWN:(half + 1) * TOWN])
    d = {
        "x_full": xs,
        "x_own": x_own,
        "x_own_tr": np.ascontiguousarray(x_own.T),
        "wq_t": np.ascontiguousarray(Wq[ho * DK:(ho + NH) * DK, :].T),
        "wq_bh": np.ascontiguousarray(
            Wqb[ho * DK:(ho + NH) * DK].reshape(NH, 128).T),
        "rms_gc": np.ascontiguousarray(g.reshape(ND, 128).T),
        "wo_t": np.ascontiguousarray(Wo.T),
        "wo_bc": np.ascontiguousarray(Wob.reshape(ND, 128).T),
        "w1_t": np.ascontiguousarray(W1.T),
        "w3_t": np.ascontiguousarray(W3.T),
        "coloff_i": np.array([[half * TOWN]], dtype=np.int32),
    }
    T = np.ascontiguousarray(W2.T).reshape(FH, 128, D)
    T2 = T.reshape(FH, 128 // GP, GP, D)
    W2s = np.ascontiguousarray(T2.transpose(2, 0, 1, 3)).reshape(
        128, (128 // GP) * D)
    d["w2_s"] = W2s

    h128 = _hadamard(128)
    h64 = _hadamard(FH)
    bd = np.zeros((128, 128), dtype=np.float64)
    for gi in range(GP):
        bd[gi * FH:(gi + 1) * FH, gi * FH:(gi + 1) * FH] = h64
    d["h128_i"] = h128.astype(BF)
    d["bd_i"] = bd.astype(BF)
    d["identb_i"] = np.eye(128).astype(BF)
    d["identf_i"] = np.eye(128, dtype=np.float32)
    d["onesf_i"] = np.ones((1, 128), dtype=np.float32)
    qi = np.arange(128)[:, None]
    ck = np.arange(512)[None, :]
    mq = np.stack([np.where(ck <= 128 * rr + qi, 0.0, NEG_BIG)
                   for rr in range(4)]).astype(np.float32)
    mt = np.stack([np.where(ck >= 128 * rr + qi, 0.0, NEG_BIG)
                   for rr in range(4)]).astype(np.float32)
    d["maskq_i"] = np.ascontiguousarray(mq.reshape(4 * 128, 512))
    d["maskt_i"] = np.ascontiguousarray(mt.reshape(4 * 128, 512))
    return d


_BUILD_CACHE = {}


def kernel(**inputs):
    if "full" not in _BUILD_CACHE:
        _BUILD_CACHE["full"] = build_program()
    nc, _ = _BUILD_CACHE["full"]
    in_maps = [make_core_inputs(inputs, c) for c in range(8)]
    res = run_bass_kernel_spmd(nc, in_maps, core_ids=list(range(8)))
    out = np.empty((4, S, D), dtype=np.float32)
    for c in range(8):
        s, half = c // 2, c % 2
        ot = np.asarray(res.results[c]["out_t"])
        out[s, half * TOWN:(half + 1) * TOWN, :] = ot.T
    return out


# version 04
# speedup vs baseline: 1.0568x; 1.0568x over previous
import math
from contextlib import ExitStack

import numpy as np
import ml_dtypes

import concourse.bass as bass
import concourse.mybir as mybir
from concourse.bass_utils import run_bass_kernel_spmd

F32 = mybir.dt.float32
BF16 = mybir.dt.bfloat16
I32 = mybir.dt.int32
AF = mybir.ActivationFunctionType
ALU = mybir.AluOpType
AX = mybir.AxisListType
BF = ml_dtypes.bfloat16

S = 4096
D = 2048
DK = 128
NH = 8
F = 8192
FH = F // 128
GP = 128 // FH
TOWN = S // 2
NT = S // 128
NTO = TOWN // 128
ND = D // 128
NQT = S // 128
NQG = S // 512
EPS_LN = 1e-5
EPS_F32 = float(np.finfo(np.float32).eps)
SQ4 = 128.0 ** (-0.25)
ISQ128 = 1.0 / math.sqrt(128.0)
ISQ64 = 1.0 / math.sqrt(float(FH))
TWO_PI = 2.0 * math.pi
NEG_BIG = -1.0e30

ENGS = ("sync", "vector", "scalar", "tensor", "gpsimd")


class Sem:
    __slots__ = ("h", "n", "name")

    def __init__(self, h, name):
        self.h, self.n, self.name = h, 0, name


_CHAIN_ENGS = ("vector", "scalar", "gpsimd")
_NO_CHAIN_METHODS = {"dma_start", "nop", "engine_nop"}


def _ap_name(a):
    t = getattr(a, "tensor", None)
    return getattr(t, "name", None) if t is not None else None


def _rw(method, args, kw):
    writes, reads = set(), set()
    aps = [a for a in args if _ap_name(a)]
    if method in ("memset", "iota"):
        if aps:
            writes.add(_ap_name(aps[0]))
    else:
        if aps:
            writes.add(_ap_name(aps[0]))
        for a in aps[1:]:
            reads.add(_ap_name(a))
    for k, v in kw.items():
        n = _ap_name(v)
        if n is None:
            continue
        if k == "accum_out":
            writes.add(n)
        else:
            reads.add(n)
    return writes, reads


class Rec:

    chain_sems = {}

    def __init__(self):
        self.q = {e: [] for e in ENGS}
        self.lw = {e: {} for e in _CHAIN_ENGS}
        self.lr = {e: {} for e in _CHAIN_ENGS}
        self.fence = {e: 0 for e in _CHAIN_ENGS}

    def wait(self, eng, sem, val):
        if val > 0:
            self.q[eng].append(("w", sem.h, int(val)))

    def op(self, eng, method, *args, sem=None, inc=1, **kw):
        chain = None
        if eng in _CHAIN_ENGS and method not in _NO_CHAIN_METHODS:
            cs = Rec.chain_sems[eng]
            writes, reads = _rw(method, args, kw)
            need = 0
            for n in reads:
                need = max(need, self.lw[eng].get(n, 0))
            for n in writes:
                need = max(need, self.lw[eng].get(n, 0),
                           self.lr[eng].get(n, 0))
            if need > self.fence[eng]:
                self.q[eng].append(("w", cs.h, int(need)))
                self.fence[eng] = need
            cs.n += 1
            chain = cs
            for n in writes:
                self.lw[eng][n] = cs.n
            for n in reads:
                self.lr[eng][n] = max(self.lr[eng].get(n, 0), cs.n)
        if sem is not None and sem is chain:
            sem = None
        if sem is not None:
            sem.n += inc
        self.q[eng].append(("o", method, args, kw,
                            (sem.h if sem else None), inc,
                            (chain.h if chain else None)))
        return chain.n if chain is not None else (
            sem.n if sem is not None else None)

    def call(self, eng, fn):
        self.q[eng].append(("c", fn))

    def flush(self, nc):
        if not any(self.q[e] for e in ENGS):
            return
        with nc.Block() as block:
            for name in ENGS:
                items = self.q[name]
                if not items:
                    continue

                def body(eng, items=items):
                    for it in items:
                        if it[0] == "w":
                            eng.wait_ge(it[1], it[2])
                        elif it[0] == "c":
                            it[1](eng)
                        else:
                            _, method, args, kw, semh, inc, chainh = it
                            ins = getattr(eng, method)(*args, **kw)
                            if semh is not None:
                                ins.then_inc(semh, inc)
                            if chainh is not None:
                                ins.then_inc(chainh, 1)

                getattr(block, name)(body)


def bcol(t, col=0):
    return t[:, col:col + 1]


def bcast(t, n, col=0):
    return bass.AP(t, col, [[t.shape[1], 128], [0, n]])


def build_program(upto=None, dbg=(), b4_units=None, b4_pa_only=False, b4_groups=None, b4_noav=False, b4_nokq=False, b4_nob=False, b4_bstage=0, b4_nobm=False, b4_nobp=False):
    nc = bass.Bass("TRN2", target_bir_lowering=False)
    dbg = set(dbg)

    x_full = nc.dram_tensor("x_full", [S, D], F32, kind="ExternalInput")
    x_own = nc.dram_tensor("x_own", [TOWN, D], F32, kind="ExternalInput")
    x_own_tr = nc.dram_tensor("x_own_tr", [D, TOWN], F32, kind="ExternalInput")
    wq_t = nc.dram_tensor("wq_t", [D, NH * DK], F32, kind="ExternalInput")
    wq_bh = nc.dram_tensor("wq_bh", [128, NH], F32, kind="ExternalInput")
    rms_gc = nc.dram_tensor("rms_gc", [128, ND], F32, kind="ExternalInput")
    wo_t = nc.dram_tensor("wo_t", [D, D], F32, kind="ExternalInput")
    wo_bc = nc.dram_tensor("wo_bc", [128, ND], F32, kind="ExternalInput")
    w1_t = nc.dram_tensor("w1_t", [D, F], F32, kind="ExternalInput")
    w3_t = nc.dram_tensor("w3_t", [D, F], F32, kind="ExternalInput")
    w2_s = nc.dram_tensor("w2_s", [128, (128 // GP) * D], F32, kind="ExternalInput")
    h128_i = nc.dram_tensor("h128_i", [128, 128], BF16, kind="ExternalInput")
    bd_i = nc.dram_tensor("bd_i", [128, 128], BF16, kind="ExternalInput")
    identb_i = nc.dram_tensor("identb_i", [128, 128], BF16, kind="ExternalInput")
    identf_i = nc.dram_tensor("identf_i", [128, 128], F32, kind="ExternalInput")
    onesf_i = nc.dram_tensor("onesf_i", [1, 128], F32, kind="ExternalInput")
    maskq_i = nc.dram_tensor("maskq_i", [4 * 128, 512], F32, kind="ExternalInput")
    maskt_i = nc.dram_tensor("maskt_i", [4 * 128, 512], F32, kind="ExternalInput")
    coloff_i = nc.dram_tensor("coloff_i", [1, 1], I32, kind="ExternalInput")

    out_t = nc.dram_tensor("out_t", [D, TOWN], F32, kind="ExternalOutput")

    xnT_sp = nc.dram_tensor("xnT_sp", [128, ND * TOWN], BF16)
    qr_dram = nc.dram_tensor("qr_dram", [NH, 128, S], BF16)
    v_dram = nc.dram_tensor("v_dram", [NH, 128, S], BF16)
    s1_dram = nc.dram_tensor("s1_dram", [FH, 128, TOWN], BF16)
    w2f_dram = nc.dram_tensor("w2f_dram", [ND, FH, 128, 128], BF16)
    wob_dram = nc.dram_tensor("wob_dram", [D, D], BF16)
    o_cc_in = nc.dram_tensor("o_cc_in", [NH * 128, S], BF16)
    o_cc_out = nc.dram_tensor("o_cc_out", [2 * NH * 128, S], BF16)

    dbg_outs = {}

    def dbg_out(name, shape, dtype=F32):
        dbg_outs[name] = nc.dram_tensor(name, shape, dtype, kind="ExternalOutput")
        return dbg_outs[name]

    with ExitStack() as ctx:
        e = ctx.enter_context

        def sem(name):
            return Sem(e(nc.semaphore(name)), name)

        sL = sem("sL")
        sSp = sem("sSp")
        sT = sem("sT")
        sT2 = sem("sT2")
        sCC = sem("sCC")
        sOut = sem("sOut")
        Rec.chain_sems = {"vector": sem("cV"), "scalar": sem("cA"),
                          "gpsimd": sem("cG")}
        sV = sV2 = Rec.chain_sems["vector"]
        sA = sA2 = Rec.chain_sems["scalar"]
        sG = Rec.chain_sems["gpsimd"]
        sLx = [sem("sLx0"), sem("sLx1")]
        sLu = [sem("sLu0"), sem("sLu1")]
        sLxn = sem("sLxn")
        sLb6 = sem("sLb6")
        sLog = sem("sLog")
        sLwo6 = sem("sLwo6")
        sLw26 = sem("sLw26")
        sLwf = sem("sLwf")
        sLw2m = sem("sLw2m")
        sLwom = sem("sLwom")
        sSpO = [sem("sSpO0"), sem("sSpO1")]
        sSpS1 = sem("sSpS1")
        sSpW2 = sem("sSpW2")
        sSpWo = sem("sSpWo")

        h128b = e(nc.sbuf_tensor("h128b", [128, 128], BF16))
        bdb = e(nc.sbuf_tensor("bdb", [128, 128], BF16))
        identb = e(nc.sbuf_tensor("identb", [128, 128], BF16))
        identf = e(nc.sbuf_tensor("identf", [128, 128], F32))
        onesf = e(nc.sbuf_tensor("onesf", [1, 128], F32))
        rmsg = e(nc.sbuf_tensor("rmsg", [128, ND], F32))
        wqbias = e(nc.sbuf_tensor("wqbias", [128, NH], F32))
        wobias = e(nc.sbuf_tensor("wobias", [128, ND], F32))
        coloff = e(nc.sbuf_tensor("coloff", [1, 1], I32))
        cIsq128 = e(nc.sbuf_tensor("cIsq128", [128, 1], F32))
        cIsq64 = e(nc.sbuf_tensor("cIsq64", [128, 1], F32))

        r = Rec()
        for dst, src in ((h128b, h128_i), (bdb, bd_i), (identb, identb_i),
                         (identf, identf_i), (onesf, onesf_i), (rmsg, rms_gc),
                         (wqbias, wq_bh), (wobias, wo_bc), (coloff, coloff_i)):
            r.op("sync", "dma_start", dst[:], src[:], sem=sL, inc=16)
        r.op("vector", "memset", cIsq128[:], ISQ128)
        r.op("vector", "memset", cIsq64[:], ISQ64)
        r.wait("sync", sL, sL.n)
        r.flush(nc)

        with ExitStack() as cEarly:
            eE = cEarly.enter_context
            cosT = eE(nc.sbuf_tensor("cosT", [128, S], F32))
            sinT = eE(nc.sbuf_tensor("sinT", [128, S], F32))

            with ExitStack() as c1:
                e1 = c1.enter_context
                jcol = e1(nc.sbuf_tensor("jcol", [128, 1], F32))
                ifcol = e1(nc.sbuf_tensor("ifcol", [128, 1], F32))
                angw = e1(nc.sbuf_tensor("angw", [128, S], F32))
                uw = e1(nc.sbuf_tensor("uw", [128, S], F32))
                kacc = e1(nc.sbuf_tensor("kacc", [128, S], F32))
                cmpw = e1(nc.sbuf_tensor("cmpw", [128, S], F32))
                cNegLn = e1(nc.sbuf_tensor("cNegLn", [128, 1], F32))

                r = Rec()
                r.op("gpsimd", "iota", jcol[:, 0:1], [[1, 1]],
                     channel_multiplier=1, allow_small_or_imprecise_dtypes=True)
                r.op("gpsimd", "iota", angw[:, :], [[1, S]],
                     channel_multiplier=0, allow_small_or_imprecise_dtypes=True,
                     sem=sG)
                r.op("vector", "memset", cNegLn[:], -math.log(10000.0) / 64.0)
                r.wait("vector", sG, sG.n)
                w = r.op("vector", "tensor_single_scalar", cmpw[:, 0:1],
                         jcol[:], float(FH), ALU.is_ge, sem=sV)
                r.wait("vector", sV, w)
                r.op("vector", "scalar_tensor_tensor", jcol[:], cmpw[:, 0:1],
                     -float(FH), jcol[:], ALU.mult, ALU.add, sem=sV)
                r.wait("scalar", sV, sV.n)
                r.op("scalar", "activation", ifcol[:], jcol[:], AF.Exp,
                     scale=bcol(cNegLn), sem=sA)
                r.wait("vector", sA, sA.n)
                r.op("vector", "tensor_tensor", angw[:], angw[:],
                     bcast(ifcol, S), ALU.mult)
                r.op("vector", "tensor_single_scalar", angw[0:64, :],
                     angw[0:64, :], math.pi / 2.0, ALU.add)
                r.op("vector", "tensor_single_scalar", uw[:], angw[:],
                     1.0 / TWO_PI, ALU.mult)
                r.op("vector", "tensor_single_scalar", uw[:], uw[:], 0.5,
                     ALU.add)
                r.op("vector", "memset", kacc[:], 0.0)
                for b in (512, 256, 128, 64, 32, 16, 8, 4, 2, 1):
                    r.op("vector", "tensor_single_scalar", cmpw[:], uw[:],
                         float(b), ALU.is_ge)
                    r.op("vector", "scalar_tensor_tensor", uw[:], cmpw[:],
                         -float(b), uw[:], ALU.mult, ALU.add)
                    r.op("vector", "scalar_tensor_tensor", kacc[:], cmpw[:],
                         float(b), kacc[:], ALU.mult, ALU.add)
                r.op("vector", "scalar_tensor_tensor", angw[:], kacc[:],
                     -TWO_PI, angw[:], ALU.mult, ALU.add)
                r.op("vector", "tensor_single_scalar", angw[:], angw[:],
                     3.141592, ALU.min)
                r.op("vector", "tensor_single_scalar", angw[:], angw[:],
                     -3.141592, ALU.max, sem=sV)
                r.wait("scalar", sV, sV.n)
                r.op("scalar", "activation", uw[:], angw[:], AF.Sin, sem=sA)
                r.wait("vector", sA, sA.n)
                sc_ = r.op("vector", "tensor_single_scalar", uw[:], uw[:],
                           SQ4, ALU.mult, sem=sV)
                r.wait("vector", sV, sc_)
                r.op("vector", "tensor_copy", cosT[0:64, :], uw[0:64, :])
                r.op("vector", "tensor_copy", cosT[64:128, :], uw[0:64, :])
                r.op("vector", "tensor_copy", sinT[0:64, :], uw[64:128, :])
                r.op("vector", "tensor_copy", sinT[64:128, :], uw[64:128, :],
                     sem=sV)
                r.flush(nc)

            if "tables" in dbg:
                o = dbg_out("dbg_tables", [128, S])
                r = Rec()
                r.op("sync", "dma_start", o[:], cosT[:], sem=sOut, inc=16)
                r.wait("sync", sOut, sOut.n)
                r.flush(nc)
            if upto == "B1":
                return nc, dbg_outs

            xnrsT = [eE(nc.sbuf_tensor(f"xnrsT{k}", [128, S], BF16))
                     for k in range(ND)]

            with ExitStack() as c2:
                e2 = c2.enter_context
                xstg = [e2(nc.sbuf_tensor(f"xstg{i}", [128, D], F32))
                        for i in range(2)]
                sqdump = e2(nc.sbuf_tensor("sqdump", [128, D], BF16))
                xnb = e2(nc.sbuf_tensor("xnb", [128, D], BF16))
                xnstrip = e2(nc.sbuf_tensor("xnstrip", [128, D], BF16))
                s1c = e2(nc.sbuf_tensor("s1c", [128, 1], F32))
                s2c = e2(nc.sbuf_tensor("s2c", [128, 1], F32))
                meanc = e2(nc.sbuf_tensor("meanc", [128, 1], F32))
                varc = e2(nc.sbuf_tensor("varc", [128, 1], F32))
                r1c = e2(nc.sbuf_tensor("r1c", [128, 1], F32))
                uc = e2(nc.sbuf_tensor("uc", [128, 1], F32))
                rstdc = e2(nc.sbuf_tensor("rstdc", [128, 1], F32))
                gc = e2(nc.sbuf_tensor("gc", [128, 1], F32))
                cc = e2(nc.sbuf_tensor("cc", [128, 1], F32))
                bcl = e2(nc.sbuf_tensor("bcl", [128, 1], F32))
                trp = [e2(nc.psum_tensor(f"trp{i}", [128, 128], BF16))
                       for i in range(2)]

                r = Rec()
                ntot = NT + NTO
                norm_done = []
                ev_done = []
                trp_guard = [0, 0]
                spill_guard = 0
                for i in range(ntot):
                    own = i >= NT
                    ti = i - NT if own else i
                    src = x_own if own else x_full
                    slot = xstg[i % 2]
                    if i >= 2:
                        r.wait("sync", sA, norm_done[i - 2])
                    lsem = sLx[i % 2]
                    r.op("sync", "dma_start", slot[:],
                         src[ti * 128:(ti + 1) * 128, :], sem=lsem, inc=16)
                    ld = lsem.n
                    r.wait("vector", lsem, ld)
                    vv = r.op("vector", "reduce_sum", s1c[:], slot[:], AX.X,
                              sem=sV)
                    r.wait("scalar", lsem, ld)
                    if i >= 1:
                        r.wait("scalar", sV2, ev_done[i - 1])
                    sq = r.op("scalar", "activation", sqdump[:], slot[:],
                              AF.Square, accum_out=s2c[:, 0:1], sem=sA)
                    r.wait("vector", sA, sq)

                    def vchain(method, *args):
                        w = r.op("vector", method, *args, sem=sV)
                        r.wait("vector", sV, w)
                        return w

                    r.wait("vector", sV, vv)
                    vchain("tensor_single_scalar", meanc[:], s1c[:],
                           1.0 / D, ALU.mult)
                    vchain("tensor_single_scalar", varc[:], s2c[:],
                           1.0 / D, ALU.mult)
                    vchain("tensor_tensor", uc[:], meanc[:], meanc[:],
                           ALU.mult)
                    vchain("tensor_tensor", varc[:], varc[:], uc[:],
                           ALU.subtract)
                    vchain("tensor_single_scalar", r1c[:], varc[:],
                           EPS_LN, ALU.add)
                    vchain("reciprocal", r1c[:], r1c[:])
                    if own:
                        cols = vchain("tensor_copy", uc[:], r1c[:])
                    else:
                        vchain("tensor_tensor", uc[:], varc[:], r1c[:],
                               ALU.mult)
                        vchain("tensor_single_scalar", uc[:], uc[:],
                               EPS_F32, ALU.add)
                        cols = vchain("reciprocal", uc[:], uc[:])
                    r.wait("scalar", sV, cols)
                    r.op("scalar", "activation", rstdc[:], r1c[:], AF.Sqrt)
                    sqr = r.op("scalar", "activation", gc[:], uc[:], AF.Sqrt,
                               sem=sA)
                    r.wait("vector", sA, sqr)
                    if own:
                        ccv = r.op("vector", "tensor_copy", cc[:], rstdc[:],
                                   sem=sV)
                    else:
                        ccv = r.op("vector", "tensor_tensor", cc[:], rstdc[:],
                                   gc[:], ALU.mult, sem=sV)
                    r.wait("vector", sV, ccv)
                    cb = r.op("vector", "scalar_tensor_tensor", bcl[:],
                              meanc[:], -1.0, cc[:], ALU.mult, ALU.mult,
                              sem=sV)
                    r.wait("scalar", sV, cb)
                    nd_ = r.op("scalar", "activation", xnb[:], slot[:],
                               AF.Identity, bias=bcol(bcl), scale=bcol(cc),
                               sem=sA)
                    norm_done.append(nd_)
                    for k in range(ND):
                        ps = k % 2
                        r.wait("tensor", sA, nd_)
                        r.wait("tensor", sV2, trp_guard[ps])
                        tr = r.op("tensor", "transpose", trp[ps][:],
                                  xnb[:, k * 128:(k + 1) * 128], identb[:],
                                  sem=sT)
                        r.wait("vector", sT, tr)
                        if own and k == 0:
                            r.wait("vector", sSp, spill_guard)
                        if own:
                            evv = r.op("vector", "tensor_copy",
                                       xnstrip[:, k * 128:(k + 1) * 128],
                                       trp[ps][:], sem=sV2)
                        else:
                            evv = r.op("vector", "tensor_copy",
                                       xnrsT[k][:, ti * 128:(ti + 1) * 128],
                                       trp[ps][:], sem=sV2)
                        trp_guard[ps] = evv
                    ev_done.append(sV2.n)
                    if own:
                        r.wait("sync", sV2, ev_done[i])
                        r.op("sync", "dma_start",
                             bass.AP(xnT_sp, ti * 128,
                                     [[ND * TOWN, 128], [TOWN, ND], [1, 128]]),
                             bass.AP(xnstrip, 0,
                                     [[D, 128], [128, ND], [1, 128]]),
                             sem=sSp, inc=16)
                        spill_guard = sSp.n
                r.wait("sync", sSp, sSp.n)
                r.wait("sync", sLx[0], sLx[0].n)
                r.wait("sync", sLx[1], sLx[1].n)
                r.flush(nc)
            b2_spills = sSp.n

            if "xnrst" in dbg:
                o = dbg_out("dbg_xnrst", [ND * 128, S], BF16)
                o2 = dbg_out("dbg_xnt", [128, ND * TOWN], BF16)
                r = Rec()
                for k in range(ND):
                    r.op("sync", "dma_start", o[k * 128:(k + 1) * 128, :],
                         xnrsT[k][:], sem=sOut, inc=16)
                r.op("sync", "dma_start", o2[:], xnT_sp[:], sem=sOut,
                     inc=16)
                r.wait("sync", sOut, sOut.n)
                r.flush(nc)
            if upto == "B2":
                return nc, dbg_outs

            with ExitStack() as c3:
                e3 = c3.enter_context
                wqstg = e3(nc.sbuf_tensor("wqstg", [128, D // 2], F32))
                wqb = e3(nc.sbuf_tensor("wqb", [128, D], BF16))
                qT = e3(nc.sbuf_tensor("qT", [128, S], F32))
                t1w = e3(nc.sbuf_tensor("t1w", [128, 512], F32))
                t2w = e3(nc.sbuf_tensor("t2w", [128, 512], F32))
                qrb = e3(nc.sbuf_tensor("qrb", [128, S], BF16))
                vbst = e3(nc.sbuf_tensor("vbst", [128, S], BF16))
                qps = [e3(nc.psum_tensor(f"qps{i}", [128, 512], F32))
                       for i in range(2)]

                r = Rec()
                evac_hist = []
                vcopy_hist = []
                spill_hist = []
                for m in range(NH):
                    for hf in range(2):
                        if m >= 1 or hf == 1:
                            r.wait("sync", sV, sV.n)
                        r.op("sync", "dma_start", wqstg[:],
                             bass.AP(wq_t, m * 128 + hf * (ND // 2) * 128 * NH * DK,
                                     [[NH * DK, 128], [128 * NH * DK, ND // 2],
                                      [1, 128]]),
                             sem=sL, inc=16)
                        wq_ld = sL.n
                        r.wait("vector", sL, wq_ld)
                        if hf == 0 and m >= 1:
                            r.wait("vector", sT, sT.n)
                            r.wait("vector", sSp, spill_hist[m - 1])
                        for k2 in range(ND // 2):
                            k = hf * (ND // 2) + k2
                            r.op("vector", "tensor_tensor",
                                 wqb[:, k * 128:(k + 1) * 128],
                                 wqstg[:, k2 * 128:(k2 + 1) * 128],
                                 bcast(rmsg, 128, k), ALU.mult,
                                 sem=(sV if k2 == ND // 2 - 1 else None))
                    wq_cv = sV.n
                    for tp in range(8):
                        slot = qps[tp % 2]
                        r.wait("tensor", sV, wq_cv)
                        if tp >= 2:
                            r.wait("tensor", sA, evac_hist[-2])
                        for k in range(ND):
                            r.op("tensor", "matmul", slot[:],
                                 wqb[:, k * 128:(k + 1) * 128],
                                 xnrsT[k][:, tp * 512:(tp + 1) * 512],
                                 start=(k == 0), stop=(k == ND - 1),
                                 sem=(sT if k == ND - 1 else None))
                        r.wait("scalar", sT, sT.n)
                        if tp == 0 and m >= 1:
                            r.wait("scalar", sV2, vcopy_hist[m - 1])
                        ev = r.op("scalar", "activation",
                                  qT[:, tp * 512:(tp + 1) * 512], slot[:],
                                  AF.Identity, bias=wqbias[:, m:m + 1], sem=sA)
                        evac_hist.append(ev)
                    q_ev = evac_hist[-1]
                    r.wait("vector", sA, q_ev)
                    if m >= 1:
                        r.wait("vector", sSp, spill_hist[m - 1])
                    for ch in range(8):
                        a, b = ch * 512, (ch + 1) * 512
                        r.op("vector", "tensor_tensor", t1w[:],
                             qT[:, a:b], cosT[:, a:b], ALU.mult)
                        r.op("vector", "tensor_tensor", t2w[0:64, :],
                             qT[64:128, a:b], sinT[64:128, a:b], ALU.mult)
                        r.op("vector", "tensor_tensor", t2w[64:128, :],
                             qT[0:64, a:b], sinT[0:64, a:b], ALU.mult)
                        r.op("vector", "tensor_tensor", qrb[0:64, a:b],
                             t1w[0:64, :], t2w[0:64, :], ALU.subtract)
                        r.op("vector", "tensor_tensor", qrb[64:128, a:b],
                             t1w[64:128, :], t2w[64:128, :], ALU.add)
                    vc = r.op("vector", "tensor_copy", vbst[:], qT[:], sem=sV2)
                    vcopy_hist.append(vc)
                    r.wait("sync", sV2, vc)
                    r.op("sync", "dma_start", qr_dram[m], qrb[:], sem=sSp,
                         inc=16)
                    r.op("sync", "dma_start", v_dram[m], vbst[:], sem=sSp,
                         inc=16)
                    spill_hist.append(sSp.n)
                r.wait("sync", sSp, sSp.n)
                r.flush(nc)
            b3_spills = sSp.n

            if "qproj" in dbg:
                oq = dbg_out("dbg_qr", [NH * 128, S], BF16)
                ov = dbg_out("dbg_v", [NH * 128, S], BF16)
                r = Rec()
                for m in range(NH):
                    r.op("sync", "dma_start", oq[m * 128:(m + 1) * 128, :],
                         qr_dram[m], sem=sOut, inc=16)
                    r.op("sync", "dma_start", ov[m * 128:(m + 1) * 128, :],
                         v_dram[m], sem=sOut, inc=16)
                r.wait("sync", sOut, sOut.n)
                r.flush(nc)
            if upto == "B3":
                return nc, dbg_outs

        with ExitStack() as c4:
            e4 = c4.enter_context
            maskq = e4(nc.sbuf_tensor("maskq", [128, 4 * 512], F32))
            maskt = e4(nc.sbuf_tensor("maskt", [128, 4 * 512], F32))
            qrs = e4(nc.sbuf_tensor("qrs", [128, S], BF16))
            vfs = e4(nc.sbuf_tensor("vfs", [128, S], BF16))
            vsb = e4(nc.sbuf_tensor("vsb", [128, S], BF16))
            srow = e4(nc.sbuf_tensor("srow", [128, S], F32))
            edump = vfs
            negm = e4(nc.sbuf_tensor("negm", [128, 1], F32))
            ssum = e4(nc.sbuf_tensor("ssum", [128, 1], F32))
            lnsc = e4(nc.sbuf_tensor("lnsc", [128, 1], F32))
            nmadj = e4(nc.sbuf_tensor("nmadj", [128, 1], F32))
            nmrow = e4(nc.sbuf_tensor("nmrow", [1, NQT * 128], F32))
            bplain = e4(nc.sbuf_tensor("bplain", [128, 512], F32))
            bmask = [e4(nc.sbuf_tensor(f"bmask{i}", [128, 512], F32))
                     for i in range(4)]
            sadj = [e4(nc.sbuf_tensor(f"sadj{i}", [128, 512], F32))
                    for i in range(2)]
            pTb = [e4(nc.sbuf_tensor(f"pTb{i}", [128, 512], BF16))
                   for i in range(3)]
            oTg = [e4(nc.sbuf_tensor(f"oTg{i}", [128, 512], BF16))
                   for i in range(2)]
            xnto = e4(nc.sbuf_tensor("xnto", [128, ND * TOWN], BF16))
            wstg = e4(nc.sbuf_tensor("wstg", [128, D], F32))
            w1b = [e4(nc.sbuf_tensor(f"w1b{i}", [128, D], BF16))
                   for i in range(2)]
            w3b = [e4(nc.sbuf_tensor(f"w3b{i}", [128, D], BF16))
                   for i in range(2)]
            stmp = e4(nc.sbuf_tensor("stmp", [128, 512], F32))
            hb = e4(nc.sbuf_tensor("hb", [128, 512], BF16))
            s1strip = e4(nc.sbuf_tensor("s1strip", [128, TOWN], BF16))
            w2stg = e4(nc.sbuf_tensor("w2stg", [128, D], F32))
            w2sb = e4(nc.sbuf_tensor("w2sb", [128, D], BF16))
            w2fout = e4(nc.sbuf_tensor("w2fout", [128, D], BF16))
            qkps = [e4(nc.psum_tensor(f"qkps{i}", [128, 512], F32))
                    for i in range(2)]
            stps = [e4(nc.psum_tensor(f"stps{i}", [128, 512], F32))
                    for i in range(2)]
            avps = e4(nc.psum_tensor("avps", [128, 512], F32))
            fps = [e4(nc.psum_tensor(f"fps{i}", [128, 512], F32))
                   for i in range(3)]
            bps = qkps[0]
            nmps = avps[0:1, 0:128]
            vtp = [qkps[i][:, 0:64].bitcast(BF16) for i in range(2)]

            bankg = {}
            r = Rec()

            def gwait(eng, name):
                g_ = bankg.get(name)
                if g_ is not None:
                    r.wait(eng, g_[0], g_[1])

            def gset(name, semv):
                bankg[name] = semv

            r.op("sync", "dma_start",
                 bass.AP(maskq, 0, [[4 * 512, 128], [512, 4], [1, 512]]),
                 bass.AP(maskq_i, 0, [[512, 128], [128 * 512, 4], [1, 512]]),
                 sem=sL, inc=16)
            r.op("sync", "dma_start",
                 bass.AP(maskt, 0, [[4 * 512, 128], [512, 4], [1, 512]]),
                 bass.AP(maskt_i, 0, [[512, 128], [128 * 512, 4], [1, 512]]),
                 sem=sL, inc=16)
            mask_ld = sL.n
            r.wait("sync", sSp, b2_spills)
            r.op("sync", "dma_start", xnto[:], xnT_sp[:], sem=sLxn, inc=16)
            xn_ld = sLxn.n

            unit_done = []
            exp1_hist = []
            prev_unit_av = 0
            oTg_free = [0, 0]
            silu_last = 0
            mult_last = 0
            h128_last = 0
            wcv = [0, 0]
            wstg_free = 0
            w2cv = 0
            wo_cv = 0
            s1_sp = 0

            def q_wload(hi):
                nonlocal wstg_free, h128_last
                ws = hi % 2
                r.wait("sync", sV, wstg_free)
                r.op("sync", "dma_start", wstg[:],
                     bass.AP(w1_t, hi * 128, [[F, 128], [128 * F, ND],
                                              [1, 128]]), sem=sLwf, inc=16)
                r.wait("vector", sLwf, sLwf.n)
                if h128_last:
                    r.wait("vector", sT, h128_last)
                cv1 = r.op("vector", "tensor_copy", w1b[ws][:], wstg[:],
                           sem=sV)
                r.wait("sync", sV, cv1)
                r.op("sync", "dma_start", wstg[:],
                     bass.AP(w3_t, hi * 128, [[F, 128], [128 * F, ND],
                                              [1, 128]]), sem=sLwf, inc=16)
                r.wait("vector", sLwf, sLwf.n)
                cv2 = r.op("vector", "tensor_copy", w3b[ws][:], wstg[:],
                           sem=sV)
                wcv[ws] = cv2
                wstg_free = cv2

            def q_ffn(hi, tp):
                nonlocal silu_last, mult_last, h128_last, s1_sp
                ws = hi % 2
                cols = slice(tp * 512, (tp + 1) * 512)
                if hi == 0 and tp == 0:
                    r.wait("tensor", sLxn, xn_ld)
                r.wait("tensor", sV, wcv[ws])
                if silu_last:
                    r.wait("tensor", sA, silu_last)
                for k in range(ND):
                    r.op("tensor", "matmul", fps[0][:],
                         w1b[ws][:, k * 128:(k + 1) * 128],
                         xnto[:, k * TOWN + tp * 512:
                               k * TOWN + (tp + 1) * 512],
                         start=(k == 0), stop=(k == ND - 1),
                         sem=(sT if k == ND - 1 else None))
                up_done = sT.n
                if mult_last:
                    r.wait("tensor", sV2, mult_last)
                for k in range(ND):
                    r.op("tensor", "matmul", fps[1][:],
                         w3b[ws][:, k * 128:(k + 1) * 128],
                         xnto[:, k * TOWN + tp * 512:
                               k * TOWN + (tp + 1) * 512],
                         start=(k == 0), stop=(k == ND - 1),
                         sem=(sT if k == ND - 1 else None))
                lin_done = sT.n
                r.wait("scalar", sT, up_done)
                if mult_last:
                    r.wait("scalar", sV2, mult_last)
                sl_ = r.op("scalar", "activation", stmp[:], fps[0][:],
                           AF.Silu, sem=sA)
                silu_last = sl_
                r.wait("vector", sA, sl_)
                r.wait("vector", sT, lin_done)
                if h128_last:
                    r.wait("vector", sT, h128_last)
                ml = r.op("vector", "tensor_tensor", hb[:], stmp[:],
                          fps[1][:], ALU.mult, sem=sV2)
                mult_last = ml
                r.wait("tensor", sV2, ml)
                gwait("tensor", "f2")
                hm = r.op("tensor", "matmul", fps[2][:], h128b[:], hb[:],
                          start=True, stop=True, sem=sT)
                h128_last = hm
                r.wait("scalar", sT, hm)
                if tp == 0:
                    r.wait("scalar", sSpS1, s1_sp)
                s1e = r.op("scalar", "activation", s1strip[:, cols],
                           fps[2][:], AF.Copy, scale=bcol(cIsq128), sem=sA)
                gset("f2", (sA, s1e))
                if tp == 3:
                    r.wait("sync", sA, s1e)
                    r.op("sync", "dma_start", s1_dram[hi], s1strip[:],
                         sem=sSpS1, inc=16)
                    s1_sp = sSpS1.n

            def q_w2f(lam):
                nonlocal w2cv, wo_cv
                r.wait("sync", sV, w2cv)
                r.wait("sync", sG, wo_cv)
                r.op("sync", "dma_start", w2stg[:],
                     bass.AP(w2_s, lam * D, [[(128 // GP) * D, 128], [1, D]]),
                     sem=sLw2m, inc=16)
                r.wait("vector", sLw2m, sLw2m.n)
                r.wait("vector", sT, sT.n)
                w2cv = r.op("vector", "tensor_copy", w2sb[:], w2stg[:],
                            sem=sV)
                w2ev = 0
                for q4 in range(4):
                    r.wait("tensor", sV, w2cv)
                    gwait("tensor", "f2")
                    r.op("tensor", "matmul", fps[2][:], bdb[:],
                         w2sb[:, q4 * 512:(q4 + 1) * 512], start=True,
                         stop=True, sem=sT)
                    r.wait("scalar", sT, sT.n)
                    if q4 == 0:
                        r.wait("scalar", sSpW2, sSpW2.n)
                        r.wait("scalar", sSpWo, sSpWo.n)
                    w2ev = r.op("scalar", "activation",
                                w2fout[:, q4 * 512:(q4 + 1) * 512],
                                fps[2][:], AF.Copy, scale=bcol(cIsq64),
                                sem=sA)
                    gset("f2", (sA, w2ev))
                r.wait("sync", sA, w2ev)
                for gi in range(GP):
                    ap_sb2 = bass.AP(w2fout, gi * FH * D,
                                     [[D, FH], [128, ND], [1, 128]])
                    ap_dr2 = bass.AP(w2f_dram, lam * GP * 128 + gi * 128,
                                     [[128 * 128, FH], [FH * 128 * 128, ND],
                                      [1, 128]])
                    r.op("sync", "dma_start", ap_dr2, ap_sb2, sem=sSpW2,
                         inc=16)

            def q_wo(k):
                nonlocal wo_cv
                r.wait("sync", sV, w2cv)
                r.wait("sync", sG, wo_cv)
                r.op("sync", "dma_start", w2stg[:],
                     wo_t[k * 128:(k + 1) * 128, :], sem=sLwom, inc=16)
                r.wait("gpsimd", sLwom, sLwom.n)
                r.wait("gpsimd", sSpW2, sSpW2.n)
                r.wait("gpsimd", sSpWo, sSpWo.n)
                r.wait("gpsimd", sA, sA.n)
                wo_cv = r.op("gpsimd", "tensor_copy", w2fout[:], w2stg[:],
                             sem=sG)
                r.wait("sync", sG, wo_cv)
                r.op("sync", "dma_start", wob_dram[k * 128:(k + 1) * 128, :],
                     w2fout[:], sem=sSpWo, inc=16)

            def emit_unit(h):
                nonlocal prev_unit_av
                qsl = qrs
                quanta = []
                for hi in range(8 * h, 8 * h + 8):
                    quanta.append(("wl", hi))
                    for tp in range(4):
                        quanta.append(("tp", hi, tp))
                qi = [0]

                def pump(n=1):
                    for _ in range(n):
                        if qi[0] >= len(quanta):
                            return
                        q = quanta[qi[0]]
                        qi[0] += 1
                        if q[0] == "wl":
                            q_wload(q[1])
                        else:
                            q_ffn(q[1], q[2])

                if h == 0:
                    r.wait("sync", sSp, b3_spills)
                if h >= 2:
                    r.wait("sync", sA2, unit_done[h - 2])
                usem = sLu[h % 2]
                if h >= 1:
                    r.wait("sync", sT2, prev_unit_av)
                    r.wait("sync", sA, exp1_hist[-1])
                r.op("sync", "dma_start", qsl[:], qr_dram[h], sem=usem,
                     inc=16)
                r.op("sync", "dma_start", vfs[:], v_dram[h], sem=usem,
                     inc=16)
                u_ld = usem.n
                for j in range(NT):
                    ps = j % 2
                    r.wait("tensor", usem, u_ld)
                    gwait("tensor", f"q{ps}")
                    tr = r.op("tensor", "transpose", vtp[ps],
                              vfs[:, j * 128:(j + 1) * 128], identb[:],
                              sem=sT)
                    r.wait("vector", sT, tr)
                    if j == 0:
                        r.wait("vector", sT2, prev_unit_av)
                    evv = r.op("vector", "tensor_copy",
                               vsb[:, j * 128:(j + 1) * 128], vtp[ps],
                               sem=sV2)
                    gset(f"q{ps}", (sV2, evv))
                nm_tr = nm_cp = 0
                for t in range(NQT):
                    G = t // 4 + 1
                    rr = t % 4
                    prev_exp1 = exp1_hist[-1] if exp1_hist else 0
                    for j in range(G):
                        ps = j % 2
                        r.wait("tensor", usem, u_ld)
                        gwait("tensor", f"q{ps}")
                        mm = r.op("tensor", "matmul", qkps[ps][:],
                                  qsl[:, t * 128:(t + 1) * 128],
                                  qsl[:, j * 512:(j + 1) * 512], start=True,
                                  stop=True, sem=sT)
                        if j == G - 1:
                            r.wait("vector", sT, mm)
                            r.wait("vector", sL, mask_ld)
                            r.wait("vector", sA, prev_exp1)
                            evv = r.op("vector", "tensor_tensor",
                                       srow[:, j * 512:(j + 1) * 512],
                                       qkps[ps][:],
                                       maskq[:, rr * 512:(rr + 1) * 512],
                                       ALU.add, sem=sV)
                            gset(f"q{ps}", (sV, evv))
                        else:
                            r.wait("scalar", sT, mm)
                            r.wait("scalar", sA, prev_exp1)
                            evv = r.op("scalar", "copy",
                                       srow[:, j * 512:(j + 1) * 512],
                                       qkps[ps][:], sem=sA2)
                            gset(f"q{ps}", (sA2, evv))
                    r.wait("vector", sV, sV.n)
                    r.wait("vector", sA2, sA2.n)
                    r.wait("vector", sA, prev_exp1)
                    rd = r.op("vector", "tensor_reduce", negm[:],
                              srow[:, 0:G * 512], AX.X, ALU.max, negate=True,
                              sem=sV)
                    r.wait("scalar", sV, rd)
                    ex1 = r.op("scalar", "activation", edump[:, 0:G * 512],
                               srow[:, 0:G * 512], AF.Exp, bias=bcol(negm),
                               accum_out=ssum[:, 0:1], sem=sA)
                    r.wait("scalar", sA, ex1)
                    lnv = r.op("scalar", "activation", lnsc[:], ssum[:],
                               AF.Ln, sem=sA)
                    exp1_hist.append(lnv)
                    r.wait("vector", sA, lnv)
                    if nm_tr:
                        r.wait("vector", sT, nm_tr)
                    r.op("vector", "tensor_tensor", nmadj[:, 0:1],
                         negm[:], lnsc[:], ALU.subtract, sem=sV)
                    r.wait("tensor", sV, sV.n)
                    gwait("tensor", "av")
                    nm_tr = r.op("tensor", "transpose", nmps,
                                 nmadj[:, 0:1], identf[:], sem=sT)
                    r.wait("scalar", sT, nm_tr)
                    nm_cp = r.op("scalar", "copy",
                                 nmrow[0:1, t * 128:(t + 1) * 128], nmps,
                                 sem=sA)
                    gset("av", (sA, nm_cp))
                    pump(1)
                for g in range(NQG):
                    nk = 4 * (g + 1)
                    gav = []
                    gwait("tensor", "q0")
                    for rr2 in range(4):
                        j = 4 * g + rr2
                        r.op("tensor", "matmul",
                             bps[:, rr2 * 128:(rr2 + 1) * 128],
                             onesf[0:1, :],
                             nmrow[0:1, j * 128:(j + 1) * 128],
                             start=True, stop=True,
                             sem=(sT if rr2 == 3 else None))
                    bmm = sT.n
                    r.wait("scalar", sT, bmm)
                    bev_a = r.op("scalar", "copy", bplain[:], bps[:], sem=sA)
                    r.wait("vector", sT, bmm)
                    r.wait("vector", sL, mask_ld)
                    r.wait("vector", sA, bev_a)
                    for rr2 in range(4):
                        r.op("vector", "tensor_tensor", bmask[rr2][:],
                             bps[:], maskt[:, rr2 * 512:(rr2 + 1) * 512],
                             ALU.add, sem=(sV if rr2 == 3 else None))
                    bev_v = sV.n
                    gset("q0", (sV, bev_v))
                    sub_hist = []
                    exp_hist = []
                    for jk in range(nk):
                        p2, p3 = jk % 2, jk % 3
                        if jk >= 2:
                            r.wait("tensor", sV, sub_hist[jk - 2])
                        qkmm = r.op("tensor", "matmul", stps[p2][:],
                                    qsl[:, jk * 128:(jk + 1) * 128],
                                    qsl[:, g * 512:(g + 1) * 512],
                                    start=True, stop=True, sem=sT)
                        if jk >= 1:
                            r.wait("tensor", sA, exp_hist[jk - 1])
                            av = r.op("tensor", "matmul", avps[:],
                                      vsb[:, (jk - 1) * 128:jk * 128],
                                      pTb[(jk - 1) % 3][:],
                                      start=(jk - 1 == 0), stop=False,
                                      skip_group_check=True, sem=sT2)
                            gav.append(av)
                        r.wait("vector", sT, qkmm)
                        if jk < 4 * g:
                            r.wait("vector", sA, bev_a)
                            bsrc = bplain
                        else:
                            r.wait("vector", sV, bev_v)
                            bsrc = bmask[jk - 4 * g]
                        if jk >= 2:
                            r.wait("vector", sA, exp_hist[jk - 2])
                        sb = r.op("vector", "tensor_tensor", sadj[p2][:],
                                  stps[p2][:], bsrc[:], ALU.add, sem=sV)
                        sub_hist.append(sb)
                        r.wait("scalar", sV, sb)
                        if jk >= 3:
                            r.wait("scalar", sT2, gav[jk - 3])
                        ex = r.op("scalar", "activation", pTb[p3][:],
                                  sadj[p2][:], AF.Exp, sem=sA)
                        exp_hist.append(ex)
                    r.wait("tensor", sA, exp_hist[nk - 1])
                    gwait("tensor", "av")
                    av = r.op("tensor", "matmul", avps[:],
                              vsb[:, (nk - 1) * 128:nk * 128],
                              pTb[(nk - 1) % 3][:],
                              start=(nk == 1), stop=True,
                              skip_group_check=True, sem=sT2)
                    gav.append(av)
                    r.wait("scalar", sT2, av)
                    gs = g % 2
                    r.wait("scalar", sSpO[gs], oTg_free[gs])
                    oe = r.op("scalar", "activation", oTg[gs][:], avps[:],
                              AF.Copy, sem=sA2)
                    gset("av", (sA2, oe))
                    r.wait("tensor", sA2, oe)
                    r.wait("sync", sA2, oe)
                    r.op("sync", "dma_start",
                         o_cc_in[h * 128:(h + 1) * 128,
                                 g * 512:(g + 1) * 512],
                         oTg[gs][:], sem=sSpO[gs], inc=16)
                    oTg_free[gs] = sSpO[gs].n
                    pump(1)
                    q_w2f(8 * h + g)
                    if g >= 6:
                        q_wo(2 * h + (g - 6))
                pump(99)
                unit_done.append(sA2.n)
                prev_unit_av = sT2.n

            for h in range(NH):
                emit_unit(h)

            r.wait("gpsimd", sSpO[0], sSpO[0].n)
            r.wait("gpsimd", sSpO[1], sSpO[1].n)

            def do_cc(gp):
                gp.collective_compute(
                    "AllGather", ALU.bypass,
                    replica_groups=[[0, 1], [2, 3], [4, 5], [6, 7]],
                    ins=[o_cc_in.ap().opt()], outs=[o_cc_out.ap().opt()],
                ).then_inc(sCC.h, 1)
            sCC.n += 1
            r.call("gpsimd", do_cc)
            for s_ in (sSp, sSpS1, sSpW2, sSpWo, sL, sLxn, sLwf, sLw2m,
                       sLwom, sLu[0], sLu[1], sSpO[0], sSpO[1]):
                r.wait("sync", s_, s_.n)
            r.flush(nc)
        b5_spill_state = [(s_, s_.n) for s_ in
                          (sSp, sSpS1, sSpW2, sSpWo, sSpO[0], sSpO[1])]

        if "attn" in dbg:
            o = dbg_out("dbg_oT", [NH * 128, S], BF16)
            r = Rec()
            r.wait("sync", sSp, sSp.n)
            r.op("sync", "dma_start", o[:], o_cc_in[:], sem=sOut, inc=16)
            r.wait("sync", sOut, sOut.n)
            r.flush(nc)
        if upto == "B4":
            return nc, dbg_outs

        if "ffnup" in dbg:
            o = dbg_out("dbg_s1", [FH * 128, TOWN], BF16)
            o2 = dbg_out("dbg_w2f", [ND * FH * 128, 128], BF16)
            r = Rec()
            r.wait("sync", sSp, sSp.n)
            for hi in range(FH):
                r.op("sync", "dma_start", o[hi * 128:(hi + 1) * 128, :],
                     s1_dram[hi], sem=sOut, inc=16)
            r.op("sync", "dma_start", o2.ap(),
                 w2f_dram.ap().rearrange("m h p d -> (m h p) d"), sem=sOut,
                 inc=16)
            r.wait("sync", sOut, sOut.n)
            r.flush(nc)
        if upto == "B5":
            return nc, dbg_outs

        with ExitStack() as c6:
            e6 = c6.enter_context
            acc = e6(nc.sbuf_tensor("acc", [128, ND * 512], F32))
            ogst = e6(nc.sbuf_tensor("ogst", [128, ND * 512], BF16))
            s1t = e6(nc.sbuf_tensor("s1t", [128, FH * 512], BF16))
            w2fst = e6(nc.sbuf_tensor("w2fst", [128, FH * 128], BF16))
            wost = e6(nc.sbuf_tensor("wost", [128, D], BF16))
            xnst = e6(nc.sbuf_tensor("xnst", [128, ND * 512], BF16))
            ops_ = [e6(nc.psum_tensor(f"ops{i}", [128, 512], F32))
                    for i in range(2)]
            dps = [e6(nc.psum_tensor(f"dps{i}", [128, 512], F32))
                   for i in range(2)]

            r = Rec()
            for s_, v_ in b5_spill_state:
                r.wait("sync", s_, v_)
            out_hist = 0
            oev_hist = []
            dev_hist = []
            for t4 in range(TOWN // 512):
                if t4 > 0:
                    r.wait("sync", sV, sV.n)
                    r.wait("sync", sV2, sV2.n)
                    r.wait("sync", sOut, out_hist)
                r.op("sync", "dma_start",
                     bass.AP(acc, 0, [[ND * 512, 128], [512, ND], [1, 512]]),
                     bass.AP(x_own_tr, t4 * 512,
                             [[TOWN, 128], [128 * TOWN, ND], [1, 512]]),
                     sem=sLb6, inc=16)
                r.op("sync", "dma_start",
                     bass.AP(xnst, 0, [[ND * 512, 128], [512, ND], [1, 512]]),
                     bass.AP(xnT_sp, t4 * 512,
                             [[ND * TOWN, 128], [TOWN, ND], [1, 512]]),
                     sem=sLb6, inc=16)
                r.op("sync", "dma_start",
                     bass.AP(s1t, 0, [[FH * 512, 128], [512, FH], [1, 512]]),
                     bass.AP(s1_dram, t4 * 512,
                             [[TOWN, 128], [128 * TOWN, FH], [1, 512]]),
                     sem=sLb6, inc=16)
                base_ld = sLb6.n

                def og_load(gp, t4=t4):
                    with gp.register(f"rgo{t4}") as rg:
                        gp.reg_load(rg, coloff[0:1, 0:1])
                        if t4:
                            gp.reg_add(rg, rg, t4 * 512)
                        gp.dma_start(
                            bass.AP(ogst, 0, [[ND * 512, 128], [512, ND],
                                              [1, 512]]),
                            bass.AP(o_cc_out, rg,
                                    [[S, 128], [128 * S, ND], [1, 512]])
                        ).then_inc(sLog.h, 16)
                if t4 > 0:
                    r.wait("gpsimd", sT, sT.n)
                r.wait("gpsimd", sCC, sCC.n)
                r.call("gpsimd", og_load)
                sLog.n += 16
                og_ld = sLog.n

                r.wait("vector", sLb6, base_ld)
                for k in range(ND):
                    r.op("vector", "tensor_tensor",
                         acc[:, k * 512:(k + 1) * 512],
                         acc[:, k * 512:(k + 1) * 512],
                         bcast(wobias, 512, k), ALU.add)
                init_done = r.op("vector", "tensor_tensor", acc[:], acc[:],
                                 xnst[:], ALU.add, sem=sV)

                wo_lds = []
                for m in range(ND):
                    r.wait("sync", sT, sT.n)
                    r.op("sync", "dma_start", wost[:],
                         bass.AP(wob_dram, m * 128,
                                 [[D, 128], [128 * D, ND], [1, 128]]),
                         sem=sLwo6, inc=16)
                    wo_lds.append(sLwo6.n)
                    slot = ops_[m % 2]
                    r.wait("tensor", sLwo6, wo_lds[m])
                    r.wait("tensor", sLog, og_ld)
                    if m >= 2:
                        r.wait("tensor", sV, oev_hist[m - 2])
                    for k in range(ND):
                        r.op("tensor", "matmul", slot[:],
                             wost[:, k * 128:(k + 1) * 128],
                             ogst[:, k * 512:(k + 1) * 512],
                             start=(k == 0), stop=(k == ND - 1),
                             sem=(sT if k == ND - 1 else None))
                    r.wait("vector", sT, sT.n)
                    if m == 0:
                        r.wait("vector", sV, init_done)
                    ov_ = r.op("vector", "tensor_tensor",
                               acc[:, m * 512:(m + 1) * 512], slot[:],
                               acc[:, m * 512:(m + 1) * 512], ALU.add, sem=sV)
                    oev_hist.append(ov_)
                for m in range(ND):
                    r.wait("sync", sT2, sT2.n)
                    r.op("sync", "dma_start",
                         bass.AP(w2fst, 0, [[FH * 128, 128], [128, FH],
                                            [1, 128]]),
                         bass.AP(w2f_dram, m * FH * 128 * 128,
                                 [[128, 128], [128 * 128, FH], [1, 128]]),
                         sem=sLw26, inc=16)
                    w2_ld = sLw26.n
                    slot = dps[m % 2]
                    r.wait("tensor", sLw26, w2_ld)
                    if m >= 2:
                        r.wait("tensor", sV2, dev_hist[m - 2])
                    for hi in range(FH):
                        r.op("tensor", "matmul", slot[:],
                             w2fst[:, hi * 128:(hi + 1) * 128],
                             s1t[:, hi * 512:(hi + 1) * 512],
                             start=(hi == 0), stop=(hi == FH - 1),
                             sem=(sT2 if hi == FH - 1 else None))
                    r.wait("vector", sT2, sT2.n)
                    dv_ = r.op("vector", "tensor_tensor",
                               acc[:, m * 512:(m + 1) * 512], slot[:],
                               acc[:, m * 512:(m + 1) * 512], ALU.add,
                               sem=sV2)
                    dev_hist.append(dv_)
                oev_hist = []
                dev_hist = []
                r.wait("sync", sV, sV.n)
                r.wait("sync", sV2, sV2.n)
                r.op("sync", "dma_start",
                     bass.AP(out_t, t4 * 512,
                             [[TOWN, 128], [128 * TOWN, ND], [1, 512]]),
                     bass.AP(acc, 0, [[ND * 512, 128], [512, ND], [1, 512]]),
                     sem=sOut, inc=16)
                out_hist = sOut.n
            r.wait("sync", sOut, sOut.n)
            r.flush(nc)

    return nc, dbg_outs


def _hadamard(n):
    h = np.array([[1.0]])
    while h.shape[0] < n:
        h = np.block([[h, h], [h, -h]])
    return h


def make_core_inputs(inputs, core):
    X = np.asarray(inputs["L0_attn_X"])
    Wq = np.asarray(inputs["L0_attn_Wq_w"])
    Wqb = np.asarray(inputs["L0_attn_Wq_b"])
    Wo = np.asarray(inputs["L0_attn_Wo_w"])
    Wob = np.asarray(inputs["L0_attn_Wo_b"])
    g = np.asarray(inputs["L0_attn_rms_w"])
    W1 = np.asarray(inputs["L0_ffn_W1"])
    W2 = np.asarray(inputs["L0_ffn_W2"])
    W3 = np.asarray(inputs["L0_ffn_W3"])

    s, half = core // 2, core % 2
    ho = half * NH
    xs = np.ascontiguousarray(X[s], dtype=np.float32)
    x_own = np.ascontiguousarray(xs[half * TOWN:(half + 1) * TOWN])
    d = {
        "x_full": xs,
        "x_own": x_own,
        "x_own_tr": np.ascontiguousarray(x_own.T),
        "wq_t": np.ascontiguousarray(Wq[ho * DK:(ho + NH) * DK, :].T),
        "wq_bh": np.ascontiguousarray(
            Wqb[ho * DK:(ho + NH) * DK].reshape(NH, 128).T),
        "rms_gc": np.ascontiguousarray(g.reshape(ND, 128).T),
        "wo_t": np.ascontiguousarray(Wo.T),
        "wo_bc": np.ascontiguousarray(Wob.reshape(ND, 128).T),
        "w1_t": np.ascontiguousarray(W1.T),
        "w3_t": np.ascontiguousarray(W3.T),
        "coloff_i": np.array([[half * TOWN]], dtype=np.int32),
    }
    T = np.ascontiguousarray(W2.T).reshape(FH, 128, D)
    T2 = T.reshape(FH, 128 // GP, GP, D)
    W2s = np.ascontiguousarray(T2.transpose(2, 0, 1, 3)).reshape(
        128, (128 // GP) * D)
    d["w2_s"] = W2s

    h128 = _hadamard(128)
    h64 = _hadamard(FH)
    bd = np.zeros((128, 128), dtype=np.float64)
    for gi in range(GP):
        bd[gi * FH:(gi + 1) * FH, gi * FH:(gi + 1) * FH] = h64
    d["h128_i"] = h128.astype(BF)
    d["bd_i"] = bd.astype(BF)
    d["identb_i"] = np.eye(128).astype(BF)
    d["identf_i"] = np.eye(128, dtype=np.float32)
    d["onesf_i"] = np.ones((1, 128), dtype=np.float32)
    qi = np.arange(128)[:, None]
    ck = np.arange(512)[None, :]
    mq = np.stack([np.where(ck <= 128 * rr + qi, 0.0, NEG_BIG)
                   for rr in range(4)]).astype(np.float32)
    mt = np.stack([np.where(ck >= 128 * rr + qi, 0.0, NEG_BIG)
                   for rr in range(4)]).astype(np.float32)
    d["maskq_i"] = np.ascontiguousarray(mq.reshape(4 * 128, 512))
    d["maskt_i"] = np.ascontiguousarray(mt.reshape(4 * 128, 512))
    return d


_BUILD_CACHE = {}


def kernel(**inputs):
    if "full" not in _BUILD_CACHE:
        _BUILD_CACHE["full"] = build_program()
    nc, _ = _BUILD_CACHE["full"]
    in_maps = [make_core_inputs(inputs, c) for c in range(8)]
    res = run_bass_kernel_spmd(nc, in_maps, core_ids=list(range(8)))
    out = np.empty((4, S, D), dtype=np.float32)
    for c in range(8):
        s, half = c // 2, c % 2
        ot = np.asarray(res.results[c]["out_t"])
        out[s, half * TOWN:(half + 1) * TOWN, :] = ot.T
    return out


# version 06
# speedup vs baseline: 1.1777x; 1.1144x over previous
import math
from contextlib import ExitStack

import numpy as np
import ml_dtypes

import concourse.bass as bass
import concourse.mybir as mybir
from concourse.bass_utils import run_bass_kernel_spmd

F32 = mybir.dt.float32
BF16 = mybir.dt.bfloat16
I32 = mybir.dt.int32
AF = mybir.ActivationFunctionType
ALU = mybir.AluOpType
AX = mybir.AxisListType
BF = ml_dtypes.bfloat16

S = 4096
D = 2048
DK = 128
NH = 8
F = 8192
FH = F // 128
GP = 128 // FH
TOWN = S // 2
NT = S // 128
NTO = TOWN // 128
ND = D // 128
NQT = S // 128
NQG = S // 512
EPS_LN = 1e-5
EPS_F32 = float(np.finfo(np.float32).eps)
SQ4 = 128.0 ** (-0.25)
ISQ128 = 1.0 / math.sqrt(128.0)
ISQ64 = 1.0 / math.sqrt(float(FH))
TWO_PI = 2.0 * math.pi
NEG_BIG = -1.0e30

ENGS = ("sync", "vector", "scalar", "tensor", "gpsimd")


class Sem:
    __slots__ = ("h", "n", "name")

    def __init__(self, h, name):
        self.h, self.n, self.name = h, 0, name


_CHAIN_ENGS = ("vector", "scalar", "gpsimd")
_NO_CHAIN_METHODS = {"dma_start", "nop", "engine_nop"}


def _ap_name(a):
    t = getattr(a, "tensor", None)
    return getattr(t, "name", None) if t is not None else None


def _rw(method, args, kw):
    writes, reads = set(), set()
    aps = [a for a in args if _ap_name(a)]
    if method in ("memset", "iota"):
        if aps:
            writes.add(_ap_name(aps[0]))
    else:
        if aps:
            writes.add(_ap_name(aps[0]))
        for a in aps[1:]:
            reads.add(_ap_name(a))
    for k, v in kw.items():
        n = _ap_name(v)
        if n is None:
            continue
        if k == "accum_out":
            writes.add(n)
        else:
            reads.add(n)
    return writes, reads


class Rec:

    chain_sems = {}

    def __init__(self):
        self.q = {e: [] for e in ENGS}
        self.lw = {e: {} for e in _CHAIN_ENGS}
        self.lr = {e: {} for e in _CHAIN_ENGS}
        self.fence = {e: 0 for e in _CHAIN_ENGS}

    def wait(self, eng, sem, val):
        if val > 0:
            self.q[eng].append(("w", sem.h, int(val)))

    def op(self, eng, method, *args, sem=None, inc=1, **kw):
        chain = None
        if eng in _CHAIN_ENGS and method not in _NO_CHAIN_METHODS:
            cs = Rec.chain_sems[eng]
            writes, reads = _rw(method, args, kw)
            need = 0
            for n in reads:
                need = max(need, self.lw[eng].get(n, 0))
            for n in writes:
                need = max(need, self.lw[eng].get(n, 0),
                           self.lr[eng].get(n, 0))
            if need > self.fence[eng]:
                self.q[eng].append(("w", cs.h, int(need)))
                self.fence[eng] = need
            cs.n += 1
            chain = cs
            for n in writes:
                self.lw[eng][n] = cs.n
            for n in reads:
                self.lr[eng][n] = max(self.lr[eng].get(n, 0), cs.n)
        if sem is not None and sem is chain:
            sem = None
        if sem is not None:
            sem.n += inc
        self.q[eng].append(("o", method, args, kw,
                            (sem.h if sem else None), inc,
                            (chain.h if chain else None)))
        return chain.n if chain is not None else (
            sem.n if sem is not None else None)

    def call(self, eng, fn):
        self.q[eng].append(("c", fn))

    def flush(self, nc):
        if not any(self.q[e] for e in ENGS):
            return
        with nc.Block() as block:
            for name in ENGS:
                items = self.q[name]
                if not items:
                    continue

                def body(eng, items=items):
                    for it in items:
                        if it[0] == "w":
                            eng.wait_ge(it[1], it[2])
                        elif it[0] == "c":
                            it[1](eng)
                        else:
                            _, method, args, kw, semh, inc, chainh = it
                            ins = getattr(eng, method)(*args, **kw)
                            if semh is not None:
                                ins.then_inc(semh, inc)
                            if chainh is not None:
                                ins.then_inc(chainh, 1)

                getattr(block, name)(body)


def bcol(t, col=0):
    return t[:, col:col + 1]


def bcast(t, n, col=0):
    return bass.AP(t, col, [[t.shape[1], 128], [0, n]])


def build_program(upto=None, dbg=(), b4_units=None, b4_pa_only=False, b4_groups=None, b4_noav=False, b4_nokq=False, b4_nob=False, b4_bstage=0, b4_nobm=False, b4_nobp=False):
    nc = bass.Bass("TRN2", target_bir_lowering=False)
    dbg = set(dbg)

    x_full = nc.dram_tensor("x_full", [S, D], F32, kind="ExternalInput")
    x_own = nc.dram_tensor("x_own", [TOWN, D], F32, kind="ExternalInput")
    x_own_tr = nc.dram_tensor("x_own_tr", [D, TOWN], F32, kind="ExternalInput")
    wq_t = nc.dram_tensor("wq_t", [D, NH * DK], F32, kind="ExternalInput")
    wq_bh = nc.dram_tensor("wq_bh", [128, NH], F32, kind="ExternalInput")
    rms_gc = nc.dram_tensor("rms_gc", [128, ND], F32, kind="ExternalInput")
    wo_t = nc.dram_tensor("wo_t", [D, D], F32, kind="ExternalInput")
    wo_bc = nc.dram_tensor("wo_bc", [128, ND], F32, kind="ExternalInput")
    w1_t = nc.dram_tensor("w1_t", [D, F], F32, kind="ExternalInput")
    w3_t = nc.dram_tensor("w3_t", [D, F], F32, kind="ExternalInput")
    w2_s = nc.dram_tensor("w2_s", [128, (128 // GP) * D], F32, kind="ExternalInput")
    h128_i = nc.dram_tensor("h128_i", [128, 128], BF16, kind="ExternalInput")
    bd_i = nc.dram_tensor("bd_i", [128, 128], BF16, kind="ExternalInput")
    identb_i = nc.dram_tensor("identb_i", [128, 128], BF16, kind="ExternalInput")
    identf_i = nc.dram_tensor("identf_i", [128, 128], F32, kind="ExternalInput")
    onesf_i = nc.dram_tensor("onesf_i", [1, 128], F32, kind="ExternalInput")
    maskq_i = nc.dram_tensor("maskq_i", [4 * 128, 512], F32, kind="ExternalInput")
    maskt_i = nc.dram_tensor("maskt_i", [4 * 128, 512], F32, kind="ExternalInput")
    coloff_i = nc.dram_tensor("coloff_i", [1, 1], I32, kind="ExternalInput")

    out_t = nc.dram_tensor("out_t", [D, TOWN], F32, kind="ExternalOutput")

    xnT_sp = nc.dram_tensor("xnT_sp", [128, ND * TOWN], BF16)
    qr_dram = nc.dram_tensor("qr_dram", [NH, 128, S], BF16)
    v_dram = nc.dram_tensor("v_dram", [NH, 128, S], BF16)
    s1_dram = nc.dram_tensor("s1_dram", [FH, 128, TOWN], BF16)
    w2f_dram = nc.dram_tensor("w2f_dram", [ND, FH, 128, 128], BF16)
    wob_dram = nc.dram_tensor("wob_dram", [D, D], BF16)
    o_cc_in = nc.dram_tensor("o_cc_in", [NH * 128, S], BF16)
    o_cc_out = nc.dram_tensor("o_cc_out", [2 * NH * 128, S], BF16)

    dbg_outs = {}

    def dbg_out(name, shape, dtype=F32):
        dbg_outs[name] = nc.dram_tensor(name, shape, dtype, kind="ExternalOutput")
        return dbg_outs[name]

    with ExitStack() as ctx:
        e = ctx.enter_context

        def sem(name):
            return Sem(e(nc.semaphore(name)), name)

        sL = sem("sL")
        sSp = sem("sSp")
        sT = sem("sT")
        sT2 = sem("sT2")
        sCC = sem("sCC")
        sOut = sem("sOut")
        Rec.chain_sems = {"vector": sem("cV"), "scalar": sem("cA"),
                          "gpsimd": sem("cG")}
        sV = sV2 = Rec.chain_sems["vector"]
        sA = sA2 = Rec.chain_sems["scalar"]
        sG = Rec.chain_sems["gpsimd"]
        sLx = [sem("sLx0"), sem("sLx1")]
        sLu = [sem("sLu0"), sem("sLu1")]
        sLxn = sem("sLxn")
        sLb6 = sem("sLb6")
        sLog = sem("sLog")
        sLwo6 = sem("sLwo6")
        sLw26 = sem("sLw26")
        sLwf = sem("sLwf")
        sLw2m = sem("sLw2m")
        sLwom = sem("sLwom")
        sSpO = [sem("sSpO0"), sem("sSpO1")]
        sSpS1 = sem("sSpS1")
        sSpW2 = sem("sSpW2")
        sSpWo = sem("sSpWo")

        h128b = e(nc.sbuf_tensor("h128b", [128, 128], BF16))
        bdb = e(nc.sbuf_tensor("bdb", [128, 128], BF16))
        identb = e(nc.sbuf_tensor("identb", [128, 128], BF16))
        identf = e(nc.sbuf_tensor("identf", [128, 128], F32))
        onesf_t = e(nc.sbuf_tensor("onesf_t", [128, 128], F32))
        onesf = onesf_t[0:1, 0:128]
        rmsg = e(nc.sbuf_tensor("rmsg", [128, ND], F32))
        wqbias = e(nc.sbuf_tensor("wqbias", [128, NH], F32))
        wobias = e(nc.sbuf_tensor("wobias", [128, ND], F32))
        coloff_t = e(nc.sbuf_tensor("coloff_t", [128, 1], I32))
        coloff = coloff_t[0:1, 0:1]
        cIsq128 = e(nc.sbuf_tensor("cIsq128", [128, 1], F32))
        cIsq64 = e(nc.sbuf_tensor("cIsq64", [128, 1], F32))

        r = Rec()
        for dst, src in ((h128b[:], h128_i), (bdb[:], bd_i),
                         (identb[:], identb_i), (identf[:], identf_i),
                         (onesf, onesf_i), (rmsg[:], rms_gc),
                         (wqbias[:], wq_bh), (wobias[:], wo_bc),
                         (coloff, coloff_i)):
            r.op("sync", "dma_start", dst, src[:], sem=sL, inc=16)
        r.op("vector", "memset", cIsq128[:], ISQ128)
        r.op("vector", "memset", cIsq64[:], ISQ64)
        r.wait("sync", sL, sL.n)
        r.flush(nc)

        with ExitStack() as cEarly:
            eE = cEarly.enter_context
            cosT = eE(nc.sbuf_tensor("cosT", [128, S], F32))
            sinT = eE(nc.sbuf_tensor("sinT", [128, S], F32))

            with ExitStack() as c1:
                e1 = c1.enter_context
                jcol = e1(nc.sbuf_tensor("jcol", [128, 1], F32))
                ifcol = e1(nc.sbuf_tensor("ifcol", [128, 1], F32))
                angw = e1(nc.sbuf_tensor("angw", [128, S], F32))
                uw = e1(nc.sbuf_tensor("uw", [128, S], F32))
                kacc = e1(nc.sbuf_tensor("kacc", [128, S], F32))
                cmpw = e1(nc.sbuf_tensor("cmpw", [128, S], F32))
                cNegLn = e1(nc.sbuf_tensor("cNegLn", [128, 1], F32))

                r = Rec()
                r.op("gpsimd", "iota", jcol[:, 0:1], [[1, 1]],
                     channel_multiplier=1, allow_small_or_imprecise_dtypes=True)
                r.op("gpsimd", "iota", angw[:, :], [[1, S]],
                     channel_multiplier=0, allow_small_or_imprecise_dtypes=True,
                     sem=sG)
                r.op("vector", "memset", cNegLn[:], -math.log(10000.0) / 64.0)
                r.wait("vector", sG, sG.n)
                w = r.op("vector", "tensor_single_scalar", cmpw[:, 0:1],
                         jcol[:], float(FH), ALU.is_ge, sem=sV)
                r.wait("vector", sV, w)
                r.op("vector", "scalar_tensor_tensor", jcol[:], cmpw[:, 0:1],
                     -float(FH), jcol[:], ALU.mult, ALU.add, sem=sV)
                r.wait("scalar", sV, sV.n)
                r.op("scalar", "activation", ifcol[:], jcol[:], AF.Exp,
                     scale=bcol(cNegLn), sem=sA)
                r.wait("vector", sA, sA.n)
                r.op("vector", "tensor_tensor", angw[:], angw[:],
                     bcast(ifcol, S), ALU.mult)
                r.op("vector", "tensor_single_scalar", angw[0:64, :],
                     angw[0:64, :], math.pi / 2.0, ALU.add)
                r.op("vector", "tensor_single_scalar", uw[:], angw[:],
                     1.0 / TWO_PI, ALU.mult)
                r.op("vector", "tensor_single_scalar", uw[:], uw[:], 0.5,
                     ALU.add)
                r.op("vector", "memset", kacc[:], 0.0)
                for b in (512, 256, 128, 64, 32, 16, 8, 4, 2, 1):
                    r.op("vector", "tensor_single_scalar", cmpw[:], uw[:],
                         float(b), ALU.is_ge)
                    r.op("vector", "scalar_tensor_tensor", uw[:], cmpw[:],
                         -float(b), uw[:], ALU.mult, ALU.add)
                    r.op("vector", "scalar_tensor_tensor", kacc[:], cmpw[:],
                         float(b), kacc[:], ALU.mult, ALU.add)
                r.op("vector", "scalar_tensor_tensor", angw[:], kacc[:],
                     -TWO_PI, angw[:], ALU.mult, ALU.add)
                r.op("vector", "tensor_single_scalar", angw[:], angw[:],
                     3.141592, ALU.min)
                r.op("vector", "tensor_single_scalar", angw[:], angw[:],
                     -3.141592, ALU.max, sem=sV)
                r.wait("scalar", sV, sV.n)
                r.op("scalar", "activation", uw[:], angw[:], AF.Sin, sem=sA)
                r.wait("vector", sA, sA.n)
                sc_ = r.op("vector", "tensor_single_scalar", uw[:], uw[:],
                           SQ4, ALU.mult, sem=sV)
                r.wait("vector", sV, sc_)
                r.op("vector", "tensor_copy", cosT[0:64, :], uw[0:64, :])
                r.op("vector", "tensor_copy", cosT[64:128, :], uw[0:64, :])
                r.op("vector", "tensor_copy", sinT[0:64, :], uw[64:128, :])
                r.op("vector", "tensor_copy", sinT[64:128, :], uw[64:128, :],
                     sem=sV)
                r.flush(nc)

            if "tables" in dbg:
                o = dbg_out("dbg_tables", [128, S])
                r = Rec()
                r.op("sync", "dma_start", o[:], cosT[:], sem=sOut, inc=16)
                r.wait("sync", sOut, sOut.n)
                r.flush(nc)
            if upto == "B1":
                return nc, dbg_outs

            xnrsT = [eE(nc.sbuf_tensor(f"xnrsT{k}", [128, S], BF16))
                     for k in range(ND)]

            with ExitStack() as c2:
                e2 = c2.enter_context
                xstg = [e2(nc.sbuf_tensor(f"xstg{i}", [128, D], F32))
                        for i in range(2)]
                sqdump = e2(nc.sbuf_tensor("sqdump", [128, D], BF16))
                xnb = e2(nc.sbuf_tensor("xnb", [128, D], BF16))
                xnstrip = e2(nc.sbuf_tensor("xnstrip", [128, D], BF16))
                s1c = e2(nc.sbuf_tensor("s1c", [128, 1], F32))
                s2c = e2(nc.sbuf_tensor("s2c", [128, 1], F32))
                meanc = e2(nc.sbuf_tensor("meanc", [128, 1], F32))
                varc = e2(nc.sbuf_tensor("varc", [128, 1], F32))
                r1c = e2(nc.sbuf_tensor("r1c", [128, 1], F32))
                uc = e2(nc.sbuf_tensor("uc", [128, 1], F32))
                rstdc = e2(nc.sbuf_tensor("rstdc", [128, 1], F32))
                gc = e2(nc.sbuf_tensor("gc", [128, 1], F32))
                cc = e2(nc.sbuf_tensor("cc", [128, 1], F32))
                bcl = e2(nc.sbuf_tensor("bcl", [128, 1], F32))
                trp = [e2(nc.psum_tensor(f"trp{i}", [128, 128], BF16))
                       for i in range(2)]

                r = Rec()
                ntot = NT + NTO
                norm_done = []
                ev_done = []
                trp_guard = [0, 0]
                spill_guard = 0
                for i in range(ntot):
                    own = i >= NT
                    ti = i - NT if own else i
                    src = x_own if own else x_full
                    slot = xstg[i % 2]
                    if i >= 2:
                        r.wait("sync", sA, norm_done[i - 2])
                    lsem = sLx[i % 2]
                    r.op("sync", "dma_start", slot[:],
                         src[ti * 128:(ti + 1) * 128, :], sem=lsem, inc=16)
                    ld = lsem.n
                    r.wait("vector", lsem, ld)
                    vv = r.op("vector", "reduce_sum", s1c[:], slot[:], AX.X,
                              sem=sV)
                    r.wait("scalar", lsem, ld)
                    if i >= 1:
                        r.wait("scalar", sV2, ev_done[i - 1])
                    sq = r.op("scalar", "activation", sqdump[:], slot[:],
                              AF.Square, accum_out=s2c[:, 0:1], sem=sA)
                    r.wait("vector", sA, sq)

                    def vchain(method, *args):
                        w = r.op("vector", method, *args, sem=sV)
                        r.wait("vector", sV, w)
                        return w

                    r.wait("vector", sV, vv)
                    vchain("tensor_single_scalar", meanc[:], s1c[:],
                           1.0 / D, ALU.mult)
                    vchain("tensor_single_scalar", varc[:], s2c[:],
                           1.0 / D, ALU.mult)
                    vchain("tensor_tensor", uc[:], meanc[:], meanc[:],
                           ALU.mult)
                    vchain("tensor_tensor", varc[:], varc[:], uc[:],
                           ALU.subtract)
                    vchain("tensor_single_scalar", r1c[:], varc[:],
                           EPS_LN, ALU.add)
                    vchain("reciprocal", r1c[:], r1c[:])
                    if own:
                        cols = vchain("tensor_copy", uc[:], r1c[:])
                    else:
                        vchain("tensor_tensor", uc[:], varc[:], r1c[:],
                               ALU.mult)
                        vchain("tensor_single_scalar", uc[:], uc[:],
                               EPS_F32, ALU.add)
                        cols = vchain("reciprocal", uc[:], uc[:])
                    r.wait("scalar", sV, cols)
                    r.op("scalar", "activation", rstdc[:], r1c[:], AF.Sqrt)
                    sqr = r.op("scalar", "activation", gc[:], uc[:], AF.Sqrt,
                               sem=sA)
                    r.wait("vector", sA, sqr)
                    if own:
                        ccv = r.op("vector", "tensor_copy", cc[:], rstdc[:],
                                   sem=sV)
                    else:
                        ccv = r.op("vector", "tensor_tensor", cc[:], rstdc[:],
                                   gc[:], ALU.mult, sem=sV)
                    r.wait("vector", sV, ccv)
                    cb = r.op("vector", "scalar_tensor_tensor", bcl[:],
                              meanc[:], -1.0, cc[:], ALU.mult, ALU.mult,
                              sem=sV)
                    r.wait("scalar", sV, cb)
                    nd_ = r.op("scalar", "activation", xnb[:], slot[:],
                               AF.Identity, bias=bcol(bcl), scale=bcol(cc),
                               sem=sA)
                    norm_done.append(nd_)
                    for k in range(ND):
                        ps = k % 2
                        r.wait("tensor", sA, nd_)
                        r.wait("tensor", sV2, trp_guard[ps])
                        tr = r.op("tensor", "transpose", trp[ps][:],
                                  xnb[:, k * 128:(k + 1) * 128], identb[:],
                                  sem=sT)
                        r.wait("vector", sT, tr)
                        if own and k == 0:
                            r.wait("vector", sSp, spill_guard)
                        if own:
                            evv = r.op("vector", "tensor_copy",
                                       xnstrip[:, k * 128:(k + 1) * 128],
                                       trp[ps][:], sem=sV2)
                        else:
                            evv = r.op("vector", "tensor_copy",
                                       xnrsT[k][:, ti * 128:(ti + 1) * 128],
                                       trp[ps][:], sem=sV2)
                        trp_guard[ps] = evv
                    ev_done.append(sV2.n)
                    if own:
                        r.wait("sync", sV2, ev_done[i])
                        r.op("sync", "dma_start",
                             bass.AP(xnT_sp, ti * 128,
                                     [[ND * TOWN, 128], [TOWN, ND], [1, 128]]),
                             bass.AP(xnstrip, 0,
                                     [[D, 128], [128, ND], [1, 128]]),
                             sem=sSp, inc=16)
                        spill_guard = sSp.n
                r.wait("sync", sSp, sSp.n)
                r.wait("sync", sLx[0], sLx[0].n)
                r.wait("sync", sLx[1], sLx[1].n)
                r.flush(nc)
            b2_spills = sSp.n

            if "xnrst" in dbg:
                o = dbg_out("dbg_xnrst", [ND * 128, S], BF16)
                o2 = dbg_out("dbg_xnt", [128, ND * TOWN], BF16)
                r = Rec()
                for k in range(ND):
                    r.op("sync", "dma_start", o[k * 128:(k + 1) * 128, :],
                         xnrsT[k][:], sem=sOut, inc=16)
                r.op("sync", "dma_start", o2[:], xnT_sp[:], sem=sOut,
                     inc=16)
                r.wait("sync", sOut, sOut.n)
                r.flush(nc)
            if upto == "B2":
                return nc, dbg_outs

            with ExitStack() as c3:
                e3 = c3.enter_context
                wqstg = e3(nc.sbuf_tensor("wqstg", [128, D // 2], F32))
                wqb = e3(nc.sbuf_tensor("wqb", [128, D], BF16))
                qT = e3(nc.sbuf_tensor("qT", [128, S], F32))
                t1w = e3(nc.sbuf_tensor("t1w", [128, 512], F32))
                t2w = e3(nc.sbuf_tensor("t2w", [128, 512], F32))
                qrb = e3(nc.sbuf_tensor("qrb", [128, S], BF16))
                vbst = e3(nc.sbuf_tensor("vbst", [128, S], BF16))
                qps = [e3(nc.psum_tensor(f"qps{i}", [128, 512], F32))
                       for i in range(2)]

                r = Rec()
                evac_hist = []
                vcopy_hist = []
                spill_hist = []
                for m in range(NH):
                    for hf in range(2):
                        if m >= 1 or hf == 1:
                            r.wait("sync", sV, sV.n)
                        r.op("sync", "dma_start", wqstg[:],
                             bass.AP(wq_t, m * 128 + hf * (ND // 2) * 128 * NH * DK,
                                     [[NH * DK, 128], [128 * NH * DK, ND // 2],
                                      [1, 128]]),
                             sem=sL, inc=16)
                        wq_ld = sL.n
                        r.wait("vector", sL, wq_ld)
                        if hf == 0 and m >= 1:
                            r.wait("vector", sT, sT.n)
                            r.wait("vector", sSp, spill_hist[m - 1])
                        for k2 in range(ND // 2):
                            k = hf * (ND // 2) + k2
                            r.op("vector", "tensor_tensor",
                                 wqb[:, k * 128:(k + 1) * 128],
                                 wqstg[:, k2 * 128:(k2 + 1) * 128],
                                 bcast(rmsg, 128, k), ALU.mult,
                                 sem=(sV if k2 == ND // 2 - 1 else None))
                    wq_cv = sV.n
                    for tp in range(8):
                        slot = qps[tp % 2]
                        r.wait("tensor", sV, wq_cv)
                        if tp >= 2:
                            r.wait("tensor", sA, evac_hist[-2])
                        for k in range(ND):
                            r.op("tensor", "matmul", slot[:],
                                 wqb[:, k * 128:(k + 1) * 128],
                                 xnrsT[k][:, tp * 512:(tp + 1) * 512],
                                 start=(k == 0), stop=(k == ND - 1),
                                 sem=(sT if k == ND - 1 else None))
                        r.wait("scalar", sT, sT.n)
                        if tp == 0 and m >= 1:
                            r.wait("scalar", sV2, vcopy_hist[m - 1])
                        ev = r.op("scalar", "activation",
                                  qT[:, tp * 512:(tp + 1) * 512], slot[:],
                                  AF.Identity, bias=wqbias[:, m:m + 1], sem=sA)
                        evac_hist.append(ev)
                    q_ev = evac_hist[-1]
                    r.wait("vector", sA, q_ev)
                    if m >= 1:
                        r.wait("vector", sSp, spill_hist[m - 1])
                    for ch in range(8):
                        a, b = ch * 512, (ch + 1) * 512
                        r.op("vector", "tensor_tensor", t1w[:],
                             qT[:, a:b], cosT[:, a:b], ALU.mult)
                        r.op("vector", "tensor_tensor", t2w[0:64, :],
                             qT[64:128, a:b], sinT[64:128, a:b], ALU.mult)
                        r.op("vector", "tensor_tensor", t2w[64:128, :],
                             qT[0:64, a:b], sinT[0:64, a:b], ALU.mult)
                        r.op("vector", "tensor_tensor", qrb[0:64, a:b],
                             t1w[0:64, :], t2w[0:64, :], ALU.subtract)
                        r.op("vector", "tensor_tensor", qrb[64:128, a:b],
                             t1w[64:128, :], t2w[64:128, :], ALU.add)
                    vc = r.op("vector", "tensor_copy", vbst[:], qT[:], sem=sV2)
                    vcopy_hist.append(vc)
                    r.wait("sync", sV2, vc)
                    r.op("sync", "dma_start", qr_dram[m], qrb[:], sem=sSp,
                         inc=16)
                    r.op("sync", "dma_start", v_dram[m], vbst[:], sem=sSp,
                         inc=16)
                    spill_hist.append(sSp.n)
                r.wait("sync", sSp, sSp.n)
                r.flush(nc)
            b3_spills = sSp.n

            if "qproj" in dbg:
                oq = dbg_out("dbg_qr", [NH * 128, S], BF16)
                ov = dbg_out("dbg_v", [NH * 128, S], BF16)
                r = Rec()
                for m in range(NH):
                    r.op("sync", "dma_start", oq[m * 128:(m + 1) * 128, :],
                         qr_dram[m], sem=sOut, inc=16)
                    r.op("sync", "dma_start", ov[m * 128:(m + 1) * 128, :],
                         v_dram[m], sem=sOut, inc=16)
                r.wait("sync", sOut, sOut.n)
                r.flush(nc)
            if upto == "B3":
                return nc, dbg_outs

        with ExitStack() as c4:
            e4 = c4.enter_context
            maskq = e4(nc.sbuf_tensor("maskq", [128, 4 * 512], F32))
            maskt = e4(nc.sbuf_tensor("maskt", [128, 4 * 512], F32))
            qrs = e4(nc.sbuf_tensor("qrs", [128, S], BF16))
            vfs = e4(nc.sbuf_tensor("vfs", [128, S], BF16))
            vsb = e4(nc.sbuf_tensor("vsb", [128, S], BF16))
            edump = vfs
            bnd = [e4(nc.sbuf_tensor(f"bnd{i}", [128, 512], F32))
                   for i in range(2)]
            nmstrip = [e4(nc.sbuf_tensor(f"nmstrip{i}", [128, 8], F32))
                       for i in range(2)]
            sstrip = [e4(nc.sbuf_tensor(f"sstrip{i}", [128, 8], F32))
                      for i in range(2)]
            dstrip = [e4(nc.sbuf_tensor(f"dstrip{i}", [128, 8], F32))
                      for i in range(2)]
            estrip = [e4(nc.sbuf_tensor(f"estrip{i}", [128, 8], F32))
                      for i in range(2)]
            negm = [e4(nc.sbuf_tensor(f"negm{i}", [128, 1], F32))
                    for i in range(2)]
            ssum = [e4(nc.sbuf_tensor(f"ssum{i}", [128, 1], F32))
                    for i in range(2)]
            lnsc = [e4(nc.sbuf_tensor(f"lnsc{i}", [128, 1], F32))
                    for i in range(2)]
            nmadj = [e4(nc.sbuf_tensor(f"nmadj{i}", [128, 1], F32))
                     for i in range(2)]
            nmcols = e4(nc.sbuf_tensor("nmcols", [128, NQT], F32))
            nmev = e4(nc.sbuf_tensor("nmev", [128, 512], F32))
            bplain = e4(nc.sbuf_tensor("bplain", [128, 512], F32))
            bplain2 = e4(nc.sbuf_tensor("bplain2", [128, 512], F32))
            bmask = [e4(nc.sbuf_tensor(f"bmask{i}", [128, 512], F32))
                     for i in range(4)]
            bmask2 = [e4(nc.sbuf_tensor(f"bmask2_{i}", [128, 512], F32))
                      for i in range(4)]
            sadj = [e4(nc.sbuf_tensor(f"sadj{i}", [128, 512], F32))
                    for i in range(3)]
            pTb = [e4(nc.sbuf_tensor(f"pTb{i}", [128, 512], BF16))
                   for i in range(3)]
            oTg = [e4(nc.sbuf_tensor(f"oTg{i}", [128, 512], BF16))
                   for i in range(2)]
            xnto = e4(nc.sbuf_tensor("xnto", [128, ND * TOWN], BF16))
            wstg = e4(nc.sbuf_tensor("wstg", [128, D], F32))
            w1b = [e4(nc.sbuf_tensor(f"w1b{i}", [128, D], BF16))
                   for i in range(2)]
            w3b = [e4(nc.sbuf_tensor(f"w3b{i}", [128, D], BF16))
                   for i in range(2)]
            stmp = e4(nc.sbuf_tensor("stmp", [128, 512], F32))
            hbd = [e4(nc.sbuf_tensor(f"hbd{i}", [128, 512], BF16))
                   for i in range(2)]
            s1strip = e4(nc.sbuf_tensor("s1strip", [128, TOWN], BF16))
            w2stg = e4(nc.sbuf_tensor("w2stg", [128, D], F32))
            w2sb = e4(nc.sbuf_tensor("w2sb", [128, D], BF16))
            w2fout = e4(nc.sbuf_tensor("w2fout", [128, D], BF16))
            qkps = [e4(nc.psum_tensor(f"qkps{i}", [128, 512], F32))
                    for i in range(2)]
            stps = [e4(nc.psum_tensor(f"stps{i}", [128, 512], F32))
                    for i in range(2)]
            avps = e4(nc.psum_tensor("avps", [128, 512], F32))
            fps = [e4(nc.psum_tensor(f"fps{i}", [128, 512], F32))
                   for i in range(3)]
            bps = qkps[0]
            nmps = qkps[0][0:1, 0:128]
            vtp = [qkps[i][:, 0:64].bitcast(BF16) for i in range(2)]

            bankg = {}
            r = Rec()

            def gwait(eng, name):
                g_ = bankg.get(name)
                if g_ is not None:
                    r.wait(eng, g_[0], g_[1])

            def gset(name, semv):
                bankg[name] = semv

            r.op("sync", "dma_start",
                 bass.AP(maskq, 0, [[4 * 512, 128], [512, 4], [1, 512]]),
                 bass.AP(maskq_i, 0, [[512, 128], [128 * 512, 4], [1, 512]]),
                 sem=sL, inc=16)
            r.op("sync", "dma_start",
                 bass.AP(maskt, 0, [[4 * 512, 128], [512, 4], [1, 512]]),
                 bass.AP(maskt_i, 0, [[512, 128], [128 * 512, 4], [1, 512]]),
                 sem=sL, inc=16)
            mask_ld = sL.n
            r.wait("sync", sSp, b2_spills)
            r.op("sync", "dma_start", xnto[:], xnT_sp[:], sem=sLxn, inc=16)
            xn_ld = sLxn.n

            unit_done = []
            exp1_hist = []
            prev_unit_av = 0
            oTg_free = [0, 0]
            silu_last = 0
            mult_last = 0
            h128_last = 0
            wcv = [0, 0]
            pend_h128 = None
            wstg_free = 0
            w2cv = 0
            wo_cv = 0
            s1_sp = 0

            def q_wload(hi):
                nonlocal wstg_free, h128_last
                ws = hi % 2
                r.wait("sync", sV, wstg_free)
                r.op("sync", "dma_start", wstg[:],
                     bass.AP(w1_t, hi * 128, [[F, 128], [128 * F, ND],
                                              [1, 128]]), sem=sLwf, inc=16)
                r.wait("vector", sLwf, sLwf.n)
                if h128_last:
                    r.wait("vector", sT, h128_last)
                cv1 = r.op("vector", "tensor_copy", w1b[ws][:], wstg[:],
                           sem=sV)
                r.wait("sync", sV, cv1)
                r.op("sync", "dma_start", wstg[:],
                     bass.AP(w3_t, hi * 128, [[F, 128], [128 * F, ND],
                                              [1, 128]]), sem=sLwf, inc=16)
                r.wait("vector", sLwf, sLwf.n)
                cv2 = r.op("vector", "tensor_copy", w3b[ws][:], wstg[:],
                           sem=sV)
                wcv[ws] = cv2
                wstg_free = cv2

            def q_ffn(hi, tp):
                nonlocal silu_last, mult_last, h128_last, s1_sp, pend_h128
                ws = hi % 2
                if hi == 0 and tp == 0:
                    r.wait("tensor", sLxn, xn_ld)
                r.wait("tensor", sV, wcv[ws])
                if silu_last:
                    r.wait("tensor", sA, silu_last)
                for k in range(ND):
                    r.op("tensor", "matmul", fps[0][:],
                         w1b[ws][:, k * 128:(k + 1) * 128],
                         xnto[:, k * TOWN + tp * 512:
                               k * TOWN + (tp + 1) * 512],
                         start=(k == 0), stop=(k == ND - 1),
                         sem=(sT if k == ND - 1 else None))
                up_done = sT.n
                if mult_last:
                    r.wait("tensor", sV2, mult_last)
                for k in range(ND):
                    r.op("tensor", "matmul", fps[1][:],
                         w3b[ws][:, k * 128:(k + 1) * 128],
                         xnto[:, k * TOWN + tp * 512:
                               k * TOWN + (tp + 1) * 512],
                         start=(k == 0), stop=(k == ND - 1),
                         sem=(sT if k == ND - 1 else None))
                lin_done = sT.n
                do_h128()
                r.wait("scalar", sT, up_done)
                if mult_last:
                    r.wait("scalar", sV2, mult_last)
                sl_ = r.op("scalar", "activation", stmp[:], fps[0][:],
                           AF.Silu, sem=sA)
                silu_last = sl_
                r.wait("vector", sA, sl_)
                r.wait("vector", sT, lin_done)
                if h128_last:
                    r.wait("vector", sT, h128_last)
                ml = r.op("vector", "tensor_tensor",
                          hbd[(hi * 4 + tp) % 2][:], stmp[:], fps[1][:],
                          ALU.mult, sem=sV2)
                mult_last = ml
                pend_h128 = (hi, tp, ml, (hi * 4 + tp) % 2)

            def do_h128():
                nonlocal h128_last, s1_sp, pend_h128
                if pend_h128 is None:
                    return
                hi, tp, ml, hs = pend_h128
                pend_h128 = None
                cols = slice(tp * 512, (tp + 1) * 512)
                r.wait("tensor", sV2, ml)
                gwait("tensor", "f2")
                hm = r.op("tensor", "matmul", fps[2][:], h128b[:],
                          hbd[hs][:], start=True, stop=True, sem=sT)
                h128_last = hm
                r.wait("scalar", sT, hm)
                if tp == 0:
                    r.wait("scalar", sSpS1, s1_sp)
                s1e = r.op("scalar", "activation", s1strip[:, cols],
                           fps[2][:], AF.Copy, scale=bcol(cIsq128), sem=sA)
                gset("f2", (sA, s1e))
                if tp == 3:
                    r.wait("sync", sA, s1e)
                    r.op("sync", "dma_start", s1_dram[hi], s1strip[:],
                         sem=sSpS1, inc=16)
                    s1_sp = sSpS1.n

            def q_w2f(lam):
                nonlocal w2cv, wo_cv
                do_h128()
                r.wait("sync", sV, w2cv)
                r.wait("sync", sG, wo_cv)
                r.op("sync", "dma_start", w2stg[:],
                     bass.AP(w2_s, lam * D, [[(128 // GP) * D, 128], [1, D]]),
                     sem=sLw2m, inc=16)
                r.wait("vector", sLw2m, sLw2m.n)
                r.wait("vector", sT, sT.n)
                w2cv = r.op("vector", "tensor_copy", w2sb[:], w2stg[:],
                            sem=sV)
                w2ev = 0
                for q4 in range(4):
                    r.wait("tensor", sV, w2cv)
                    gwait("tensor", "f2")
                    r.op("tensor", "matmul", fps[2][:], bdb[:],
                         w2sb[:, q4 * 512:(q4 + 1) * 512], start=True,
                         stop=True, sem=sT)
                    r.wait("scalar", sT, sT.n)
                    if q4 == 0:
                        r.wait("scalar", sSpW2, sSpW2.n)
                        r.wait("scalar", sSpWo, sSpWo.n)
                    w2ev = r.op("scalar", "activation",
                                w2fout[:, q4 * 512:(q4 + 1) * 512],
                                fps[2][:], AF.Copy, scale=bcol(cIsq64),
                                sem=sA)
                    gset("f2", (sA, w2ev))
                r.wait("sync", sA, w2ev)
                for gi in range(GP):
                    ap_sb2 = bass.AP(w2fout, gi * FH * D,
                                     [[D, FH], [128, ND], [1, 128]])
                    ap_dr2 = bass.AP(w2f_dram, lam * GP * 128 + gi * 128,
                                     [[128 * 128, FH], [FH * 128 * 128, ND],
                                      [1, 128]])
                    r.op("sync", "dma_start", ap_dr2, ap_sb2, sem=sSpW2,
                         inc=16)

            def q_wo(k):
                nonlocal wo_cv
                r.wait("sync", sV, w2cv)
                r.wait("sync", sG, wo_cv)
                r.op("sync", "dma_start", w2stg[:],
                     wo_t[k * 128:(k + 1) * 128, :], sem=sLwom, inc=16)
                r.wait("gpsimd", sLwom, sLwom.n)
                r.wait("gpsimd", sSpW2, sSpW2.n)
                r.wait("gpsimd", sSpWo, sSpWo.n)
                r.wait("gpsimd", sA, sA.n)
                wo_cv = r.op("gpsimd", "tensor_copy", w2fout[:], w2stg[:],
                             sem=sG)
                r.wait("sync", sG, wo_cv)
                r.op("sync", "dma_start", wob_dram[k * 128:(k + 1) * 128, :],
                     w2fout[:], sem=sSpWo, inc=16)

            def emit_unit(h):
                nonlocal prev_unit_av
                qsl = qrs
                quanta = []
                for hi in range(8 * h, 8 * h + 8):
                    quanta.append(("wl", hi))
                    for tp in range(4):
                        quanta.append(("tp", hi, tp))
                qi = [0]

                def pump(n=1):
                    for _ in range(n):
                        if qi[0] >= len(quanta):
                            return
                        q = quanta[qi[0]]
                        qi[0] += 1
                        if q[0] == "wl":
                            q_wload(q[1])
                        else:
                            q_ffn(q[1], q[2])

                if h == 0:
                    r.wait("sync", sSp, b3_spills)
                if h >= 2:
                    r.wait("sync", sA2, unit_done[h - 2])
                usem = sLu[h % 2]
                if h >= 1:
                    r.wait("sync", sT2, prev_unit_av)
                    r.wait("sync", sA, exp1_hist[-1])
                r.op("sync", "dma_start", qsl[:], qr_dram[h], sem=usem,
                     inc=16)
                r.op("sync", "dma_start", vfs[:], v_dram[h], sem=usem,
                     inc=16)
                u_ld = usem.n
                for j in range(NT):
                    ps = j % 2
                    r.wait("tensor", usem, u_ld)
                    gwait("tensor", f"q{ps}")
                    tr = r.op("tensor", "transpose", vtp[ps],
                              vfs[:, j * 128:(j + 1) * 128], identb[:],
                              sem=sT)
                    r.wait("vector", sT, tr)
                    if j == 0:
                        r.wait("vector", sT2, prev_unit_av)
                    evv = r.op("vector", "tensor_copy",
                               vsb[:, j * 128:(j + 1) * 128], vtp[ps],
                               sem=sV2)
                    gset(f"q{ps}", (sV2, evv))
                pend_tail = None
                nmcol_done = [0] * NQT

                def do_tail(G, par, t, exp_done):
                    r.wait("vector", sA, exp_done)
                    r.op("vector", "tensor_reduce", negm[par][:],
                         nmstrip[par][:, 0:G], AX.X, ALU.min, sem=sV)
                    r.op("vector", "tensor_tensor", dstrip[par][:, 0:G],
                         bcast(negm[par], G), nmstrip[par][:, 0:G],
                         ALU.subtract, sem=sV)
                    r.wait("scalar", sV, sV.n)
                    ee = r.op("scalar", "activation", estrip[par][:, 0:G],
                              dstrip[par][:, 0:G], AF.Exp, sem=sA)
                    r.wait("vector", sA, ee)
                    r.op("vector", "tensor_tensor", estrip[par][:, 0:G],
                         estrip[par][:, 0:G], sstrip[par][:, 0:G], ALU.mult,
                         sem=sV)
                    sm = r.op("vector", "tensor_reduce", ssum[par][:],
                              estrip[par][:, 0:G], AX.X, ALU.add, sem=sV)
                    r.wait("scalar", sV, sm)
                    lnv = r.op("scalar", "activation", lnsc[par][:],
                               ssum[par][:], AF.Ln, sem=sA)
                    exp1_hist.append(lnv)
                    r.wait("vector", sA, lnv)
                    nmc = r.op("vector", "tensor_tensor", nmcols[:, t:t + 1],
                               negm[par][:], lnsc[par][:], ALU.subtract,
                               sem=sV)
                    nmcol_done[t] = nmc
                    pump(1)

                for t in range(NQT):
                    G = t // 4 + 1
                    rr = t % 4
                    par = t % 2
                    for j in range(G):
                        ps = j % 2
                        r.wait("tensor", usem, u_ld)
                        gwait("tensor", f"q{ps}")
                        mm = r.op("tensor", "matmul", qkps[ps][:],
                                  qsl[:, t * 128:(t + 1) * 128],
                                  qsl[:, j * 512:(j + 1) * 512], start=True,
                                  stop=True, sem=sT)
                        if j == G - 1:
                            r.wait("vector", sT, mm)
                            r.wait("vector", sL, mask_ld)
                            r.wait("vector", sA, sA.n)
                            bv = r.op("vector", "tensor_tensor", bnd[par][:],
                                      qkps[ps][:],
                                      maskq[:, rr * 512:(rr + 1) * 512],
                                      ALU.add, sem=sV)
                            gset(f"q{ps}", (sV, bv))
                            r.op("vector", "tensor_reduce",
                                 nmstrip[par][:, j:j + 1], bnd[par][:],
                                 AX.X, ALU.max, negate=True, sem=sV)
                            r.wait("scalar", sV, sV.n)
                            ex = r.op("scalar", "activation",
                                      edump[:, (t % 4) * 1024:
                                            (t % 4) * 1024 + 512],
                                      bnd[par][:], AF.Exp,
                                      bias=nmstrip[par][:, j:j + 1],
                                      accum_out=sstrip[par][:, j:j + 1],
                                      sem=sA)
                        else:
                            r.wait("vector", sT, mm)
                            rd = r.op("vector", "tensor_reduce",
                                      nmstrip[par][:, j:j + 1], qkps[ps][:],
                                      AX.X, ALU.max, negate=True, sem=sV)
                            r.wait("scalar", sV, rd)
                            ex = r.op("scalar", "activation",
                                      edump[:, (t % 4) * 1024 +
                                            (j % 2) * 512:
                                            (t % 4) * 1024 +
                                            (j % 2) * 512 + 512],
                                      qkps[ps][:], AF.Exp,
                                      bias=nmstrip[par][:, j:j + 1],
                                      accum_out=sstrip[par][:, j:j + 1],
                                      sem=sA)
                            gset(f"q{ps}", (sA, ex))
                    exp_done = sA.n
                    if pend_tail is not None:
                        do_tail(*pend_tail)
                    pend_tail = (G, t % 2, t, exp_done)
                if pend_tail is not None:
                    do_tail(*pend_tail)
                    pend_tail = None
                bpl = [bplain, bplain2]
                bmk = [bmask, bmask2]
                bev_as = [0, 0]
                bev_vs = [0, 0]

                def build_b(g):
                    gwait("tensor", "q0")
                    for rr2 in range(4):
                        j = 4 * g + rr2
                        r.wait("tensor", sV, nmcol_done[j])
                        if rr2:
                            r.wait("tensor", sA, sA.n)
                        tr = r.op("tensor", "transpose", nmps,
                                  nmcols[:, j:j + 1], identf[:], sem=sT)
                        r.wait("scalar", sT, tr)
                        r.op("scalar", "copy",
                             nmev[0:1, rr2 * 128:(rr2 + 1) * 128], nmps,
                             sem=sA)
                    r.wait("tensor", sA, sA.n)
                    for rr2 in range(4):
                        r.op("tensor", "matmul",
                             bps[:, rr2 * 128:(rr2 + 1) * 128],
                             onesf,
                             nmev[0:1, rr2 * 128:(rr2 + 1) * 128],
                             start=True, stop=True,
                             sem=(sT if rr2 == 3 else None))
                    bmm = sT.n
                    r.wait("scalar", sT, bmm)
                    ba = r.op("scalar", "copy", bpl[g % 2][:], bps[:],
                              sem=sA)
                    r.wait("vector", sT, bmm)
                    r.wait("vector", sL, mask_ld)
                    r.wait("vector", sA, ba)
                    for rr2 in range(4):
                        r.op("vector", "tensor_tensor", bmk[g % 2][rr2][:],
                             bps[:], maskt[:, rr2 * 512:(rr2 + 1) * 512],
                             ALU.add, sem=(sV if rr2 == 3 else None))
                    bev_as[g % 2] = ba
                    bev_vs[g % 2] = sV.n
                    gset("q0", (sV, sV.n))

                build_b(0)
                for g in range(NQG):
                    nk = 4 * (g + 1)
                    gav = []
                    bev_a = bev_as[g % 2]
                    bev_v = bev_vs[g % 2]
                    bplain_g = bpl[g % 2]
                    bmask_g = bmk[g % 2]
                    sub_hist = []
                    exp_hist = []
                    stl = [stps[0], stps[1], qkps[1]]
                    for jk in range(nk):
                        p3 = jk % 3
                        if jk >= 3:
                            r.wait("tensor", sV, sub_hist[jk - 3])
                        elif jk == 0:
                            gwait("tensor", "q1")
                        qkmm = r.op("tensor", "matmul", stl[p3][:],
                                    qsl[:, jk * 128:(jk + 1) * 128],
                                    qsl[:, g * 512:(g + 1) * 512],
                                    start=True, stop=True, sem=sT)
                        if jk >= 2:
                            r.wait("tensor", sA, exp_hist[jk - 2])
                            if jk == 2:
                                gwait("tensor", "av")
                            av = r.op("tensor", "matmul", avps[:],
                                      vsb[:, (jk - 2) * 128:
                                          (jk - 1) * 128],
                                      pTb[(jk - 2) % 3][:],
                                      start=(jk - 2 == 0), stop=False,
                                      skip_group_check=True, sem=sT2)
                            gav.append(av)
                        r.wait("vector", sT, qkmm)
                        if jk < 4 * g:
                            r.wait("vector", sA, bev_a)
                            bsrc = bplain_g
                        else:
                            r.wait("vector", sV, bev_v)
                            bsrc = bmask_g[jk - 4 * g]
                        if jk >= 3:
                            r.wait("vector", sA, exp_hist[jk - 3])
                        sb = r.op("vector", "tensor_tensor", sadj[p3][:],
                                  stl[p3][:], bsrc[:], ALU.add, sem=sV)
                        sub_hist.append(sb)
                        r.wait("scalar", sV, sb)
                        if jk >= 3:
                            r.wait("scalar", sT2,
                                   gav[jk - 3] if jk - 3 < len(gav)
                                   else sT2.n)
                        ex = r.op("scalar", "activation", pTb[p3][:],
                                  sadj[p3][:], AF.Exp, sem=sA)
                        exp_hist.append(ex)
                    for jj in (nk - 2, nk - 1):
                        if jj < 0:
                            continue
                        r.wait("tensor", sA, exp_hist[jj])
                        gwait("tensor", "av")
                        av = r.op("tensor", "matmul", avps[:],
                                  vsb[:, jj * 128:(jj + 1) * 128],
                                  pTb[jj % 3][:],
                                  start=(jj == 0), stop=(jj == nk - 1),
                                  skip_group_check=True, sem=sT2)
                        gav.append(av)
                    gset("q1", (sV, sub_hist[-1] if nk >= 3 else sV.n))
                    if True:
                        pass
                    if g + 1 < NQG:
                        build_b(g + 1)
                    r.wait("scalar", sT2, gav[-1])
                    gs = g % 2
                    r.wait("scalar", sSpO[gs], oTg_free[gs])
                    oe = r.op("scalar", "activation", oTg[gs][:], avps[:],
                              AF.Copy, sem=sA2)
                    gset("av", (sA2, oe))
                    r.wait("tensor", sA2, oe)
                    r.wait("sync", sA2, oe)
                    r.op("sync", "dma_start",
                         o_cc_in[h * 128:(h + 1) * 128,
                                 g * 512:(g + 1) * 512],
                         oTg[gs][:], sem=sSpO[gs], inc=16)
                    oTg_free[gs] = sSpO[gs].n
                    pump(1)
                    q_w2f(8 * h + g)
                    if g >= 6:
                        q_wo(2 * h + (g - 6))
                pump(99)
                unit_done.append(sA2.n)
                prev_unit_av = sT2.n

            for h in range(NH):
                emit_unit(h)

            r.wait("gpsimd", sSpO[0], sSpO[0].n)
            r.wait("gpsimd", sSpO[1], sSpO[1].n)

            def do_cc(gp):
                gp.collective_compute(
                    "AllGather", ALU.bypass,
                    replica_groups=[[0, 1], [2, 3], [4, 5], [6, 7]],
                    ins=[o_cc_in.ap().opt()], outs=[o_cc_out.ap().opt()],
                ).then_inc(sCC.h, 1)
            sCC.n += 1
            r.call("gpsimd", do_cc)
            for s_ in (sSp, sSpS1, sSpW2, sSpWo, sL, sLxn, sLwf, sLw2m,
                       sLwom, sLu[0], sLu[1], sSpO[0], sSpO[1]):
                r.wait("sync", s_, s_.n)
            r.flush(nc)
        b5_spill_state = [(s_, s_.n) for s_ in
                          (sSp, sSpS1, sSpW2, sSpWo, sSpO[0], sSpO[1])]

        if "attn" in dbg:
            o = dbg_out("dbg_oT", [NH * 128, S], BF16)
            r = Rec()
            r.wait("sync", sSp, sSp.n)
            r.op("sync", "dma_start", o[:], o_cc_in[:], sem=sOut, inc=16)
            r.wait("sync", sOut, sOut.n)
            r.flush(nc)
        if upto == "B4":
            return nc, dbg_outs

        if "ffnup" in dbg:
            o = dbg_out("dbg_s1", [FH * 128, TOWN], BF16)
            o2 = dbg_out("dbg_w2f", [ND * FH * 128, 128], BF16)
            r = Rec()
            r.wait("sync", sSp, sSp.n)
            for hi in range(FH):
                r.op("sync", "dma_start", o[hi * 128:(hi + 1) * 128, :],
                     s1_dram[hi], sem=sOut, inc=16)
            r.op("sync", "dma_start", o2.ap(),
                 w2f_dram.ap().rearrange("m h p d -> (m h p) d"), sem=sOut,
                 inc=16)
            r.wait("sync", sOut, sOut.n)
            r.flush(nc)
        if upto == "B5":
            return nc, dbg_outs

        with ExitStack() as c6:
            e6 = c6.enter_context
            acc = e6(nc.sbuf_tensor("acc", [128, ND * 512], F32))
            ogst = e6(nc.sbuf_tensor("ogst", [128, ND * 512], BF16))
            s1t = e6(nc.sbuf_tensor("s1t", [128, FH * 512], BF16))
            w2fst = e6(nc.sbuf_tensor("w2fst", [128, FH * 128], BF16))
            wost = e6(nc.sbuf_tensor("wost", [128, D], BF16))
            xnst = e6(nc.sbuf_tensor("xnst", [128, ND * 512], BF16))
            ops_ = [e6(nc.psum_tensor(f"ops{i}", [128, 512], F32))
                    for i in range(2)]
            dps = [e6(nc.psum_tensor(f"dps{i}", [128, 512], F32))
                   for i in range(2)]

            r = Rec()
            for s_, v_ in b5_spill_state:
                r.wait("sync", s_, v_)
            out_hist = 0
            oev_hist = []
            dev_hist = []
            for t4 in range(TOWN // 512):
                if t4 > 0:
                    r.wait("sync", sV, sV.n)
                    r.wait("sync", sV2, sV2.n)
                    r.wait("sync", sOut, out_hist)
                r.op("sync", "dma_start",
                     bass.AP(acc, 0, [[ND * 512, 128], [512, ND], [1, 512]]),
                     bass.AP(x_own_tr, t4 * 512,
                             [[TOWN, 128], [128 * TOWN, ND], [1, 512]]),
                     sem=sLb6, inc=16)
                r.op("sync", "dma_start",
                     bass.AP(xnst, 0, [[ND * 512, 128], [512, ND], [1, 512]]),
                     bass.AP(xnT_sp, t4 * 512,
                             [[ND * TOWN, 128], [TOWN, ND], [1, 512]]),
                     sem=sLb6, inc=16)
                r.op("sync", "dma_start",
                     bass.AP(s1t, 0, [[FH * 512, 128], [512, FH], [1, 512]]),
                     bass.AP(s1_dram, t4 * 512,
                             [[TOWN, 128], [128 * TOWN, FH], [1, 512]]),
                     sem=sLb6, inc=16)
                base_ld = sLb6.n

                def og_load(gp, t4=t4):
                    with gp.register(f"rgo{t4}") as rg:
                        gp.reg_load(rg, coloff)
                        if t4:
                            gp.reg_add(rg, rg, t4 * 512)
                        gp.dma_start(
                            bass.AP(ogst, 0, [[ND * 512, 128], [512, ND],
                                              [1, 512]]),
                            bass.AP(o_cc_out, rg,
                                    [[S, 128], [128 * S, ND], [1, 512]])
                        ).then_inc(sLog.h, 16)
                if t4 > 0:
                    r.wait("gpsimd", sT, sT.n)
                r.wait("gpsimd", sCC, sCC.n)
                r.call("gpsimd", og_load)
                sLog.n += 16
                og_ld = sLog.n

                r.wait("vector", sLb6, base_ld)
                for k in range(ND):
                    r.op("vector", "tensor_tensor",
                         acc[:, k * 512:(k + 1) * 512],
                         acc[:, k * 512:(k + 1) * 512],
                         bcast(wobias, 512, k), ALU.add)
                init_done = r.op("vector", "tensor_tensor", acc[:], acc[:],
                                 xnst[:], ALU.add, sem=sV)

                wo_lds = []
                for m in range(ND):
                    r.wait("sync", sT, sT.n)
                    r.op("sync", "dma_start", wost[:],
                         bass.AP(wob_dram, m * 128,
                                 [[D, 128], [128 * D, ND], [1, 128]]),
                         sem=sLwo6, inc=16)
                    wo_lds.append(sLwo6.n)
                    slot = ops_[m % 2]
                    r.wait("tensor", sLwo6, wo_lds[m])
                    r.wait("tensor", sLog, og_ld)
                    if m >= 2:
                        r.wait("tensor", sV, oev_hist[m - 2])
                    for k in range(ND):
                        r.op("tensor", "matmul", slot[:],
                             wost[:, k * 128:(k + 1) * 128],
                             ogst[:, k * 512:(k + 1) * 512],
                             start=(k == 0), stop=(k == ND - 1),
                             sem=(sT if k == ND - 1 else None))
                    r.wait("vector", sT, sT.n)
                    if m == 0:
                        r.wait("vector", sV, init_done)
                    ov_ = r.op("vector", "tensor_tensor",
                               acc[:, m * 512:(m + 1) * 512], slot[:],
                               acc[:, m * 512:(m + 1) * 512], ALU.add, sem=sV)
                    oev_hist.append(ov_)
                for m in range(ND):
                    r.wait("sync", sT2, sT2.n)
                    r.op("sync", "dma_start",
                         bass.AP(w2fst, 0, [[FH * 128, 128], [128, FH],
                                            [1, 128]]),
                         bass.AP(w2f_dram, m * FH * 128 * 128,
                                 [[128, 128], [128 * 128, FH], [1, 128]]),
                         sem=sLw26, inc=16)
                    w2_ld = sLw26.n
                    slot = dps[m % 2]
                    r.wait("tensor", sLw26, w2_ld)
                    if m >= 2:
                        r.wait("tensor", sV2, dev_hist[m - 2])
                    for hi in range(FH):
                        r.op("tensor", "matmul", slot[:],
                             w2fst[:, hi * 128:(hi + 1) * 128],
                             s1t[:, hi * 512:(hi + 1) * 512],
                             start=(hi == 0), stop=(hi == FH - 1),
                             sem=(sT2 if hi == FH - 1 else None))
                    r.wait("vector", sT2, sT2.n)
                    dv_ = r.op("vector", "tensor_tensor",
                               acc[:, m * 512:(m + 1) * 512], slot[:],
                               acc[:, m * 512:(m + 1) * 512], ALU.add,
                               sem=sV2)
                    dev_hist.append(dv_)
                oev_hist = []
                dev_hist = []
                r.wait("sync", sV, sV.n)
                r.wait("sync", sV2, sV2.n)
                r.op("sync", "dma_start",
                     bass.AP(out_t, t4 * 512,
                             [[TOWN, 128], [128 * TOWN, ND], [1, 512]]),
                     bass.AP(acc, 0, [[ND * 512, 128], [512, ND], [1, 512]]),
                     sem=sOut, inc=16)
                out_hist = sOut.n
            r.wait("sync", sOut, sOut.n)
            r.flush(nc)

    return nc, dbg_outs


def _hadamard(n):
    h = np.array([[1.0]])
    while h.shape[0] < n:
        h = np.block([[h, h], [h, -h]])
    return h


def make_core_inputs(inputs, core):
    X = np.asarray(inputs["L0_attn_X"])
    Wq = np.asarray(inputs["L0_attn_Wq_w"])
    Wqb = np.asarray(inputs["L0_attn_Wq_b"])
    Wo = np.asarray(inputs["L0_attn_Wo_w"])
    Wob = np.asarray(inputs["L0_attn_Wo_b"])
    g = np.asarray(inputs["L0_attn_rms_w"])
    W1 = np.asarray(inputs["L0_ffn_W1"])
    W2 = np.asarray(inputs["L0_ffn_W2"])
    W3 = np.asarray(inputs["L0_ffn_W3"])

    s, half = core // 2, core % 2
    ho = half * NH
    xs = np.ascontiguousarray(X[s], dtype=np.float32)
    x_own = np.ascontiguousarray(xs[half * TOWN:(half + 1) * TOWN])
    d = {
        "x_full": xs,
        "x_own": x_own,
        "x_own_tr": np.ascontiguousarray(x_own.T),
        "wq_t": np.ascontiguousarray(Wq[ho * DK:(ho + NH) * DK, :].T),
        "wq_bh": np.ascontiguousarray(
            Wqb[ho * DK:(ho + NH) * DK].reshape(NH, 128).T),
        "rms_gc": np.ascontiguousarray(g.reshape(ND, 128).T),
        "wo_t": np.ascontiguousarray(Wo.T),
        "wo_bc": np.ascontiguousarray(Wob.reshape(ND, 128).T),
        "w1_t": np.ascontiguousarray(W1.T),
        "w3_t": np.ascontiguousarray(W3.T),
        "coloff_i": np.array([[half * TOWN]], dtype=np.int32),
    }
    T = np.ascontiguousarray(W2.T).reshape(FH, 128, D)
    T2 = T.reshape(FH, 128 // GP, GP, D)
    W2s = np.ascontiguousarray(T2.transpose(2, 0, 1, 3)).reshape(
        128, (128 // GP) * D)
    d["w2_s"] = W2s

    h128 = _hadamard(128)
    h64 = _hadamard(FH)
    bd = np.zeros((128, 128), dtype=np.float64)
    for gi in range(GP):
        bd[gi * FH:(gi + 1) * FH, gi * FH:(gi + 1) * FH] = h64
    d["h128_i"] = h128.astype(BF)
    d["bd_i"] = bd.astype(BF)
    d["identb_i"] = np.eye(128).astype(BF)
    d["identf_i"] = np.eye(128, dtype=np.float32)
    d["onesf_i"] = np.ones((1, 128), dtype=np.float32)
    qi = np.arange(128)[:, None]
    ck = np.arange(512)[None, :]
    mq = np.stack([np.where(ck <= 128 * rr + qi, 0.0, NEG_BIG)
                   for rr in range(4)]).astype(np.float32)
    mt = np.stack([np.where(ck >= 128 * rr + qi, 0.0, NEG_BIG)
                   for rr in range(4)]).astype(np.float32)
    d["maskq_i"] = np.ascontiguousarray(mq.reshape(4 * 128, 512))
    d["maskt_i"] = np.ascontiguousarray(mt.reshape(4 * 128, 512))
    return d


_BUILD_CACHE = {}


def kernel(**inputs):
    if "full" not in _BUILD_CACHE:
        _BUILD_CACHE["full"] = build_program()
    nc, _ = _BUILD_CACHE["full"]
    in_maps = [make_core_inputs(inputs, c) for c in range(8)]
    res = run_bass_kernel_spmd(nc, in_maps, core_ids=list(range(8)))
    out = np.empty((4, S, D), dtype=np.float32)
    for c in range(8):
        s, half = c // 2, c % 2
        ot = np.asarray(res.results[c]["out_t"])
        out[s, half * TOWN:(half + 1) * TOWN, :] = ot.T
    return out


# version 07
# speedup vs baseline: 1.2954x; 1.0999x over previous
import math
from contextlib import ExitStack

import numpy as np
import ml_dtypes

import concourse.bass as bass
import concourse.mybir as mybir
from concourse.bass_utils import run_bass_kernel_spmd

F32 = mybir.dt.float32
BF16 = mybir.dt.bfloat16
I32 = mybir.dt.int32
AF = mybir.ActivationFunctionType
ALU = mybir.AluOpType
AX = mybir.AxisListType
BF = ml_dtypes.bfloat16

S = 4096
D = 2048
DK = 128
NH = 8
F = 8192
FH = F // 128
GP = 128 // FH
TOWN = S // 2
NT = S // 128
NTO = TOWN // 128
ND = D // 128
NQT = S // 128
NQG = S // 512
EPS_LN = 1e-5
EPS_F32 = float(np.finfo(np.float32).eps)
SQ4 = 128.0 ** (-0.25)
ISQ128 = 1.0 / math.sqrt(128.0)
ISQ64 = 1.0 / math.sqrt(float(FH))
TWO_PI = 2.0 * math.pi
NEG_BIG = -1.0e30

ENGS = ("sync", "vector", "scalar", "tensor", "gpsimd")


class Sem:
    __slots__ = ("h", "n", "name")

    def __init__(self, h, name):
        self.h, self.n, self.name = h, 0, name


_CHAIN_ENGS = ("vector", "scalar", "gpsimd")
_NO_CHAIN_METHODS = {"dma_start", "nop", "engine_nop"}


def _ap_name(a):
    t = getattr(a, "tensor", None)
    return getattr(t, "name", None) if t is not None else None


def _rw(method, args, kw):
    writes, reads = set(), set()
    aps = [a for a in args if _ap_name(a)]
    if method in ("memset", "iota"):
        if aps:
            writes.add(_ap_name(aps[0]))
    else:
        if aps:
            writes.add(_ap_name(aps[0]))
        for a in aps[1:]:
            reads.add(_ap_name(a))
    for k, v in kw.items():
        n = _ap_name(v)
        if n is None:
            continue
        if k == "accum_out":
            writes.add(n)
        else:
            reads.add(n)
    return writes, reads


class Rec:

    chain_sems = {}

    def __init__(self):
        self.q = {e: [] for e in ENGS}
        self.lw = {e: {} for e in _CHAIN_ENGS}
        self.lr = {e: {} for e in _CHAIN_ENGS}
        self.fence = {e: 0 for e in _CHAIN_ENGS}

    def wait(self, eng, sem, val):
        if val > 0:
            self.q[eng].append(("w", sem.h, int(val)))

    def op(self, eng, method, *args, sem=None, inc=1, **kw):
        chain = None
        if eng in _CHAIN_ENGS and method not in _NO_CHAIN_METHODS:
            cs = Rec.chain_sems[eng]
            writes, reads = _rw(method, args, kw)
            need = 0
            for n in reads:
                need = max(need, self.lw[eng].get(n, 0))
            for n in writes:
                need = max(need, self.lw[eng].get(n, 0),
                           self.lr[eng].get(n, 0))
            if need > self.fence[eng]:
                self.q[eng].append(("w", cs.h, int(need)))
                self.fence[eng] = need
            cs.n += 1
            chain = cs
            for n in writes:
                self.lw[eng][n] = cs.n
            for n in reads:
                self.lr[eng][n] = max(self.lr[eng].get(n, 0), cs.n)
        if sem is not None and sem is chain:
            sem = None
        if sem is not None:
            sem.n += inc
        self.q[eng].append(("o", method, args, kw,
                            (sem.h if sem else None), inc,
                            (chain.h if chain else None)))
        return chain.n if chain is not None else (
            sem.n if sem is not None else None)

    def call(self, eng, fn):
        self.q[eng].append(("c", fn))

    def flush(self, nc):
        if not any(self.q[e] for e in ENGS):
            return
        with nc.Block() as block:
            for name in ENGS:
                items = self.q[name]
                if not items:
                    continue

                def body(eng, items=items):
                    for it in items:
                        if it[0] == "w":
                            eng.wait_ge(it[1], it[2])
                        elif it[0] == "c":
                            it[1](eng)
                        else:
                            _, method, args, kw, semh, inc, chainh = it
                            ins = getattr(eng, method)(*args, **kw)
                            if semh is not None:
                                ins.then_inc(semh, inc)
                            if chainh is not None:
                                ins.then_inc(chainh, 1)

                getattr(block, name)(body)


def bcol(t, col=0):
    return t[:, col:col + 1]


def bcast(t, n, col=0):
    return bass.AP(t, col, [[t.shape[1], 128], [0, n]])


def build_program(upto=None, dbg=(), b4_units=None, b4_pa_only=False, b4_groups=None, b4_noav=False, b4_nokq=False, b4_nob=False, b4_bstage=0, b4_nobm=False, b4_nobp=False):
    nc = bass.Bass("TRN2", target_bir_lowering=False)
    dbg = set(dbg)

    x_full = nc.dram_tensor("x_full", [S, D], F32, kind="ExternalInput")
    x_own = nc.dram_tensor("x_own", [TOWN, D], F32, kind="ExternalInput")
    x_own_tr = nc.dram_tensor("x_own_tr", [D, TOWN], F32, kind="ExternalInput")
    wq_t = nc.dram_tensor("wq_t", [D, NH * DK], F32, kind="ExternalInput")
    wq_bh = nc.dram_tensor("wq_bh", [128, NH], F32, kind="ExternalInput")
    rms_gc = nc.dram_tensor("rms_gc", [128, ND], F32, kind="ExternalInput")
    wo_t = nc.dram_tensor("wo_t", [D, D], F32, kind="ExternalInput")
    wo_bc = nc.dram_tensor("wo_bc", [128, ND], F32, kind="ExternalInput")
    w1_t = nc.dram_tensor("w1_t", [D, F], F32, kind="ExternalInput")
    w3_t = nc.dram_tensor("w3_t", [D, F], F32, kind="ExternalInput")
    w2_s = nc.dram_tensor("w2_s", [128, (128 // GP) * D], F32, kind="ExternalInput")
    h128_i = nc.dram_tensor("h128_i", [128, 128], BF16, kind="ExternalInput")
    bd_i = nc.dram_tensor("bd_i", [128, 128], BF16, kind="ExternalInput")
    identb_i = nc.dram_tensor("identb_i", [128, 128], BF16, kind="ExternalInput")
    identf_i = nc.dram_tensor("identf_i", [128, 128], F32, kind="ExternalInput")
    onesf_i = nc.dram_tensor("onesf_i", [1, 128], F32, kind="ExternalInput")
    maskq_i = nc.dram_tensor("maskq_i", [4 * 128, 512], F32, kind="ExternalInput")
    maskt_i = nc.dram_tensor("maskt_i", [4 * 128, 512], F32, kind="ExternalInput")
    coloff_i = nc.dram_tensor("coloff_i", [1, 1], I32, kind="ExternalInput")

    out_t = nc.dram_tensor("out_t", [D, TOWN], F32, kind="ExternalOutput")

    xnT_sp = nc.dram_tensor("xnT_sp", [128, ND * TOWN], BF16)
    qr_dram = nc.dram_tensor("qr_dram", [NH, 128, S], BF16)
    v_dram = nc.dram_tensor("v_dram", [NH, 128, S], BF16)
    s1_dram = nc.dram_tensor("s1_dram", [FH, 128, TOWN], BF16)
    w2f_dram = nc.dram_tensor("w2f_dram", [ND, FH, 128, 128], BF16)
    wob_dram = nc.dram_tensor("wob_dram", [D, D], BF16)
    o_cc_in = nc.dram_tensor("o_cc_in", [NH * 128, S], BF16)
    o_cc_out = nc.dram_tensor("o_cc_out", [2 * NH * 128, S], BF16)

    dbg_outs = {}

    def dbg_out(name, shape, dtype=F32):
        dbg_outs[name] = nc.dram_tensor(name, shape, dtype, kind="ExternalOutput")
        return dbg_outs[name]

    with ExitStack() as ctx:
        e = ctx.enter_context

        def sem(name):
            return Sem(e(nc.semaphore(name)), name)

        sL = sem("sL")
        sSp = sem("sSp")
        sT = sem("sT")
        sT2 = sem("sT2")
        sCC = sem("sCC")
        sOut = sem("sOut")
        Rec.chain_sems = {"vector": sem("cV"), "scalar": sem("cA"),
                          "gpsimd": sem("cG")}
        sV = sV2 = Rec.chain_sems["vector"]
        sA = sA2 = Rec.chain_sems["scalar"]
        sG = Rec.chain_sems["gpsimd"]
        sLx = [sem("sLx0"), sem("sLx1")]
        sLu = [sem("sLu0"), sem("sLu1")]
        sLxn = sem("sLxn")
        sLb6 = sem("sLb6")
        sLs1c = [sem(f"sLs1c{i}") for i in range(4)]
        sLog = sem("sLog")
        sLwo6 = [sem(f"sLwo6{i}") for i in range(2)]
        sLw26 = [sem(f"sLw26{i}") for i in range(2)]
        sLwf = sem("sLwf")
        sLw2m = sem("sLw2m")
        sLwom = sem("sLwom")
        sSpO = [sem("sSpO0"), sem("sSpO1")]
        sSpS1 = sem("sSpS1")
        sSpW2 = sem("sSpW2")
        sSpWo = sem("sSpWo")

        h128b = e(nc.sbuf_tensor("h128b", [128, 128], BF16))
        bdb = e(nc.sbuf_tensor("bdb", [128, 128], BF16))
        identb = e(nc.sbuf_tensor("identb", [128, 128], BF16))
        identf = e(nc.sbuf_tensor("identf", [128, 128], F32))
        onesf_t = e(nc.sbuf_tensor("onesf_t", [128, 128], F32))
        onesf = onesf_t[0:1, 0:128]
        rmsg = e(nc.sbuf_tensor("rmsg", [128, ND], F32))
        wqbias = e(nc.sbuf_tensor("wqbias", [128, NH], F32))
        wobias = e(nc.sbuf_tensor("wobias", [128, ND], F32))
        coloff_t = e(nc.sbuf_tensor("coloff_t", [128, 1], I32))
        coloff = coloff_t[0:1, 0:1]
        cIsq128 = e(nc.sbuf_tensor("cIsq128", [128, 1], F32))
        cIsq64 = e(nc.sbuf_tensor("cIsq64", [128, 1], F32))

        r = Rec()
        for dst, src in ((h128b[:], h128_i), (bdb[:], bd_i),
                         (identb[:], identb_i), (identf[:], identf_i),
                         (onesf, onesf_i), (rmsg[:], rms_gc),
                         (wqbias[:], wq_bh), (wobias[:], wo_bc),
                         (coloff, coloff_i)):
            r.op("sync", "dma_start", dst, src[:], sem=sL, inc=16)
        r.op("vector", "memset", cIsq128[:], ISQ128)
        r.op("vector", "memset", cIsq64[:], ISQ64)
        r.wait("sync", sL, sL.n)
        r.flush(nc)

        with ExitStack() as cEarly:
            eE = cEarly.enter_context
            cosT = eE(nc.sbuf_tensor("cosT", [128, S], F32))
            sinT = eE(nc.sbuf_tensor("sinT", [128, S], F32))

            with ExitStack() as c1:
                e1 = c1.enter_context
                jcol = e1(nc.sbuf_tensor("jcol", [128, 1], F32))
                ifcol = e1(nc.sbuf_tensor("ifcol", [128, 1], F32))
                angw = e1(nc.sbuf_tensor("angw", [128, S], F32))
                uw = e1(nc.sbuf_tensor("uw", [128, S], F32))
                kacc = e1(nc.sbuf_tensor("kacc", [128, S], F32))
                cmpw = e1(nc.sbuf_tensor("cmpw", [128, S], F32))
                cNegLn = e1(nc.sbuf_tensor("cNegLn", [128, 1], F32))

                r = Rec()
                r.op("gpsimd", "iota", jcol[:, 0:1], [[1, 1]],
                     channel_multiplier=1, allow_small_or_imprecise_dtypes=True)
                r.op("gpsimd", "iota", angw[:, :], [[1, S]],
                     channel_multiplier=0, allow_small_or_imprecise_dtypes=True,
                     sem=sG)
                r.op("vector", "memset", cNegLn[:], -math.log(10000.0) / 64.0)
                r.wait("vector", sG, sG.n)
                w = r.op("vector", "tensor_single_scalar", cmpw[:, 0:1],
                         jcol[:], float(FH), ALU.is_ge, sem=sV)
                r.wait("vector", sV, w)
                r.op("vector", "scalar_tensor_tensor", jcol[:], cmpw[:, 0:1],
                     -float(FH), jcol[:], ALU.mult, ALU.add, sem=sV)
                r.wait("scalar", sV, sV.n)
                r.op("scalar", "activation", ifcol[:], jcol[:], AF.Exp,
                     scale=bcol(cNegLn), sem=sA)
                r.wait("vector", sA, sA.n)
                r.op("vector", "tensor_tensor", angw[:], angw[:],
                     bcast(ifcol, S), ALU.mult)
                r.op("vector", "tensor_single_scalar", angw[0:64, :],
                     angw[0:64, :], math.pi / 2.0, ALU.add)
                r.op("vector", "tensor_single_scalar", uw[:], angw[:],
                     1.0 / TWO_PI, ALU.mult)
                r.op("vector", "tensor_single_scalar", uw[:], uw[:], 0.5,
                     ALU.add)
                r.op("vector", "memset", kacc[:], 0.0)
                for b in (512, 256, 128, 64, 32, 16, 8, 4, 2, 1):
                    r.op("vector", "tensor_single_scalar", cmpw[:], uw[:],
                         float(b), ALU.is_ge)
                    r.op("vector", "scalar_tensor_tensor", uw[:], cmpw[:],
                         -float(b), uw[:], ALU.mult, ALU.add)
                    r.op("vector", "scalar_tensor_tensor", kacc[:], cmpw[:],
                         float(b), kacc[:], ALU.mult, ALU.add)
                r.op("vector", "scalar_tensor_tensor", angw[:], kacc[:],
                     -TWO_PI, angw[:], ALU.mult, ALU.add)
                r.op("vector", "tensor_single_scalar", angw[:], angw[:],
                     3.141592, ALU.min)
                r.op("vector", "tensor_single_scalar", angw[:], angw[:],
                     -3.141592, ALU.max, sem=sV)
                r.wait("scalar", sV, sV.n)
                r.op("scalar", "activation", uw[:], angw[:], AF.Sin, sem=sA)
                r.wait("vector", sA, sA.n)
                sc_ = r.op("vector", "tensor_single_scalar", uw[:], uw[:],
                           SQ4, ALU.mult, sem=sV)
                r.wait("vector", sV, sc_)
                r.op("vector", "tensor_copy", cosT[0:64, :], uw[0:64, :])
                r.op("vector", "tensor_copy", cosT[64:128, :], uw[0:64, :])
                r.op("vector", "tensor_copy", sinT[0:64, :], uw[64:128, :])
                r.op("vector", "tensor_copy", sinT[64:128, :], uw[64:128, :],
                     sem=sV)
                r.flush(nc)

            if "tables" in dbg:
                o = dbg_out("dbg_tables", [128, S])
                r = Rec()
                r.op("sync", "dma_start", o[:], cosT[:], sem=sOut, inc=16)
                r.wait("sync", sOut, sOut.n)
                r.flush(nc)
            if upto == "B1":
                return nc, dbg_outs

            xnrsT = [eE(nc.sbuf_tensor(f"xnrsT{k}", [128, S], BF16))
                     for k in range(ND)]

            with ExitStack() as c2:
                e2 = c2.enter_context
                xstg = [e2(nc.sbuf_tensor(f"xstg{i}", [128, D], F32))
                        for i in range(2)]
                sqdump = e2(nc.sbuf_tensor("sqdump", [128, D], BF16))
                xnb = e2(nc.sbuf_tensor("xnb", [128, D], BF16))
                xnstrip = e2(nc.sbuf_tensor("xnstrip", [128, D], BF16))
                s1c = e2(nc.sbuf_tensor("s1c", [128, 1], F32))
                s2c = e2(nc.sbuf_tensor("s2c", [128, 1], F32))
                meanc = e2(nc.sbuf_tensor("meanc", [128, 1], F32))
                varc = e2(nc.sbuf_tensor("varc", [128, 1], F32))
                r1c = e2(nc.sbuf_tensor("r1c", [128, 1], F32))
                uc = e2(nc.sbuf_tensor("uc", [128, 1], F32))
                rstdc = e2(nc.sbuf_tensor("rstdc", [128, 1], F32))
                gc = e2(nc.sbuf_tensor("gc", [128, 1], F32))
                cc = e2(nc.sbuf_tensor("cc", [128, 1], F32))
                bcl = e2(nc.sbuf_tensor("bcl", [128, 1], F32))
                trp = [e2(nc.psum_tensor(f"trp{i}", [128, 128], BF16))
                       for i in range(2)]

                r = Rec()
                ntot = NT + NTO
                norm_done = []
                ev_done = []
                trp_guard = [0, 0]
                spill_guard = 0
                for i in range(ntot):
                    own = i >= NT
                    ti = i - NT if own else i
                    src = x_own if own else x_full
                    slot = xstg[i % 2]
                    if i >= 2:
                        r.wait("sync", sA, norm_done[i - 2])
                    lsem = sLx[i % 2]
                    r.op("sync", "dma_start", slot[:],
                         src[ti * 128:(ti + 1) * 128, :], sem=lsem, inc=16)
                    ld = lsem.n
                    r.wait("vector", lsem, ld)
                    vv = r.op("vector", "reduce_sum", s1c[:], slot[:], AX.X,
                              sem=sV)
                    r.wait("scalar", lsem, ld)
                    if i >= 1:
                        r.wait("scalar", sV2, ev_done[i - 1])
                    sq = r.op("scalar", "activation", sqdump[:], slot[:],
                              AF.Square, accum_out=s2c[:, 0:1], sem=sA)
                    r.wait("vector", sA, sq)

                    def vchain(method, *args):
                        w = r.op("vector", method, *args, sem=sV)
                        r.wait("vector", sV, w)
                        return w

                    r.wait("vector", sV, vv)
                    vchain("tensor_single_scalar", meanc[:], s1c[:],
                           1.0 / D, ALU.mult)
                    vchain("tensor_single_scalar", varc[:], s2c[:],
                           1.0 / D, ALU.mult)
                    vchain("tensor_tensor", uc[:], meanc[:], meanc[:],
                           ALU.mult)
                    vchain("tensor_tensor", varc[:], varc[:], uc[:],
                           ALU.subtract)
                    vchain("tensor_single_scalar", r1c[:], varc[:],
                           EPS_LN, ALU.add)
                    vchain("reciprocal", r1c[:], r1c[:])
                    if own:
                        cols = vchain("tensor_copy", uc[:], r1c[:])
                    else:
                        vchain("tensor_tensor", uc[:], varc[:], r1c[:],
                               ALU.mult)
                        vchain("tensor_single_scalar", uc[:], uc[:],
                               EPS_F32, ALU.add)
                        cols = vchain("reciprocal", uc[:], uc[:])
                    r.wait("scalar", sV, cols)
                    r.op("scalar", "activation", rstdc[:], r1c[:], AF.Sqrt)
                    sqr = r.op("scalar", "activation", gc[:], uc[:], AF.Sqrt,
                               sem=sA)
                    r.wait("vector", sA, sqr)
                    if own:
                        ccv = r.op("vector", "tensor_copy", cc[:], rstdc[:],
                                   sem=sV)
                    else:
                        ccv = r.op("vector", "tensor_tensor", cc[:], rstdc[:],
                                   gc[:], ALU.mult, sem=sV)
                    r.wait("vector", sV, ccv)
                    cb = r.op("vector", "scalar_tensor_tensor", bcl[:],
                              meanc[:], -1.0, cc[:], ALU.mult, ALU.mult,
                              sem=sV)
                    r.wait("scalar", sV, cb)
                    nd_ = r.op("scalar", "activation", xnb[:], slot[:],
                               AF.Identity, bias=bcol(bcl), scale=bcol(cc),
                               sem=sA)
                    norm_done.append(nd_)
                    for k in range(ND):
                        ps = k % 2
                        r.wait("tensor", sA, nd_)
                        r.wait("tensor", sV2, trp_guard[ps])
                        tr = r.op("tensor", "transpose", trp[ps][:],
                                  xnb[:, k * 128:(k + 1) * 128], identb[:],
                                  sem=sT)
                        r.wait("vector", sT, tr)
                        if own and k == 0:
                            r.wait("vector", sSp, spill_guard)
                        if own:
                            evv = r.op("vector", "tensor_copy",
                                       xnstrip[:, k * 128:(k + 1) * 128],
                                       trp[ps][:], sem=sV2)
                        else:
                            evv = r.op("vector", "tensor_copy",
                                       xnrsT[k][:, ti * 128:(ti + 1) * 128],
                                       trp[ps][:], sem=sV2)
                        trp_guard[ps] = evv
                    ev_done.append(sV2.n)
                    if own:
                        r.wait("sync", sV2, ev_done[i])
                        r.op("sync", "dma_start",
                             bass.AP(xnT_sp, ti * 128,
                                     [[ND * TOWN, 128], [TOWN, ND], [1, 128]]),
                             bass.AP(xnstrip, 0,
                                     [[D, 128], [128, ND], [1, 128]]),
                             sem=sSp, inc=16)
                        spill_guard = sSp.n
                r.wait("sync", sSp, sSp.n)
                r.wait("sync", sLx[0], sLx[0].n)
                r.wait("sync", sLx[1], sLx[1].n)
                r.flush(nc)
            b2_spills = sSp.n

            if "xnrst" in dbg:
                o = dbg_out("dbg_xnrst", [ND * 128, S], BF16)
                o2 = dbg_out("dbg_xnt", [128, ND * TOWN], BF16)
                r = Rec()
                for k in range(ND):
                    r.op("sync", "dma_start", o[k * 128:(k + 1) * 128, :],
                         xnrsT[k][:], sem=sOut, inc=16)
                r.op("sync", "dma_start", o2[:], xnT_sp[:], sem=sOut,
                     inc=16)
                r.wait("sync", sOut, sOut.n)
                r.flush(nc)
            if upto == "B2":
                return nc, dbg_outs

            with ExitStack() as c3:
                e3 = c3.enter_context
                wqstg = e3(nc.sbuf_tensor("wqstg", [128, D // 2], F32))
                wqb = e3(nc.sbuf_tensor("wqb", [128, D], BF16))
                qT = e3(nc.sbuf_tensor("qT", [128, S], F32))
                t1w = e3(nc.sbuf_tensor("t1w", [128, 512], F32))
                t2w = e3(nc.sbuf_tensor("t2w", [128, 512], F32))
                qrb = e3(nc.sbuf_tensor("qrb", [128, S], BF16))
                vbst = e3(nc.sbuf_tensor("vbst", [128, S], BF16))
                qps = [e3(nc.psum_tensor(f"qps{i}", [128, 512], F32))
                       for i in range(2)]

                r = Rec()
                evac_hist = []
                vcopy_hist = []
                spill_hist = []
                for m in range(NH):
                    for hf in range(2):
                        if m >= 1 or hf == 1:
                            r.wait("sync", sV, sV.n)
                        r.op("sync", "dma_start", wqstg[:],
                             bass.AP(wq_t, m * 128 + hf * (ND // 2) * 128 * NH * DK,
                                     [[NH * DK, 128], [128 * NH * DK, ND // 2],
                                      [1, 128]]),
                             sem=sL, inc=16)
                        wq_ld = sL.n
                        r.wait("vector", sL, wq_ld)
                        if hf == 0 and m >= 1:
                            r.wait("vector", sT, sT.n)
                            r.wait("vector", sSp, spill_hist[m - 1])
                        for k2 in range(ND // 2):
                            k = hf * (ND // 2) + k2
                            r.op("vector", "tensor_tensor",
                                 wqb[:, k * 128:(k + 1) * 128],
                                 wqstg[:, k2 * 128:(k2 + 1) * 128],
                                 bcast(rmsg, 128, k), ALU.mult,
                                 sem=(sV if k2 == ND // 2 - 1 else None))
                    wq_cv = sV.n
                    for tp in range(8):
                        slot = qps[tp % 2]
                        r.wait("tensor", sV, wq_cv)
                        if tp >= 2:
                            r.wait("tensor", sA, evac_hist[-2])
                        for k in range(ND):
                            r.op("tensor", "matmul", slot[:],
                                 wqb[:, k * 128:(k + 1) * 128],
                                 xnrsT[k][:, tp * 512:(tp + 1) * 512],
                                 start=(k == 0), stop=(k == ND - 1),
                                 sem=(sT if k == ND - 1 else None))
                        r.wait("scalar", sT, sT.n)
                        if tp == 0 and m >= 1:
                            r.wait("scalar", sV2, vcopy_hist[m - 1])
                        ev = r.op("scalar", "activation",
                                  qT[:, tp * 512:(tp + 1) * 512], slot[:],
                                  AF.Identity, bias=wqbias[:, m:m + 1], sem=sA)
                        evac_hist.append(ev)
                    q_ev = evac_hist[-1]
                    r.wait("vector", sA, q_ev)
                    if m >= 1:
                        r.wait("vector", sSp, spill_hist[m - 1])
                    for ch in range(8):
                        a, b = ch * 512, (ch + 1) * 512
                        r.op("vector", "tensor_tensor", t1w[:],
                             qT[:, a:b], cosT[:, a:b], ALU.mult)
                        r.op("vector", "tensor_tensor", t2w[0:64, :],
                             qT[64:128, a:b], sinT[64:128, a:b], ALU.mult)
                        r.op("vector", "tensor_tensor", t2w[64:128, :],
                             qT[0:64, a:b], sinT[0:64, a:b], ALU.mult)
                        r.op("vector", "tensor_tensor", qrb[0:64, a:b],
                             t1w[0:64, :], t2w[0:64, :], ALU.subtract)
                        r.op("vector", "tensor_tensor", qrb[64:128, a:b],
                             t1w[64:128, :], t2w[64:128, :], ALU.add)
                    vc = r.op("vector", "tensor_copy", vbst[:], qT[:], sem=sV2)
                    vcopy_hist.append(vc)
                    r.wait("sync", sV2, vc)
                    r.op("sync", "dma_start", qr_dram[m], qrb[:], sem=sSp,
                         inc=16)
                    r.op("sync", "dma_start", v_dram[m], vbst[:], sem=sSp,
                         inc=16)
                    spill_hist.append(sSp.n)
                r.wait("sync", sSp, sSp.n)
                r.flush(nc)
            b3_spills = sSp.n

            if "qproj" in dbg:
                oq = dbg_out("dbg_qr", [NH * 128, S], BF16)
                ov = dbg_out("dbg_v", [NH * 128, S], BF16)
                r = Rec()
                for m in range(NH):
                    r.op("sync", "dma_start", oq[m * 128:(m + 1) * 128, :],
                         qr_dram[m], sem=sOut, inc=16)
                    r.op("sync", "dma_start", ov[m * 128:(m + 1) * 128, :],
                         v_dram[m], sem=sOut, inc=16)
                r.wait("sync", sOut, sOut.n)
                r.flush(nc)
            if upto == "B3":
                return nc, dbg_outs

        with ExitStack() as c4:
            e4 = c4.enter_context
            maskq = e4(nc.sbuf_tensor("maskq", [128, 4 * 512], F32))
            maskt = e4(nc.sbuf_tensor("maskt", [128, 4 * 512], F32))
            qrs = e4(nc.sbuf_tensor("qrs", [128, S], BF16))
            vfs = e4(nc.sbuf_tensor("vfs", [128, S], BF16))
            vsb = e4(nc.sbuf_tensor("vsb", [128, S], BF16))
            edump = vfs
            bnd = [e4(nc.sbuf_tensor(f"bnd{i}", [128, 512], F32))
                   for i in range(2)]
            nmstrip = [e4(nc.sbuf_tensor(f"nmstrip{i}", [128, 8], F32))
                       for i in range(2)]
            sstrip = [e4(nc.sbuf_tensor(f"sstrip{i}", [128, 8], F32))
                      for i in range(2)]
            dstrip = [e4(nc.sbuf_tensor(f"dstrip{i}", [128, 8], F32))
                      for i in range(2)]
            estrip = [e4(nc.sbuf_tensor(f"estrip{i}", [128, 8], F32))
                      for i in range(2)]
            negm = [e4(nc.sbuf_tensor(f"negm{i}", [128, 1], F32))
                    for i in range(2)]
            ssum = [e4(nc.sbuf_tensor(f"ssum{i}", [128, 1], F32))
                    for i in range(2)]
            lnsc = [e4(nc.sbuf_tensor(f"lnsc{i}", [128, 1], F32))
                    for i in range(2)]
            nmadj = [e4(nc.sbuf_tensor(f"nmadj{i}", [128, 1], F32))
                     for i in range(2)]
            nmcols = e4(nc.sbuf_tensor("nmcols", [128, NQT], F32))
            nmev = e4(nc.sbuf_tensor("nmev", [128, 512], F32))
            bplain = e4(nc.sbuf_tensor("bplain", [128, 512], F32))
            bplain2 = e4(nc.sbuf_tensor("bplain2", [128, 512], F32))
            bmask = [e4(nc.sbuf_tensor(f"bmask{i}", [128, 512], F32))
                     for i in range(4)]
            bmask2 = [e4(nc.sbuf_tensor(f"bmask2_{i}", [128, 512], F32))
                      for i in range(4)]
            sadj = [e4(nc.sbuf_tensor(f"sadj{i}", [128, 512], F32))
                    for i in range(3)]
            pTb = [e4(nc.sbuf_tensor(f"pTb{i}", [128, 512], BF16))
                   for i in range(3)]
            oTg = [e4(nc.sbuf_tensor(f"oTg{i}", [128, 512], BF16))
                   for i in range(2)]
            xnto = e4(nc.sbuf_tensor("xnto", [128, ND * TOWN], BF16))
            wstg = e4(nc.sbuf_tensor("wstg", [128, D], F32))
            w1b = [e4(nc.sbuf_tensor(f"w1b{i}", [128, D], BF16))
                   for i in range(2)]
            w3b = [e4(nc.sbuf_tensor(f"w3b{i}", [128, D], BF16))
                   for i in range(2)]
            stmp = e4(nc.sbuf_tensor("stmp", [128, 512], F32))
            hbd = [e4(nc.sbuf_tensor(f"hbd{i}", [128, 512], BF16))
                   for i in range(2)]
            s1strip = e4(nc.sbuf_tensor("s1strip", [128, TOWN], BF16))
            w2stg = e4(nc.sbuf_tensor("w2stg", [128, D], F32))
            w2sb = e4(nc.sbuf_tensor("w2sb", [128, D], BF16))
            w2fout = e4(nc.sbuf_tensor("w2fout", [128, D], BF16))
            qkps = [e4(nc.psum_tensor(f"qkps{i}", [128, 512], F32))
                    for i in range(2)]
            stps = [e4(nc.psum_tensor(f"stps{i}", [128, 512], F32))
                    for i in range(2)]
            avps = e4(nc.psum_tensor("avps", [128, 512], F32))
            fps = [e4(nc.psum_tensor(f"fps{i}", [128, 512], F32))
                   for i in range(3)]
            bps = qkps[0]
            nmps = qkps[0][0:1, 0:128]
            vtp = [qkps[i][:, 0:64].bitcast(BF16) for i in range(2)]

            bankg = {}
            r = Rec()

            def gwait(eng, name):
                g_ = bankg.get(name)
                if g_ is not None:
                    r.wait(eng, g_[0], g_[1])

            def gset(name, semv):
                bankg[name] = semv

            r.op("sync", "dma_start",
                 bass.AP(maskq, 0, [[4 * 512, 128], [512, 4], [1, 512]]),
                 bass.AP(maskq_i, 0, [[512, 128], [128 * 512, 4], [1, 512]]),
                 sem=sL, inc=16)
            r.op("sync", "dma_start",
                 bass.AP(maskt, 0, [[4 * 512, 128], [512, 4], [1, 512]]),
                 bass.AP(maskt_i, 0, [[512, 128], [128 * 512, 4], [1, 512]]),
                 sem=sL, inc=16)
            mask_ld = sL.n
            r.wait("sync", sSp, b2_spills)
            r.op("sync", "dma_start", xnto[:], xnT_sp[:], sem=sLxn, inc=16)
            xn_ld = sLxn.n

            unit_done = []
            exp1_hist = []
            prev_unit_av = 0
            oTg_free = [0, 0]
            silu_last = 0
            mult_last = 0
            h128_last = 0
            wcv = [0, 0]
            pend_h128 = None
            wstg_free = 0
            w2cv = 0
            wo_cv = 0
            s1_sp = 0

            def q_wload(hi):
                nonlocal wstg_free, h128_last
                ws = hi % 2
                r.wait("sync", sV, wstg_free)
                r.op("sync", "dma_start", wstg[:],
                     bass.AP(w1_t, hi * 128, [[F, 128], [128 * F, ND],
                                              [1, 128]]), sem=sLwf, inc=16)
                r.wait("vector", sLwf, sLwf.n)
                if h128_last:
                    r.wait("vector", sT, h128_last)
                cv1 = r.op("vector", "tensor_copy", w1b[ws][:], wstg[:],
                           sem=sV)
                r.wait("sync", sV, cv1)
                r.op("sync", "dma_start", wstg[:],
                     bass.AP(w3_t, hi * 128, [[F, 128], [128 * F, ND],
                                              [1, 128]]), sem=sLwf, inc=16)
                r.wait("vector", sLwf, sLwf.n)
                cv2 = r.op("vector", "tensor_copy", w3b[ws][:], wstg[:],
                           sem=sV)
                wcv[ws] = cv2
                wstg_free = cv2

            def q_ffn(hi, tp):
                nonlocal silu_last, mult_last, h128_last, s1_sp, pend_h128
                ws = hi % 2
                if hi == 0 and tp == 0:
                    r.wait("tensor", sLxn, xn_ld)
                r.wait("tensor", sV, wcv[ws])
                if silu_last:
                    r.wait("tensor", sA, silu_last)
                for k in range(ND):
                    r.op("tensor", "matmul", fps[0][:],
                         w1b[ws][:, k * 128:(k + 1) * 128],
                         xnto[:, k * TOWN + tp * 512:
                               k * TOWN + (tp + 1) * 512],
                         start=(k == 0), stop=(k == ND - 1),
                         sem=(sT if k == ND - 1 else None))
                up_done = sT.n
                if mult_last:
                    r.wait("tensor", sV2, mult_last)
                for k in range(ND):
                    r.op("tensor", "matmul", fps[1][:],
                         w3b[ws][:, k * 128:(k + 1) * 128],
                         xnto[:, k * TOWN + tp * 512:
                               k * TOWN + (tp + 1) * 512],
                         start=(k == 0), stop=(k == ND - 1),
                         sem=(sT if k == ND - 1 else None))
                lin_done = sT.n
                do_h128()
                r.wait("scalar", sT, up_done)
                if mult_last:
                    r.wait("scalar", sV2, mult_last)
                sl_ = r.op("scalar", "activation", stmp[:], fps[0][:],
                           AF.Silu, sem=sA)
                silu_last = sl_
                r.wait("vector", sA, sl_)
                r.wait("vector", sT, lin_done)
                if h128_last:
                    r.wait("vector", sT, h128_last)
                ml = r.op("vector", "tensor_tensor",
                          hbd[(hi * 4 + tp) % 2][:], stmp[:], fps[1][:],
                          ALU.mult, sem=sV2)
                mult_last = ml
                pend_h128 = (hi, tp, ml, (hi * 4 + tp) % 2)

            def do_h128():
                nonlocal h128_last, s1_sp, pend_h128
                if pend_h128 is None:
                    return
                hi, tp, ml, hs = pend_h128
                pend_h128 = None
                cols = slice(tp * 512, (tp + 1) * 512)
                r.wait("tensor", sV2, ml)
                gwait("tensor", "f2")
                hm = r.op("tensor", "matmul", fps[2][:], h128b[:],
                          hbd[hs][:], start=True, stop=True, sem=sT)
                h128_last = hm
                r.wait("scalar", sT, hm)
                if tp == 0:
                    r.wait("scalar", sSpS1, s1_sp)
                s1e = r.op("scalar", "activation", s1strip[:, cols],
                           fps[2][:], AF.Copy, scale=bcol(cIsq128), sem=sA)
                gset("f2", (sA, s1e))
                if tp == 3:
                    r.wait("sync", sA, s1e)
                    r.op("sync", "dma_start", s1_dram[hi], s1strip[:],
                         sem=sSpS1, inc=16)
                    s1_sp = sSpS1.n

            def q_w2f(lam):
                nonlocal w2cv, wo_cv
                do_h128()
                r.wait("sync", sV, w2cv)
                r.wait("sync", sG, wo_cv)
                r.op("sync", "dma_start", w2stg[:],
                     bass.AP(w2_s, lam * D, [[(128 // GP) * D, 128], [1, D]]),
                     sem=sLw2m, inc=16)
                r.wait("vector", sLw2m, sLw2m.n)
                r.wait("vector", sT, sT.n)
                w2cv = r.op("vector", "tensor_copy", w2sb[:], w2stg[:],
                            sem=sV)
                w2ev = 0
                for q4 in range(4):
                    r.wait("tensor", sV, w2cv)
                    gwait("tensor", "f2")
                    r.op("tensor", "matmul", fps[2][:], bdb[:],
                         w2sb[:, q4 * 512:(q4 + 1) * 512], start=True,
                         stop=True, sem=sT)
                    r.wait("scalar", sT, sT.n)
                    if q4 == 0:
                        r.wait("scalar", sSpW2, sSpW2.n)
                        r.wait("scalar", sSpWo, sSpWo.n)
                    w2ev = r.op("scalar", "activation",
                                w2fout[:, q4 * 512:(q4 + 1) * 512],
                                fps[2][:], AF.Copy, scale=bcol(cIsq64),
                                sem=sA)
                    gset("f2", (sA, w2ev))
                r.wait("sync", sA, w2ev)
                for gi in range(GP):
                    ap_sb2 = bass.AP(w2fout, gi * FH * D,
                                     [[D, FH], [128, ND], [1, 128]])
                    ap_dr2 = bass.AP(w2f_dram, lam * GP * 128 + gi * 128,
                                     [[128 * 128, FH], [FH * 128 * 128, ND],
                                      [1, 128]])
                    r.op("sync", "dma_start", ap_dr2, ap_sb2, sem=sSpW2,
                         inc=16)

            def q_wo(k):
                nonlocal wo_cv
                r.wait("sync", sV, w2cv)
                r.wait("sync", sG, wo_cv)
                r.op("sync", "dma_start", w2stg[:],
                     wo_t[k * 128:(k + 1) * 128, :], sem=sLwom, inc=16)
                r.wait("gpsimd", sLwom, sLwom.n)
                r.wait("gpsimd", sSpW2, sSpW2.n)
                r.wait("gpsimd", sSpWo, sSpWo.n)
                r.wait("gpsimd", sA, sA.n)
                wo_cv = r.op("gpsimd", "tensor_copy", w2fout[:], w2stg[:],
                             sem=sG)
                r.wait("sync", sG, wo_cv)
                r.op("sync", "dma_start", wob_dram[k * 128:(k + 1) * 128, :],
                     w2fout[:], sem=sSpWo, inc=16)

            def emit_unit(h):
                nonlocal prev_unit_av
                qsl = qrs
                quanta = []
                for hi in range(8 * h, 8 * h + 8):
                    quanta.append(("wl", hi))
                    for tp in range(4):
                        quanta.append(("tp", hi, tp))
                qi = [0]

                def pump(n=1):
                    for _ in range(n):
                        if qi[0] >= len(quanta):
                            return
                        q = quanta[qi[0]]
                        qi[0] += 1
                        if q[0] == "wl":
                            q_wload(q[1])
                        else:
                            q_ffn(q[1], q[2])

                if h == 0:
                    r.wait("sync", sSp, b3_spills)
                if h >= 2:
                    r.wait("sync", sA2, unit_done[h - 2])
                usem = sLu[h % 2]
                if h >= 1:
                    r.wait("sync", sT2, prev_unit_av)
                    r.wait("sync", sA, exp1_hist[-1])
                r.op("sync", "dma_start", qsl[:], qr_dram[h], sem=usem,
                     inc=16)
                r.op("sync", "dma_start", vfs[:], v_dram[h], sem=usem,
                     inc=16)
                u_ld = usem.n
                for j in range(NT):
                    ps = j % 2
                    r.wait("tensor", usem, u_ld)
                    gwait("tensor", f"q{ps}")
                    tr = r.op("tensor", "transpose", vtp[ps],
                              vfs[:, j * 128:(j + 1) * 128], identb[:],
                              sem=sT)
                    r.wait("vector", sT, tr)
                    if j == 0:
                        r.wait("vector", sT2, prev_unit_av)
                    evv = r.op("vector", "tensor_copy",
                               vsb[:, j * 128:(j + 1) * 128], vtp[ps],
                               sem=sV2)
                    gset(f"q{ps}", (sV2, evv))
                pend_tail = None
                nmcol_done = [0] * NQT

                def do_tail(G, par, t, exp_done):
                    r.wait("vector", sA, exp_done)
                    r.op("vector", "tensor_reduce", negm[par][:],
                         nmstrip[par][:, 0:G], AX.X, ALU.min, sem=sV)
                    r.op("vector", "tensor_tensor", dstrip[par][:, 0:G],
                         bcast(negm[par], G), nmstrip[par][:, 0:G],
                         ALU.subtract, sem=sV)
                    r.wait("scalar", sV, sV.n)
                    ee = r.op("scalar", "activation", estrip[par][:, 0:G],
                              dstrip[par][:, 0:G], AF.Exp, sem=sA)
                    r.wait("vector", sA, ee)
                    r.op("vector", "tensor_tensor", estrip[par][:, 0:G],
                         estrip[par][:, 0:G], sstrip[par][:, 0:G], ALU.mult,
                         sem=sV)
                    sm = r.op("vector", "tensor_reduce", ssum[par][:],
                              estrip[par][:, 0:G], AX.X, ALU.add, sem=sV)
                    r.wait("scalar", sV, sm)
                    lnv = r.op("scalar", "activation", lnsc[par][:],
                               ssum[par][:], AF.Ln, sem=sA)
                    exp1_hist.append(lnv)
                    r.wait("vector", sA, lnv)
                    nmc = r.op("vector", "tensor_tensor", nmcols[:, t:t + 1],
                               negm[par][:], lnsc[par][:], ALU.subtract,
                               sem=sV)
                    nmcol_done[t] = nmc
                    pump(1)

                for t in range(NQT):
                    G = t // 4 + 1
                    rr = t % 4
                    par = t % 2
                    for j in range(G):
                        ps = j % 2
                        r.wait("tensor", usem, u_ld)
                        gwait("tensor", f"q{ps}")
                        mm = r.op("tensor", "matmul", qkps[ps][:],
                                  qsl[:, t * 128:(t + 1) * 128],
                                  qsl[:, j * 512:(j + 1) * 512], start=True,
                                  stop=True, sem=sT)
                        if j == G - 1:
                            r.wait("vector", sT, mm)
                            r.wait("vector", sL, mask_ld)
                            r.wait("vector", sA, sA.n)
                            bv = r.op("vector", "tensor_tensor", bnd[par][:],
                                      qkps[ps][:],
                                      maskq[:, rr * 512:(rr + 1) * 512],
                                      ALU.add, sem=sV)
                            gset(f"q{ps}", (sV, bv))
                            r.op("vector", "tensor_reduce",
                                 nmstrip[par][:, j:j + 1], bnd[par][:],
                                 AX.X, ALU.max, negate=True, sem=sV)
                            r.wait("scalar", sV, sV.n)
                            ex = r.op("scalar", "activation",
                                      edump[:, (t % 4) * 1024:
                                            (t % 4) * 1024 + 512],
                                      bnd[par][:], AF.Exp,
                                      bias=nmstrip[par][:, j:j + 1],
                                      accum_out=sstrip[par][:, j:j + 1],
                                      sem=sA)
                        else:
                            r.wait("vector", sT, mm)
                            rd = r.op("vector", "tensor_reduce",
                                      nmstrip[par][:, j:j + 1], qkps[ps][:],
                                      AX.X, ALU.max, negate=True, sem=sV)
                            r.wait("scalar", sV, rd)
                            ex = r.op("scalar", "activation",
                                      edump[:, (t % 4) * 1024 +
                                            (j % 2) * 512:
                                            (t % 4) * 1024 +
                                            (j % 2) * 512 + 512],
                                      qkps[ps][:], AF.Exp,
                                      bias=nmstrip[par][:, j:j + 1],
                                      accum_out=sstrip[par][:, j:j + 1],
                                      sem=sA)
                            gset(f"q{ps}", (sA, ex))
                    exp_done = sA.n
                    if pend_tail is not None:
                        do_tail(*pend_tail)
                    pend_tail = (G, t % 2, t, exp_done)
                if pend_tail is not None:
                    do_tail(*pend_tail)
                    pend_tail = None
                bpl = [bplain, bplain2]
                bmk = [bmask, bmask2]
                bev_as = [0, 0]
                bev_vs = [0, 0]

                def build_b(g):
                    gwait("tensor", "q0")
                    for rr2 in range(4):
                        j = 4 * g + rr2
                        r.wait("tensor", sV, nmcol_done[j])
                        if rr2:
                            r.wait("tensor", sA, sA.n)
                        tr = r.op("tensor", "transpose", nmps,
                                  nmcols[:, j:j + 1], identf[:], sem=sT)
                        r.wait("scalar", sT, tr)
                        r.op("scalar", "copy",
                             nmev[0:1, rr2 * 128:(rr2 + 1) * 128], nmps,
                             sem=sA)
                    r.wait("tensor", sA, sA.n)
                    for rr2 in range(4):
                        r.op("tensor", "matmul",
                             bps[:, rr2 * 128:(rr2 + 1) * 128],
                             onesf,
                             nmev[0:1, rr2 * 128:(rr2 + 1) * 128],
                             start=True, stop=True,
                             sem=(sT if rr2 == 3 else None))
                    bmm = sT.n
                    r.wait("scalar", sT, bmm)
                    ba = r.op("scalar", "copy", bpl[g % 2][:], bps[:],
                              sem=sA)
                    r.wait("vector", sT, bmm)
                    r.wait("vector", sL, mask_ld)
                    r.wait("vector", sA, ba)
                    for rr2 in range(4):
                        r.op("vector", "tensor_tensor", bmk[g % 2][rr2][:],
                             bps[:], maskt[:, rr2 * 512:(rr2 + 1) * 512],
                             ALU.add, sem=(sV if rr2 == 3 else None))
                    bev_as[g % 2] = ba
                    bev_vs[g % 2] = sV.n
                    gset("q0", (sV, sV.n))

                build_b(0)
                for g in range(NQG):
                    nk = 4 * (g + 1)
                    gav = []
                    bev_a = bev_as[g % 2]
                    bev_v = bev_vs[g % 2]
                    bplain_g = bpl[g % 2]
                    bmask_g = bmk[g % 2]
                    sub_hist = []
                    exp_hist = []
                    stl = [stps[0], stps[1], qkps[1]]
                    for jk in range(nk):
                        p3 = jk % 3
                        if jk >= 3:
                            r.wait("tensor", sV, sub_hist[jk - 3])
                        elif jk == 0:
                            gwait("tensor", "q1")
                        qkmm = r.op("tensor", "matmul", stl[p3][:],
                                    qsl[:, jk * 128:(jk + 1) * 128],
                                    qsl[:, g * 512:(g + 1) * 512],
                                    start=True, stop=True, sem=sT)
                        if jk >= 2:
                            r.wait("tensor", sA, exp_hist[jk - 2])
                            if jk == 2:
                                gwait("tensor", "av")
                            av = r.op("tensor", "matmul", avps[:],
                                      vsb[:, (jk - 2) * 128:
                                          (jk - 1) * 128],
                                      pTb[(jk - 2) % 3][:],
                                      start=(jk - 2 == 0), stop=False,
                                      skip_group_check=True, sem=sT2)
                            gav.append(av)
                        r.wait("vector", sT, qkmm)
                        if jk < 4 * g:
                            r.wait("vector", sA, bev_a)
                            bsrc = bplain_g
                        else:
                            r.wait("vector", sV, bev_v)
                            bsrc = bmask_g[jk - 4 * g]
                        if jk >= 3:
                            r.wait("vector", sA, exp_hist[jk - 3])
                        sb = r.op("vector", "tensor_tensor", sadj[p3][:],
                                  stl[p3][:], bsrc[:], ALU.add, sem=sV)
                        sub_hist.append(sb)
                        r.wait("scalar", sV, sb)
                        if jk >= 3:
                            r.wait("scalar", sT2,
                                   gav[jk - 3] if jk - 3 < len(gav)
                                   else sT2.n)
                        ex = r.op("scalar", "activation", pTb[p3][:],
                                  sadj[p3][:], AF.Exp, sem=sA)
                        exp_hist.append(ex)
                    for jj in (nk - 2, nk - 1):
                        if jj < 0:
                            continue
                        r.wait("tensor", sA, exp_hist[jj])
                        gwait("tensor", "av")
                        av = r.op("tensor", "matmul", avps[:],
                                  vsb[:, jj * 128:(jj + 1) * 128],
                                  pTb[jj % 3][:],
                                  start=(jj == 0), stop=(jj == nk - 1),
                                  skip_group_check=True, sem=sT2)
                        gav.append(av)
                    gset("q1", (sV, sub_hist[-1] if nk >= 3 else sV.n))
                    if True:
                        pass
                    if g + 1 < NQG:
                        build_b(g + 1)
                    r.wait("scalar", sT2, gav[-1])
                    gs = g % 2
                    r.wait("scalar", sSpO[gs], oTg_free[gs])
                    oe = r.op("scalar", "activation", oTg[gs][:], avps[:],
                              AF.Copy, sem=sA2)
                    gset("av", (sA2, oe))
                    r.wait("tensor", sA2, oe)
                    r.wait("sync", sA2, oe)
                    r.op("sync", "dma_start",
                         o_cc_in[h * 128:(h + 1) * 128,
                                 g * 512:(g + 1) * 512],
                         oTg[gs][:], sem=sSpO[gs], inc=16)
                    oTg_free[gs] = sSpO[gs].n
                    pump(1)
                    q_w2f(8 * h + g)
                    if g >= 6:
                        q_wo(2 * h + (g - 6))
                pump(99)
                unit_done.append(sA2.n)
                prev_unit_av = sT2.n

            for h in range(NH):
                emit_unit(h)

            r.wait("gpsimd", sSpO[0], sSpO[0].n)
            r.wait("gpsimd", sSpO[1], sSpO[1].n)

            def do_cc(gp):
                gp.collective_compute(
                    "AllGather", ALU.bypass,
                    replica_groups=[[0, 1], [2, 3], [4, 5], [6, 7]],
                    ins=[o_cc_in.ap().opt()], outs=[o_cc_out.ap().opt()],
                ).then_inc(sCC.h, 1)
            sCC.n += 1
            r.call("gpsimd", do_cc)
            for s_ in (sSp, sSpS1, sSpW2, sSpWo, sL, sLxn, sLwf, sLw2m,
                       sLwom, sLu[0], sLu[1], sSpO[0], sSpO[1]):
                r.wait("sync", s_, s_.n)
            r.flush(nc)
        b5_spill_state = [(s_, s_.n) for s_ in
                          (sSp, sSpS1, sSpW2, sSpWo, sSpO[0], sSpO[1])]

        if "attn" in dbg:
            o = dbg_out("dbg_oT", [NH * 128, S], BF16)
            r = Rec()
            r.wait("sync", sSp, sSp.n)
            r.op("sync", "dma_start", o[:], o_cc_in[:], sem=sOut, inc=16)
            r.wait("sync", sOut, sOut.n)
            r.flush(nc)
        if upto == "B4":
            return nc, dbg_outs

        if "ffnup" in dbg:
            o = dbg_out("dbg_s1", [FH * 128, TOWN], BF16)
            o2 = dbg_out("dbg_w2f", [ND * FH * 128, 128], BF16)
            r = Rec()
            r.wait("sync", sSp, sSp.n)
            for hi in range(FH):
                r.op("sync", "dma_start", o[hi * 128:(hi + 1) * 128, :],
                     s1_dram[hi], sem=sOut, inc=16)
            r.op("sync", "dma_start", o2.ap(),
                 w2f_dram.ap().rearrange("m h p d -> (m h p) d"), sem=sOut,
                 inc=16)
            r.wait("sync", sOut, sOut.n)
            r.flush(nc)
        if upto == "B5":
            return nc, dbg_outs

        with ExitStack() as c6:
            e6 = c6.enter_context
            acc = e6(nc.sbuf_tensor("acc", [128, ND * 512], F32))
            ogst = e6(nc.sbuf_tensor("ogst", [128, ND * 512], BF16))
            s1t = e6(nc.sbuf_tensor("s1t", [128, FH * 512], BF16))
            w2fst2 = [e6(nc.sbuf_tensor(f"w2fst{i}", [128, FH * 128], BF16))
                      for i in range(2)]
            wost2 = [e6(nc.sbuf_tensor(f"wost{i}", [128, D], BF16))
                     for i in range(2)]
            xnst = e6(nc.sbuf_tensor("xnst", [128, ND * 512], BF16))
            ops_ = [e6(nc.psum_tensor(f"ops{i}", [128, 512], F32))
                    for i in range(2)]
            dps = [e6(nc.psum_tensor(f"dps{i}", [128, 512], F32))
                   for i in range(2)]

            r = Rec()
            for s_, v_ in b5_spill_state:
                r.wait("sync", s_, v_)
            out_hist = 0
            oev_hist = []
            dev_hist = []
            wo_mms = []
            w2_mms = []
            s1_lds = []
            init_done = 0

            def load_s1(t4):
                nonlocal s1_lds
                if w2_mms:
                    r.wait("sync", sT2, w2_mms[-1])
                s1_lds = []
                for c in range(4):
                    r.op("sync", "dma_start",
                         bass.AP(s1t, c * 16 * 512,
                                 [[FH * 512, 128], [512, 16], [1, 512]]),
                         bass.AP(s1_dram, c * 16 * 128 * TOWN + t4 * 512,
                                 [[TOWN, 128], [128 * TOWN, 16], [1, 512]]),
                         sem=sLs1c[c], inc=16)
                    s1_lds.append(sLs1c[c].n)

            load_s1(0)
            for t4 in range(TOWN // 512):
                if t4 > 0:
                    r.wait("sync", sOut, out_hist)
                r.op("sync", "dma_start",
                     bass.AP(acc, 0, [[ND * 512, 128], [512, ND], [1, 512]]),
                     bass.AP(x_own_tr, t4 * 512,
                             [[TOWN, 128], [128 * TOWN, ND], [1, 512]]),
                     sem=sLb6, inc=16)
                if t4 > 0:
                    r.wait("sync", sV, init_done)
                r.op("sync", "dma_start",
                     bass.AP(xnst, 0, [[ND * 512, 128], [512, ND], [1, 512]]),
                     bass.AP(xnT_sp, t4 * 512,
                             [[ND * TOWN, 128], [TOWN, ND], [1, 512]]),
                     sem=sLb6, inc=16)
                base_ld = sLb6.n

                def og_load(gp, t4=t4):
                    with gp.register(f"rgo{t4}") as rg:
                        gp.reg_load(rg, coloff)
                        if t4:
                            gp.reg_add(rg, rg, t4 * 512)
                        gp.dma_start(
                            bass.AP(ogst, 0, [[ND * 512, 128], [512, ND],
                                              [1, 512]]),
                            bass.AP(o_cc_out, rg,
                                    [[S, 128], [128 * S, ND], [1, 512]])
                        ).then_inc(sLog.h, 16)
                if t4 > 0:
                    r.wait("gpsimd", sT, wo_mms[-1])
                r.wait("gpsimd", sCC, sCC.n)
                r.call("gpsimd", og_load)
                sLog.n += 16
                og_ld = sLog.n

                r.wait("vector", sLb6, base_ld)
                for k in range(ND):
                    r.op("vector", "tensor_tensor",
                         acc[:, k * 512:(k + 1) * 512],
                         acc[:, k * 512:(k + 1) * 512],
                         bcast(wobias, 512, k), ALU.add)
                init_done = r.op("vector", "tensor_tensor", acc[:], acc[:],
                                 xnst[:], ALU.add, sem=sV)

                my_s1 = list(s1_lds)
                for m in range(ND):
                    sl = m % 2
                    if len(w2_mms) >= 2:
                        r.wait("sync", sT2, w2_mms[-2])
                    r.op("sync", "dma_start",
                         bass.AP(w2fst2[sl], 0, [[FH * 128, 128], [128, FH],
                                                 [1, 128]]),
                         bass.AP(w2f_dram, m * FH * 128 * 128,
                                 [[128, 128], [128 * 128, FH], [1, 128]]),
                         sem=sLw26[sl], inc=16)
                    w2_ld = sLw26[sl].n
                    slot = dps[m % 2]
                    r.wait("tensor", sLw26[sl], w2_ld)
                    if len(dev_hist) >= 2:
                        r.wait("tensor", sV2, dev_hist[-2])
                    for hi in range(FH):
                        if m == 0 and hi % 16 == 0:
                            r.wait("tensor", sLs1c[hi // 16],
                                   my_s1[hi // 16])
                        r.op("tensor", "matmul", slot[:],
                             w2fst2[sl][:, hi * 128:(hi + 1) * 128],
                             s1t[:, hi * 512:(hi + 1) * 512],
                             start=(hi == 0), stop=(hi == FH - 1),
                             sem=(sT2 if hi == FH - 1 else None))
                    w2_mms.append(sT2.n)
                    r.wait("vector", sT2, sT2.n)
                    if m == 0:
                        r.wait("vector", sV, init_done)
                    dv_ = r.op("vector", "tensor_tensor",
                               acc[:, m * 512:(m + 1) * 512], slot[:],
                               acc[:, m * 512:(m + 1) * 512], ALU.add,
                               sem=sV2)
                    dev_hist.append(dv_)

                if t4 + 1 < TOWN // 512:
                    load_s1(t4 + 1)

                for m in range(ND):
                    sl = m % 2
                    if len(wo_mms) >= 2:
                        r.wait("sync", sT, wo_mms[-2])
                    r.op("sync", "dma_start", wost2[sl][:],
                         bass.AP(wob_dram, m * 128,
                                 [[D, 128], [128 * D, ND], [1, 128]]),
                         sem=sLwo6[sl], inc=16)
                    wo_ld = sLwo6[sl].n
                    slot = ops_[m % 2]
                    r.wait("tensor", sLwo6[sl], wo_ld)
                    if m == 0:
                        r.wait("tensor", sLog, og_ld)
                    if len(oev_hist) >= 2:
                        r.wait("tensor", sV, oev_hist[-2])
                    for k in range(ND):
                        r.op("tensor", "matmul", slot[:],
                             wost2[sl][:, k * 128:(k + 1) * 128],
                             ogst[:, k * 512:(k + 1) * 512],
                             start=(k == 0), stop=(k == ND - 1),
                             sem=(sT if k == ND - 1 else None))
                    wo_mms.append(sT.n)
                    r.wait("vector", sT, sT.n)
                    ov_ = r.op("vector", "tensor_tensor",
                               acc[:, m * 512:(m + 1) * 512], slot[:],
                               acc[:, m * 512:(m + 1) * 512], ALU.add, sem=sV)
                    oev_hist.append(ov_)

                r.wait("gpsimd", sV, sV.n)
                r.wait("gpsimd", sV2, sV2.n)
                r.op("gpsimd", "dma_start",
                     bass.AP(out_t, t4 * 512,
                             [[TOWN, 128], [128 * TOWN, ND], [1, 512]]),
                     bass.AP(acc, 0, [[ND * 512, 128], [512, ND], [1, 512]]),
                     sem=sOut, inc=16)
                out_hist = sOut.n
            r.wait("sync", sOut, sOut.n)
            r.flush(nc)

    return nc, dbg_outs


def _hadamard(n):
    h = np.array([[1.0]])
    while h.shape[0] < n:
        h = np.block([[h, h], [h, -h]])
    return h


def make_core_inputs(inputs, core):
    X = np.asarray(inputs["L0_attn_X"])
    Wq = np.asarray(inputs["L0_attn_Wq_w"])
    Wqb = np.asarray(inputs["L0_attn_Wq_b"])
    Wo = np.asarray(inputs["L0_attn_Wo_w"])
    Wob = np.asarray(inputs["L0_attn_Wo_b"])
    g = np.asarray(inputs["L0_attn_rms_w"])
    W1 = np.asarray(inputs["L0_ffn_W1"])
    W2 = np.asarray(inputs["L0_ffn_W2"])
    W3 = np.asarray(inputs["L0_ffn_W3"])

    s, half = core // 2, core % 2
    ho = half * NH
    xs = np.ascontiguousarray(X[s], dtype=np.float32)
    x_own = np.ascontiguousarray(xs[half * TOWN:(half + 1) * TOWN])
    d = {
        "x_full": xs,
        "x_own": x_own,
        "x_own_tr": np.ascontiguousarray(x_own.T),
        "wq_t": np.ascontiguousarray(Wq[ho * DK:(ho + NH) * DK, :].T),
        "wq_bh": np.ascontiguousarray(
            Wqb[ho * DK:(ho + NH) * DK].reshape(NH, 128).T),
        "rms_gc": np.ascontiguousarray(g.reshape(ND, 128).T),
        "wo_t": np.ascontiguousarray(Wo.T),
        "wo_bc": np.ascontiguousarray(Wob.reshape(ND, 128).T),
        "w1_t": np.ascontiguousarray(W1.T),
        "w3_t": np.ascontiguousarray(W3.T),
        "coloff_i": np.array([[half * TOWN]], dtype=np.int32),
    }
    T = np.ascontiguousarray(W2.T).reshape(FH, 128, D)
    T2 = T.reshape(FH, 128 // GP, GP, D)
    W2s = np.ascontiguousarray(T2.transpose(2, 0, 1, 3)).reshape(
        128, (128 // GP) * D)
    d["w2_s"] = W2s

    h128 = _hadamard(128)
    h64 = _hadamard(FH)
    bd = np.zeros((128, 128), dtype=np.float64)
    for gi in range(GP):
        bd[gi * FH:(gi + 1) * FH, gi * FH:(gi + 1) * FH] = h64
    d["h128_i"] = h128.astype(BF)
    d["bd_i"] = bd.astype(BF)
    d["identb_i"] = np.eye(128).astype(BF)
    d["identf_i"] = np.eye(128, dtype=np.float32)
    d["onesf_i"] = np.ones((1, 128), dtype=np.float32)
    qi = np.arange(128)[:, None]
    ck = np.arange(512)[None, :]
    mq = np.stack([np.where(ck <= 128 * rr + qi, 0.0, NEG_BIG)
                   for rr in range(4)]).astype(np.float32)
    mt = np.stack([np.where(ck >= 128 * rr + qi, 0.0, NEG_BIG)
                   for rr in range(4)]).astype(np.float32)
    d["maskq_i"] = np.ascontiguousarray(mq.reshape(4 * 128, 512))
    d["maskt_i"] = np.ascontiguousarray(mt.reshape(4 * 128, 512))
    return d


_BUILD_CACHE = {}


def kernel(**inputs):
    if "full" not in _BUILD_CACHE:
        _BUILD_CACHE["full"] = build_program()
    nc, _ = _BUILD_CACHE["full"]
    in_maps = [make_core_inputs(inputs, c) for c in range(8)]
    res = run_bass_kernel_spmd(nc, in_maps, core_ids=list(range(8)))
    out = np.empty((4, S, D), dtype=np.float32)
    for c in range(8):
        s, half = c // 2, c % 2
        ot = np.asarray(res.results[c]["out_t"])
        out[s, half * TOWN:(half + 1) * TOWN, :] = ot.T
    return out


# version 08 (final)
# speedup vs baseline: 1.3938x; 1.0759x over previous
import math
from contextlib import ExitStack

import numpy as np
import ml_dtypes

import concourse.bass as bass
import concourse.mybir as mybir
from concourse.bass_utils import run_bass_kernel_spmd

F32 = mybir.dt.float32
BF16 = mybir.dt.bfloat16
I32 = mybir.dt.int32
AF = mybir.ActivationFunctionType
ALU = mybir.AluOpType
AX = mybir.AxisListType
BF = ml_dtypes.bfloat16

S = 4096
D = 2048
DK = 128
NH = 8
F = 8192
FH = F // 128
GP = 128 // FH
TOWN = S // 2
NT = S // 128
NTO = TOWN // 128
ND = D // 128
NQT = S // 128
NQG = S // 512
EPS_LN = 1e-5
EPS_F32 = float(np.finfo(np.float32).eps)
SQ4 = 128.0 ** (-0.25)
ISQ128 = 1.0 / math.sqrt(128.0)
ISQ64 = 1.0 / math.sqrt(float(FH))
TWO_PI = 2.0 * math.pi
NEG_BIG = -1.0e30

ENGS = ("sync", "vector", "scalar", "tensor", "gpsimd")


class Sem:
    __slots__ = ("h", "n", "name")

    def __init__(self, h, name):
        self.h, self.n, self.name = h, 0, name


_CHAIN_ENGS = ("vector", "scalar", "gpsimd")
_NO_CHAIN_METHODS = {"dma_start", "nop", "engine_nop"}


def _ap_name(a):
    t = getattr(a, "tensor", None)
    return getattr(t, "name", None) if t is not None else None


def _rw(method, args, kw):
    writes, reads = set(), set()
    aps = [a for a in args if _ap_name(a)]
    if method in ("memset", "iota"):
        if aps:
            writes.add(_ap_name(aps[0]))
    else:
        if aps:
            writes.add(_ap_name(aps[0]))
        for a in aps[1:]:
            reads.add(_ap_name(a))
    for k, v in kw.items():
        n = _ap_name(v)
        if n is None:
            continue
        if k == "accum_out":
            writes.add(n)
        else:
            reads.add(n)
    return writes, reads


class Rec:

    chain_sems = {}

    def __init__(self):
        self.q = {e: [] for e in ENGS}
        self.lw = {e: {} for e in _CHAIN_ENGS}
        self.lr = {e: {} for e in _CHAIN_ENGS}
        self.fence = {e: 0 for e in _CHAIN_ENGS}

    def wait(self, eng, sem, val):
        if val > 0:
            self.q[eng].append(("w", sem.h, int(val)))

    def op(self, eng, method, *args, sem=None, inc=1, **kw):
        chain = None
        if eng in _CHAIN_ENGS and method not in _NO_CHAIN_METHODS:
            cs = Rec.chain_sems[eng]
            writes, reads = _rw(method, args, kw)
            need = 0
            for n in reads:
                need = max(need, self.lw[eng].get(n, 0))
            for n in writes:
                need = max(need, self.lw[eng].get(n, 0),
                           self.lr[eng].get(n, 0))
            if need > self.fence[eng]:
                self.q[eng].append(("w", cs.h, int(need)))
                self.fence[eng] = need
            cs.n += 1
            chain = cs
            for n in writes:
                self.lw[eng][n] = cs.n
            for n in reads:
                self.lr[eng][n] = max(self.lr[eng].get(n, 0), cs.n)
        if sem is not None and sem is chain:
            sem = None
        if sem is not None:
            sem.n += inc
        self.q[eng].append(("o", method, args, kw,
                            (sem.h if sem else None), inc,
                            (chain.h if chain else None)))
        return chain.n if chain is not None else (
            sem.n if sem is not None else None)

    def call(self, eng, fn):
        self.q[eng].append(("c", fn))

    def flush(self, nc):
        if not any(self.q[e] for e in ENGS):
            return
        with nc.Block() as block:
            for name in ENGS:
                items = self.q[name]
                if not items:
                    continue

                def body(eng, items=items):
                    for it in items:
                        if it[0] == "w":
                            eng.wait_ge(it[1], it[2])
                        elif it[0] == "c":
                            it[1](eng)
                        else:
                            _, method, args, kw, semh, inc, chainh = it
                            ins = getattr(eng, method)(*args, **kw)
                            if semh is not None:
                                ins.then_inc(semh, inc)
                            if chainh is not None:
                                ins.then_inc(chainh, 1)

                getattr(block, name)(body)


def bcol(t, col=0):
    return t[:, col:col + 1]


def bcast(t, n, col=0):
    return bass.AP(t, col, [[t.shape[1], 128], [0, n]])


def build_program(upto=None, dbg=(), b4_units=None, b4_pa_only=False, b4_groups=None, b4_noav=False, b4_nokq=False, b4_nob=False, b4_bstage=0, b4_nobm=False, b4_nobp=False):
    nc = bass.Bass("TRN2", target_bir_lowering=False)
    dbg = set(dbg)

    x_full = nc.dram_tensor("x_full", [S, D], F32, kind="ExternalInput")
    x_own = nc.dram_tensor("x_own", [TOWN, D], F32, kind="ExternalInput")
    x_own_tr = nc.dram_tensor("x_own_tr", [D, TOWN], F32, kind="ExternalInput")
    wq_t = nc.dram_tensor("wq_t", [D, NH * DK], F32, kind="ExternalInput")
    wq_bh = nc.dram_tensor("wq_bh", [128, NH], F32, kind="ExternalInput")
    rms_gc = nc.dram_tensor("rms_gc", [128, ND], F32, kind="ExternalInput")
    wo_t = nc.dram_tensor("wo_t", [D, D], F32, kind="ExternalInput")
    wo_bc = nc.dram_tensor("wo_bc", [128, ND], F32, kind="ExternalInput")
    w1_t = nc.dram_tensor("w1_t", [D, F], F32, kind="ExternalInput")
    w3_t = nc.dram_tensor("w3_t", [D, F], F32, kind="ExternalInput")
    w2_s = nc.dram_tensor("w2_s", [128, (128 // GP) * D], F32, kind="ExternalInput")
    h128_i = nc.dram_tensor("h128_i", [128, 128], BF16, kind="ExternalInput")
    bd_i = nc.dram_tensor("bd_i", [128, 128], BF16, kind="ExternalInput")
    identb_i = nc.dram_tensor("identb_i", [128, 128], BF16, kind="ExternalInput")
    identf_i = nc.dram_tensor("identf_i", [128, 128], F32, kind="ExternalInput")
    onesf_i = nc.dram_tensor("onesf_i", [1, 128], F32, kind="ExternalInput")
    maskq_i = nc.dram_tensor("maskq_i", [4 * 128, 512], F32, kind="ExternalInput")
    maskt_i = nc.dram_tensor("maskt_i", [4 * 128, 512], F32, kind="ExternalInput")
    coloff_i = nc.dram_tensor("coloff_i", [1, 1], I32, kind="ExternalInput")

    out_t = nc.dram_tensor("out_t", [D, TOWN], F32, kind="ExternalOutput")

    xnT_sp = nc.dram_tensor("xnT_sp", [128, ND * TOWN], BF16)
    qr_dram = nc.dram_tensor("qr_dram", [NH, 128, S], BF16)
    v_dram = nc.dram_tensor("v_dram", [NH, 128, S], BF16)
    s1_dram = nc.dram_tensor("s1_dram", [FH, 128, TOWN], BF16)
    w2f_dram = nc.dram_tensor("w2f_dram", [ND, FH, 128, 128], BF16)
    wob_dram = nc.dram_tensor("wob_dram", [D, D], BF16)
    o_cc_in = nc.dram_tensor("o_cc_in", [NH * 128, S], BF16)
    o_cc_out = nc.dram_tensor("o_cc_out", [2 * NH * 128, S], BF16)

    dbg_outs = {}

    def dbg_out(name, shape, dtype=F32):
        dbg_outs[name] = nc.dram_tensor(name, shape, dtype, kind="ExternalOutput")
        return dbg_outs[name]

    with ExitStack() as ctx:
        e = ctx.enter_context

        def sem(name):
            return Sem(e(nc.semaphore(name)), name)

        sL = sem("sL")
        sSp = sem("sSp")
        sT = sem("sT")
        sT2 = sem("sT2")
        sCC = sem("sCC")
        sOut = sem("sOut")
        Rec.chain_sems = {"vector": sem("cV"), "scalar": sem("cA"),
                          "gpsimd": sem("cG")}
        sV = sV2 = Rec.chain_sems["vector"]
        sA = sA2 = Rec.chain_sems["scalar"]
        sG = Rec.chain_sems["gpsimd"]
        sLx = [sem("sLx0"), sem("sLx1")]
        sLq = [sem("sLq0"), sem("sLq1")]
        sLu = [sem("sLu0"), sem("sLu1")]
        sLxn = sem("sLxn")
        sLb6 = sem("sLb6")
        sLs1c = [sem(f"sLs1c{i}") for i in range(4)]
        sLog = sem("sLog")
        sLwo6 = [sem(f"sLwo6{i}") for i in range(2)]
        sLw26 = [sem(f"sLw26{i}") for i in range(2)]
        sLwf = sem("sLwf")
        sLw2m = sem("sLw2m")
        sLwom = sem("sLwom")
        sSpO = [sem("sSpO0"), sem("sSpO1")]
        sSpS1 = sem("sSpS1")
        sSpW2 = sem("sSpW2")
        sSpWo = sem("sSpWo")

        h128b = e(nc.sbuf_tensor("h128b", [128, 128], BF16))
        bdb = e(nc.sbuf_tensor("bdb", [128, 128], BF16))
        identb = e(nc.sbuf_tensor("identb", [128, 128], BF16))
        identf = e(nc.sbuf_tensor("identf", [128, 128], F32))
        onesf_t = e(nc.sbuf_tensor("onesf_t", [128, 128], F32))
        onesf = onesf_t[0:1, 0:128]
        rmsg = e(nc.sbuf_tensor("rmsg", [128, ND], F32))
        wqbias = e(nc.sbuf_tensor("wqbias", [128, NH], F32))
        wobias = e(nc.sbuf_tensor("wobias", [128, ND], F32))
        coloff_t = e(nc.sbuf_tensor("coloff_t", [128, 1], I32))
        coloff = coloff_t[0:1, 0:1]
        cIsq128 = e(nc.sbuf_tensor("cIsq128", [128, 1], F32))
        cIsq64 = e(nc.sbuf_tensor("cIsq64", [128, 1], F32))

        r = Rec()
        for dst, src in ((h128b[:], h128_i), (bdb[:], bd_i),
                         (identb[:], identb_i), (identf[:], identf_i),
                         (onesf, onesf_i), (rmsg[:], rms_gc),
                         (wqbias[:], wq_bh), (wobias[:], wo_bc),
                         (coloff, coloff_i)):
            r.op("sync", "dma_start", dst, src[:], sem=sL, inc=16)
        r.op("vector", "memset", cIsq128[:], ISQ128)
        r.op("vector", "memset", cIsq64[:], ISQ64)
        r.wait("sync", sL, sL.n)
        r.flush(nc)

        with ExitStack() as cEarly:
            eE = cEarly.enter_context
            cosT = eE(nc.sbuf_tensor("cosT", [128, S], F32))
            sinT = eE(nc.sbuf_tensor("sinT", [128, S], F32))

            with ExitStack() as c1:
                e1 = c1.enter_context
                jcol = e1(nc.sbuf_tensor("jcol", [128, 1], F32))
                ifcol = e1(nc.sbuf_tensor("ifcol", [128, 1], F32))
                angw = e1(nc.sbuf_tensor("angw", [128, S], F32))
                uw = e1(nc.sbuf_tensor("uw", [128, S], F32))
                kacc = e1(nc.sbuf_tensor("kacc", [128, S], F32))
                cmpw = e1(nc.sbuf_tensor("cmpw", [128, S], F32))
                cNegLn = e1(nc.sbuf_tensor("cNegLn", [128, 1], F32))

                r = Rec()
                r.op("gpsimd", "iota", jcol[:, 0:1], [[1, 1]],
                     channel_multiplier=1, allow_small_or_imprecise_dtypes=True)
                r.op("gpsimd", "iota", angw[:, :], [[1, S]],
                     channel_multiplier=0, allow_small_or_imprecise_dtypes=True,
                     sem=sG)
                r.op("vector", "memset", cNegLn[:], -math.log(10000.0) / 64.0)
                r.wait("vector", sG, sG.n)
                w = r.op("vector", "tensor_single_scalar", cmpw[:, 0:1],
                         jcol[:], float(FH), ALU.is_ge, sem=sV)
                r.wait("vector", sV, w)
                r.op("vector", "scalar_tensor_tensor", jcol[:], cmpw[:, 0:1],
                     -float(FH), jcol[:], ALU.mult, ALU.add, sem=sV)
                r.wait("scalar", sV, sV.n)
                r.op("scalar", "activation", ifcol[:], jcol[:], AF.Exp,
                     scale=bcol(cNegLn), sem=sA)
                r.wait("vector", sA, sA.n)
                r.op("vector", "tensor_tensor", angw[:], angw[:],
                     bcast(ifcol, S), ALU.mult)
                r.op("vector", "tensor_single_scalar", angw[0:64, :],
                     angw[0:64, :], math.pi / 2.0, ALU.add)
                r.op("vector", "tensor_single_scalar", uw[:], angw[:],
                     1.0 / TWO_PI, ALU.mult)
                r.op("vector", "tensor_single_scalar", uw[:], uw[:], 0.5,
                     ALU.add)
                r.op("vector", "memset", kacc[:], 0.0)
                for b in (512, 256, 128, 64, 32, 16, 8, 4, 2, 1):
                    r.op("vector", "tensor_single_scalar", cmpw[:], uw[:],
                         float(b), ALU.is_ge)
                    r.op("vector", "scalar_tensor_tensor", uw[:], cmpw[:],
                         -float(b), uw[:], ALU.mult, ALU.add)
                    r.op("vector", "scalar_tensor_tensor", kacc[:], cmpw[:],
                         float(b), kacc[:], ALU.mult, ALU.add)
                r.op("vector", "scalar_tensor_tensor", angw[:], kacc[:],
                     -TWO_PI, angw[:], ALU.mult, ALU.add)
                r.op("vector", "tensor_single_scalar", angw[:], angw[:],
                     3.141592, ALU.min)
                r.op("vector", "tensor_single_scalar", angw[:], angw[:],
                     -3.141592, ALU.max, sem=sV)
                r.wait("scalar", sV, sV.n)
                r.op("scalar", "activation", uw[:], angw[:], AF.Sin, sem=sA)
                r.wait("vector", sA, sA.n)
                sc_ = r.op("vector", "tensor_single_scalar", uw[:], uw[:],
                           SQ4, ALU.mult, sem=sV)
                r.wait("vector", sV, sc_)
                r.op("vector", "tensor_copy", cosT[0:64, :], uw[0:64, :])
                r.op("vector", "tensor_copy", cosT[64:128, :], uw[0:64, :])
                r.op("vector", "tensor_copy", sinT[0:64, :], uw[64:128, :])
                r.op("vector", "tensor_copy", sinT[64:128, :], uw[64:128, :],
                     sem=sV)
                r.flush(nc)

            if "tables" in dbg:
                o = dbg_out("dbg_tables", [128, S])
                r = Rec()
                r.op("sync", "dma_start", o[:], cosT[:], sem=sOut, inc=16)
                r.wait("sync", sOut, sOut.n)
                r.flush(nc)
            if upto == "B1":
                return nc, dbg_outs

            xnrsT = [eE(nc.sbuf_tensor(f"xnrsT{k}", [128, S], BF16))
                     for k in range(ND)]

            with ExitStack() as c2:
                e2 = c2.enter_context
                xstg = [e2(nc.sbuf_tensor(f"xstg{i}", [128, D], F32))
                        for i in range(2)]
                sqdump = e2(nc.sbuf_tensor("sqdump", [128, D], BF16))
                xnb = e2(nc.sbuf_tensor("xnb", [128, D], BF16))
                xnstrip = e2(nc.sbuf_tensor("xnstrip", [128, D], BF16))
                s1c = e2(nc.sbuf_tensor("s1c", [128, 1], F32))
                s2c = e2(nc.sbuf_tensor("s2c", [128, 1], F32))
                meanc = e2(nc.sbuf_tensor("meanc", [128, 1], F32))
                varc = e2(nc.sbuf_tensor("varc", [128, 1], F32))
                r1c = e2(nc.sbuf_tensor("r1c", [128, 1], F32))
                uc = e2(nc.sbuf_tensor("uc", [128, 1], F32))
                rstdc = e2(nc.sbuf_tensor("rstdc", [128, 1], F32))
                gc = e2(nc.sbuf_tensor("gc", [128, 1], F32))
                cc = e2(nc.sbuf_tensor("cc", [128, 1], F32))
                bcl = e2(nc.sbuf_tensor("bcl", [128, 1], F32))
                trp = [e2(nc.psum_tensor(f"trp{i}", [128, 128], BF16))
                       for i in range(2)]

                r = Rec()
                ntot = NT + NTO
                norm_done = []
                ev_done = []
                trp_guard = [0, 0]
                spill_guard = 0
                for i in range(ntot):
                    own = i >= NT
                    ti = i - NT if own else i
                    src = x_own if own else x_full
                    slot = xstg[i % 2]
                    if i >= 2:
                        r.wait("sync", sA, norm_done[i - 2])
                    lsem = sLx[i % 2]
                    r.op("sync", "dma_start", slot[:],
                         src[ti * 128:(ti + 1) * 128, :], sem=lsem, inc=16)
                    ld = lsem.n
                    r.wait("vector", lsem, ld)
                    vv = r.op("vector", "reduce_sum", s1c[:], slot[:], AX.X,
                              sem=sV)
                    r.wait("scalar", lsem, ld)
                    if i >= 1:
                        r.wait("scalar", sV2, ev_done[i - 1])
                    sq = r.op("scalar", "activation", sqdump[:], slot[:],
                              AF.Square, accum_out=s2c[:, 0:1], sem=sA)
                    r.wait("vector", sA, sq)

                    def vchain(method, *args):
                        w = r.op("vector", method, *args, sem=sV)
                        r.wait("vector", sV, w)
                        return w

                    r.wait("vector", sV, vv)
                    vchain("tensor_single_scalar", meanc[:], s1c[:],
                           1.0 / D, ALU.mult)
                    vchain("tensor_single_scalar", varc[:], s2c[:],
                           1.0 / D, ALU.mult)
                    vchain("tensor_tensor", uc[:], meanc[:], meanc[:],
                           ALU.mult)
                    vchain("tensor_tensor", varc[:], varc[:], uc[:],
                           ALU.subtract)
                    vchain("tensor_single_scalar", r1c[:], varc[:],
                           EPS_LN, ALU.add)
                    vchain("reciprocal", r1c[:], r1c[:])
                    if own:
                        cols = vchain("tensor_copy", uc[:], r1c[:])
                    else:
                        vchain("tensor_tensor", uc[:], varc[:], r1c[:],
                               ALU.mult)
                        vchain("tensor_single_scalar", uc[:], uc[:],
                               EPS_F32, ALU.add)
                        cols = vchain("reciprocal", uc[:], uc[:])
                    r.wait("scalar", sV, cols)
                    r.op("scalar", "activation", rstdc[:], r1c[:], AF.Sqrt)
                    sqr = r.op("scalar", "activation", gc[:], uc[:], AF.Sqrt,
                               sem=sA)
                    r.wait("vector", sA, sqr)
                    if own:
                        ccv = r.op("vector", "tensor_copy", cc[:], rstdc[:],
                                   sem=sV)
                    else:
                        ccv = r.op("vector", "tensor_tensor", cc[:], rstdc[:],
                                   gc[:], ALU.mult, sem=sV)
                    r.wait("vector", sV, ccv)
                    cb = r.op("vector", "scalar_tensor_tensor", bcl[:],
                              meanc[:], -1.0, cc[:], ALU.mult, ALU.mult,
                              sem=sV)
                    r.wait("scalar", sV, cb)
                    nd_ = r.op("scalar", "activation", xnb[:], slot[:],
                               AF.Identity, bias=bcol(bcl), scale=bcol(cc),
                               sem=sA)
                    norm_done.append(nd_)
                    for k in range(ND):
                        ps = k % 2
                        r.wait("tensor", sA, nd_)
                        r.wait("tensor", sV2, trp_guard[ps])
                        tr = r.op("tensor", "transpose", trp[ps][:],
                                  xnb[:, k * 128:(k + 1) * 128], identb[:],
                                  sem=sT)
                        r.wait("vector", sT, tr)
                        if own and k == 0:
                            r.wait("vector", sSp, spill_guard)
                        if own:
                            evv = r.op("vector", "tensor_copy",
                                       xnstrip[:, k * 128:(k + 1) * 128],
                                       trp[ps][:], sem=sV2)
                        else:
                            evv = r.op("vector", "tensor_copy",
                                       xnrsT[k][:, ti * 128:(ti + 1) * 128],
                                       trp[ps][:], sem=sV2)
                        trp_guard[ps] = evv
                    ev_done.append(sV2.n)
                    if own:
                        r.wait("sync", sV2, ev_done[i])
                        r.op("sync", "dma_start",
                             bass.AP(xnT_sp, ti * 128,
                                     [[ND * TOWN, 128], [TOWN, ND], [1, 128]]),
                             bass.AP(xnstrip, 0,
                                     [[D, 128], [128, ND], [1, 128]]),
                             sem=sSp, inc=16)
                        spill_guard = sSp.n
                r.wait("sync", sSp, sSp.n)
                r.wait("sync", sLx[0], sLx[0].n)
                r.wait("sync", sLx[1], sLx[1].n)
                r.flush(nc)
            b2_spills = sSp.n

            if "xnrst" in dbg:
                o = dbg_out("dbg_xnrst", [ND * 128, S], BF16)
                o2 = dbg_out("dbg_xnt", [128, ND * TOWN], BF16)
                r = Rec()
                for k in range(ND):
                    r.op("sync", "dma_start", o[k * 128:(k + 1) * 128, :],
                         xnrsT[k][:], sem=sOut, inc=16)
                r.op("sync", "dma_start", o2[:], xnT_sp[:], sem=sOut,
                     inc=16)
                r.wait("sync", sOut, sOut.n)
                r.flush(nc)
            if upto == "B2":
                return nc, dbg_outs

            with ExitStack() as c3:
                e3 = c3.enter_context
                wqstg2 = [e3(nc.sbuf_tensor(f"wqstg{i}", [128, D // 4], F32))
                          for i in range(2)]
                wqb = e3(nc.sbuf_tensor("wqb", [128, D], BF16))
                qT = e3(nc.sbuf_tensor("qT", [128, S], F32))
                t1w = e3(nc.sbuf_tensor("t1w", [128, 512], F32))
                t2w = e3(nc.sbuf_tensor("t2w", [128, 512], F32))
                qrb = e3(nc.sbuf_tensor("qrb", [128, S], BF16))
                vbst = e3(nc.sbuf_tensor("vbst", [128, S], BF16))
                qps = [e3(nc.psum_tensor(f"qps{i}", [128, 512], F32))
                       for i in range(2)]

                r = Rec()
                evac_hist = []
                vcopy_hist = []
                spill_hist = []
                conv_done = [0] * NH
                mm_last = [0] * NH
                conv_marks = [[0] * 4 for _ in range(NH)]
                ld_marks = [[0] * 4 for _ in range(NH)]

                def load_wq_q(m, q):
                    sl = q % 2
                    if not (m == 0 and q < 2):
                        pm, pq = (m, q - 2) if q >= 2 else (m - 1, q + 2)
                        r.wait("sync", sV, conv_marks[pm][pq])
                    r.op("sync", "dma_start", wqstg2[sl][:],
                         bass.AP(wq_t,
                                 m * 128 + q * (ND // 4) * 128 * NH * DK,
                                 [[NH * DK, 128],
                                  [128 * NH * DK, ND // 4], [1, 128]]),
                         sem=sLq[sl], inc=16)
                    ld_marks[m][q] = sLq[sl].n

                def conv_wq_q(m, q):
                    sl = q % 2
                    r.wait("vector", sLq[sl], ld_marks[m][q])
                    if q == 0 and m >= 1:
                        r.wait("vector", sT, mm_last[m - 1])
                    for k2 in range(ND // 4):
                        k = q * (ND // 4) + k2
                        r.op("vector", "tensor_tensor",
                             wqb[:, k * 128:(k + 1) * 128],
                             wqstg2[sl][:, k2 * 128:(k2 + 1) * 128],
                             bcast(rmsg, 128, k), ALU.mult,
                             sem=(sV if k2 == ND // 4 - 1 else None))
                    conv_marks[m][q] = sV.n

                def conv_wq(m):
                    conv_wq_q(m, 0)
                    load_wq_q(m, 2)
                    conv_wq_q(m, 1)
                    load_wq_q(m, 3)
                    conv_wq_q(m, 2)
                    conv_wq_q(m, 3)
                    conv_done[m] = sV.n

                for q in range(4):
                    load_wq_q(0, q)
                    conv_wq_q(0, q)
                conv_done[0] = sV.n
                for m in range(NH):
                    if m + 1 < NH:
                        load_wq_q(m + 1, 0)
                        load_wq_q(m + 1, 1)
                    for tp in range(8):
                        slot = qps[tp % 2]
                        r.wait("tensor", sV, conv_done[m])
                        if len(evac_hist) >= 2:
                            r.wait("tensor", sA, evac_hist[-2])
                        for k in range(ND):
                            r.op("tensor", "matmul", slot[:],
                                 wqb[:, k * 128:(k + 1) * 128],
                                 xnrsT[k][:, tp * 512:(tp + 1) * 512],
                                 start=(k == 0), stop=(k == ND - 1),
                                 sem=(sT if k == ND - 1 else None))
                        if tp == 7:
                            mm_last[m] = sT.n
                        r.wait("scalar", sT, sT.n)
                        if tp == 0 and m >= 1:
                            r.wait("scalar", sV2, vcopy_hist[m - 1])
                        ev = r.op("scalar", "activation",
                                  qT[:, tp * 512:(tp + 1) * 512], slot[:],
                                  AF.Identity, bias=wqbias[:, m:m + 1], sem=sA)
                        evac_hist.append(ev)
                    if m + 1 < NH:
                        conv_wq(m + 1)
                    q_ev = evac_hist[-1]
                    r.wait("vector", sA, q_ev)
                    if m >= 1:
                        r.wait("vector", sSp, spill_hist[m - 1])
                    for ch in range(8):
                        a, b = ch * 512, (ch + 1) * 512
                        r.op("vector", "tensor_tensor", t1w[:],
                             qT[:, a:b], cosT[:, a:b], ALU.mult)
                        r.op("vector", "tensor_tensor", t2w[0:64, :],
                             qT[64:128, a:b], sinT[64:128, a:b], ALU.mult)
                        r.op("vector", "tensor_tensor", t2w[64:128, :],
                             qT[0:64, a:b], sinT[0:64, a:b], ALU.mult)
                        r.op("vector", "tensor_tensor", qrb[0:64, a:b],
                             t1w[0:64, :], t2w[0:64, :], ALU.subtract)
                        r.op("vector", "tensor_tensor", qrb[64:128, a:b],
                             t1w[64:128, :], t2w[64:128, :], ALU.add)
                    vc = r.op("vector", "tensor_copy", vbst[:], qT[:], sem=sV2)
                    vcopy_hist.append(vc)
                    r.wait("sync", sV2, vc)
                    r.op("sync", "dma_start", qr_dram[m], qrb[:], sem=sSp,
                         inc=16)
                    r.op("sync", "dma_start", v_dram[m], vbst[:], sem=sSp,
                         inc=16)
                    spill_hist.append(sSp.n)
                r.wait("sync", sSp, sSp.n)
                r.flush(nc)
            b3_spills = sSp.n

            if "qproj" in dbg:
                oq = dbg_out("dbg_qr", [NH * 128, S], BF16)
                ov = dbg_out("dbg_v", [NH * 128, S], BF16)
                r = Rec()
                for m in range(NH):
                    r.op("sync", "dma_start", oq[m * 128:(m + 1) * 128, :],
                         qr_dram[m], sem=sOut, inc=16)
                    r.op("sync", "dma_start", ov[m * 128:(m + 1) * 128, :],
                         v_dram[m], sem=sOut, inc=16)
                r.wait("sync", sOut, sOut.n)
                r.flush(nc)
            if upto == "B3":
                return nc, dbg_outs

        with ExitStack() as c4:
            e4 = c4.enter_context
            maskq = e4(nc.sbuf_tensor("maskq", [128, 4 * 512], F32))
            maskt = e4(nc.sbuf_tensor("maskt", [128, 4 * 512], F32))
            qrs = e4(nc.sbuf_tensor("qrs", [128, S], BF16))
            vfs = e4(nc.sbuf_tensor("vfs", [128, S], BF16))
            vsb = e4(nc.sbuf_tensor("vsb", [128, S], BF16))
            edump = vfs
            bnd = [e4(nc.sbuf_tensor(f"bnd{i}", [128, 512], F32))
                   for i in range(2)]
            nmstrip = [e4(nc.sbuf_tensor(f"nmstrip{i}", [128, 8], F32))
                       for i in range(2)]
            sstrip = [e4(nc.sbuf_tensor(f"sstrip{i}", [128, 8], F32))
                      for i in range(2)]
            dstrip = [e4(nc.sbuf_tensor(f"dstrip{i}", [128, 8], F32))
                      for i in range(2)]
            estrip = [e4(nc.sbuf_tensor(f"estrip{i}", [128, 8], F32))
                      for i in range(2)]
            negm = [e4(nc.sbuf_tensor(f"negm{i}", [128, 1], F32))
                    for i in range(2)]
            ssum = [e4(nc.sbuf_tensor(f"ssum{i}", [128, 1], F32))
                    for i in range(2)]
            lnsc = [e4(nc.sbuf_tensor(f"lnsc{i}", [128, 1], F32))
                    for i in range(2)]
            nmadj = [e4(nc.sbuf_tensor(f"nmadj{i}", [128, 1], F32))
                     for i in range(2)]
            nmcols = e4(nc.sbuf_tensor("nmcols", [128, NQT], F32))
            nmev = e4(nc.sbuf_tensor("nmev", [128, 512], F32))
            bplain = e4(nc.sbuf_tensor("bplain", [128, 512], F32))
            bplain2 = e4(nc.sbuf_tensor("bplain2", [128, 512], F32))
            bmask = [e4(nc.sbuf_tensor(f"bmask{i}", [128, 512], F32))
                     for i in range(4)]
            bmask2 = [e4(nc.sbuf_tensor(f"bmask2_{i}", [128, 512], F32))
                      for i in range(4)]
            sadj = [e4(nc.sbuf_tensor(f"sadj{i}", [128, 512], F32))
                    for i in range(3)]
            pTb = [e4(nc.sbuf_tensor(f"pTb{i}", [128, 512], BF16))
                   for i in range(3)]
            oTg = [e4(nc.sbuf_tensor(f"oTg{i}", [128, 512], BF16))
                   for i in range(2)]
            xnto = e4(nc.sbuf_tensor("xnto", [128, ND * TOWN], BF16))
            wstg = e4(nc.sbuf_tensor("wstg", [128, D], F32))
            w1b = [e4(nc.sbuf_tensor(f"w1b{i}", [128, D], BF16))
                   for i in range(2)]
            w3b = [e4(nc.sbuf_tensor(f"w3b{i}", [128, D], BF16))
                   for i in range(2)]
            stmp = e4(nc.sbuf_tensor("stmp", [128, 512], F32))
            hbd = [e4(nc.sbuf_tensor(f"hbd{i}", [128, 512], BF16))
                   for i in range(2)]
            s1strip = e4(nc.sbuf_tensor("s1strip", [128, TOWN], BF16))
            w2stg = e4(nc.sbuf_tensor("w2stg", [128, D], F32))
            w2sb = e4(nc.sbuf_tensor("w2sb", [128, D], BF16))
            w2fout = e4(nc.sbuf_tensor("w2fout", [128, D], BF16))
            qkps = [e4(nc.psum_tensor(f"qkps{i}", [128, 512], F32))
                    for i in range(2)]
            stps = [e4(nc.psum_tensor(f"stps{i}", [128, 512], F32))
                    for i in range(2)]
            avps = e4(nc.psum_tensor("avps", [128, 512], F32))
            fps = [e4(nc.psum_tensor(f"fps{i}", [128, 512], F32))
                   for i in range(3)]
            bps = qkps[0]
            nmps = qkps[0][0:1, 0:128]
            vtp = [qkps[i][:, 0:64].bitcast(BF16) for i in range(2)]

            bankg = {}
            r = Rec()

            def gwait(eng, name):
                g_ = bankg.get(name)
                if g_ is not None:
                    r.wait(eng, g_[0], g_[1])

            def gset(name, semv):
                bankg[name] = semv

            r.op("sync", "dma_start",
                 bass.AP(maskq, 0, [[4 * 512, 128], [512, 4], [1, 512]]),
                 bass.AP(maskq_i, 0, [[512, 128], [128 * 512, 4], [1, 512]]),
                 sem=sL, inc=16)
            r.op("sync", "dma_start",
                 bass.AP(maskt, 0, [[4 * 512, 128], [512, 4], [1, 512]]),
                 bass.AP(maskt_i, 0, [[512, 128], [128 * 512, 4], [1, 512]]),
                 sem=sL, inc=16)
            mask_ld = sL.n
            r.wait("sync", sSp, b2_spills)
            r.op("sync", "dma_start", xnto[:], xnT_sp[:], sem=sLxn, inc=16)
            xn_ld = sLxn.n

            unit_done = []
            exp1_hist = []
            prev_unit_av = 0
            oTg_free = [0, 0]
            silu_last = 0
            mult_last = 0
            h128_last = 0
            wcv = [0, 0]
            pend_h128 = None
            wstg_free = 0
            w2cv = 0
            wo_cv = 0
            s1_sp = 0

            def q_wload(hi):
                nonlocal wstg_free, h128_last
                ws = hi % 2
                r.wait("sync", sV, wstg_free)
                r.op("sync", "dma_start", wstg[:],
                     bass.AP(w1_t, hi * 128, [[F, 128], [128 * F, ND],
                                              [1, 128]]), sem=sLwf, inc=16)
                r.wait("vector", sLwf, sLwf.n)
                if h128_last:
                    r.wait("vector", sT, h128_last)
                cv1 = r.op("vector", "tensor_copy", w1b[ws][:], wstg[:],
                           sem=sV)
                r.wait("sync", sV, cv1)
                r.op("sync", "dma_start", wstg[:],
                     bass.AP(w3_t, hi * 128, [[F, 128], [128 * F, ND],
                                              [1, 128]]), sem=sLwf, inc=16)
                r.wait("vector", sLwf, sLwf.n)
                cv2 = r.op("vector", "tensor_copy", w3b[ws][:], wstg[:],
                           sem=sV)
                wcv[ws] = cv2
                wstg_free = cv2

            def q_ffn(hi, tp):
                nonlocal silu_last, mult_last, h128_last, s1_sp, pend_h128
                ws = hi % 2
                if hi == 0 and tp == 0:
                    r.wait("tensor", sLxn, xn_ld)
                r.wait("tensor", sV, wcv[ws])
                if silu_last:
                    r.wait("tensor", sA, silu_last)
                for k in range(ND):
                    r.op("tensor", "matmul", fps[0][:],
                         w1b[ws][:, k * 128:(k + 1) * 128],
                         xnto[:, k * TOWN + tp * 512:
                               k * TOWN + (tp + 1) * 512],
                         start=(k == 0), stop=(k == ND - 1),
                         sem=(sT if k == ND - 1 else None))
                up_done = sT.n
                if mult_last:
                    r.wait("tensor", sV2, mult_last)
                for k in range(ND):
                    r.op("tensor", "matmul", fps[1][:],
                         w3b[ws][:, k * 128:(k + 1) * 128],
                         xnto[:, k * TOWN + tp * 512:
                               k * TOWN + (tp + 1) * 512],
                         start=(k == 0), stop=(k == ND - 1),
                         sem=(sT if k == ND - 1 else None))
                lin_done = sT.n
                do_h128()
                r.wait("scalar", sT, up_done)
                if mult_last:
                    r.wait("scalar", sV2, mult_last)
                sl_ = r.op("scalar", "activation", stmp[:], fps[0][:],
                           AF.Silu, sem=sA)
                silu_last = sl_
                r.wait("vector", sA, sl_)
                r.wait("vector", sT, lin_done)
                if h128_last:
                    r.wait("vector", sT, h128_last)
                ml = r.op("vector", "tensor_tensor",
                          hbd[(hi * 4 + tp) % 2][:], stmp[:], fps[1][:],
                          ALU.mult, sem=sV2)
                mult_last = ml
                pend_h128 = (hi, tp, ml, (hi * 4 + tp) % 2)

            def do_h128():
                nonlocal h128_last, s1_sp, pend_h128
                if pend_h128 is None:
                    return
                hi, tp, ml, hs = pend_h128
                pend_h128 = None
                cols = slice(tp * 512, (tp + 1) * 512)
                r.wait("tensor", sV2, ml)
                gwait("tensor", "f2")
                hm = r.op("tensor", "matmul", fps[2][:], h128b[:],
                          hbd[hs][:], start=True, stop=True, sem=sT)
                h128_last = hm
                r.wait("scalar", sT, hm)
                if tp == 0:
                    r.wait("scalar", sSpS1, s1_sp)
                s1e = r.op("scalar", "activation", s1strip[:, cols],
                           fps[2][:], AF.Copy, scale=bcol(cIsq128), sem=sA)
                gset("f2", (sA, s1e))
                if tp == 3:
                    r.wait("sync", sA, s1e)
                    r.op("sync", "dma_start", s1_dram[hi], s1strip[:],
                         sem=sSpS1, inc=16)
                    s1_sp = sSpS1.n

            def q_w2f(lam):
                nonlocal w2cv, wo_cv
                do_h128()
                r.wait("sync", sV, w2cv)
                r.wait("sync", sG, wo_cv)
                r.op("sync", "dma_start", w2stg[:],
                     bass.AP(w2_s, lam * D, [[(128 // GP) * D, 128], [1, D]]),
                     sem=sLw2m, inc=16)
                r.wait("vector", sLw2m, sLw2m.n)
                r.wait("vector", sT, sT.n)
                w2cv = r.op("vector", "tensor_copy", w2sb[:], w2stg[:],
                            sem=sV)
                w2ev = 0
                for q4 in range(4):
                    r.wait("tensor", sV, w2cv)
                    gwait("tensor", "f2")
                    r.op("tensor", "matmul", fps[2][:], bdb[:],
                         w2sb[:, q4 * 512:(q4 + 1) * 512], start=True,
                         stop=True, sem=sT)
                    r.wait("scalar", sT, sT.n)
                    if q4 == 0:
                        r.wait("scalar", sSpW2, sSpW2.n)
                        r.wait("scalar", sSpWo, sSpWo.n)
                    w2ev = r.op("scalar", "activation",
                                w2fout[:, q4 * 512:(q4 + 1) * 512],
                                fps[2][:], AF.Copy, scale=bcol(cIsq64),
                                sem=sA)
                    gset("f2", (sA, w2ev))
                r.wait("sync", sA, w2ev)
                for gi in range(GP):
                    ap_sb2 = bass.AP(w2fout, gi * FH * D,
                                     [[D, FH], [128, ND], [1, 128]])
                    ap_dr2 = bass.AP(w2f_dram, lam * GP * 128 + gi * 128,
                                     [[128 * 128, FH], [FH * 128 * 128, ND],
                                      [1, 128]])
                    r.op("sync", "dma_start", ap_dr2, ap_sb2, sem=sSpW2,
                         inc=16)

            def q_wo(k):
                nonlocal wo_cv
                r.wait("sync", sV, w2cv)
                r.wait("sync", sG, wo_cv)
                r.op("sync", "dma_start", w2stg[:],
                     wo_t[k * 128:(k + 1) * 128, :], sem=sLwom, inc=16)
                r.wait("gpsimd", sLwom, sLwom.n)
                r.wait("gpsimd", sSpW2, sSpW2.n)
                r.wait("gpsimd", sSpWo, sSpWo.n)
                r.wait("gpsimd", sA, sA.n)
                wo_cv = r.op("gpsimd", "tensor_copy", w2fout[:], w2stg[:],
                             sem=sG)
                r.wait("sync", sG, wo_cv)
                r.op("sync", "dma_start", wob_dram[k * 128:(k + 1) * 128, :],
                     w2fout[:], sem=sSpWo, inc=16)

            def emit_unit(h):
                nonlocal prev_unit_av
                qsl = qrs
                quanta = []
                for hi in range(8 * h, 8 * h + 8):
                    quanta.append(("wl", hi))
                    for tp in range(4):
                        quanta.append(("tp", hi, tp))
                qi = [0]

                def pump(n=1):
                    for _ in range(n):
                        if qi[0] >= len(quanta):
                            return
                        q = quanta[qi[0]]
                        qi[0] += 1
                        if q[0] == "wl":
                            q_wload(q[1])
                        else:
                            q_ffn(q[1], q[2])

                if h == 0:
                    r.wait("sync", sSp, b3_spills)
                if h >= 2:
                    r.wait("sync", sA2, unit_done[h - 2])
                usem = sLu[h % 2]
                if h >= 1:
                    r.wait("sync", sT2, prev_unit_av)
                    r.wait("sync", sA, exp1_hist[-1])
                r.op("sync", "dma_start", qsl[:], qr_dram[h], sem=usem,
                     inc=16)
                r.op("sync", "dma_start", vfs[:], v_dram[h], sem=usem,
                     inc=16)
                u_ld = usem.n
                for j in range(NT):
                    ps = j % 2
                    r.wait("tensor", usem, u_ld)
                    gwait("tensor", f"q{ps}")
                    tr = r.op("tensor", "transpose", vtp[ps],
                              vfs[:, j * 128:(j + 1) * 128], identb[:],
                              sem=sT)
                    r.wait("vector", sT, tr)
                    if j == 0:
                        r.wait("vector", sT2, prev_unit_av)
                    evv = r.op("vector", "tensor_copy",
                               vsb[:, j * 128:(j + 1) * 128], vtp[ps],
                               sem=sV2)
                    gset(f"q{ps}", (sV2, evv))
                pend_tail = None
                nmcol_done = [0] * NQT

                def do_tail(G, par, t, exp_done):
                    r.wait("vector", sA, exp_done)
                    r.op("vector", "tensor_reduce", negm[par][:],
                         nmstrip[par][:, 0:G], AX.X, ALU.min, sem=sV)
                    r.op("vector", "tensor_tensor", dstrip[par][:, 0:G],
                         bcast(negm[par], G), nmstrip[par][:, 0:G],
                         ALU.subtract, sem=sV)
                    r.wait("scalar", sV, sV.n)
                    ee = r.op("scalar", "activation", estrip[par][:, 0:G],
                              dstrip[par][:, 0:G], AF.Exp, sem=sA)
                    r.wait("vector", sA, ee)
                    r.op("vector", "tensor_tensor", estrip[par][:, 0:G],
                         estrip[par][:, 0:G], sstrip[par][:, 0:G], ALU.mult,
                         sem=sV)
                    sm = r.op("vector", "tensor_reduce", ssum[par][:],
                              estrip[par][:, 0:G], AX.X, ALU.add, sem=sV)
                    r.wait("scalar", sV, sm)
                    lnv = r.op("scalar", "activation", lnsc[par][:],
                               ssum[par][:], AF.Ln, sem=sA)
                    exp1_hist.append(lnv)
                    r.wait("vector", sA, lnv)
                    nmc = r.op("vector", "tensor_tensor", nmcols[:, t:t + 1],
                               negm[par][:], lnsc[par][:], ALU.subtract,
                               sem=sV)
                    nmcol_done[t] = nmc
                    pump(1)

                pa_slots = [qkps[0], qkps[1], stps[0], stps[1]]
                pa_keys = ["q0", "q1", "s0", "s1"]
                for t in range(NQT):
                    G = t // 4 + 1
                    rr = t % 4
                    par = t % 2
                    for j in range(G):
                        js = j % 4
                        pslot = pa_slots[js]
                        pkey = pa_keys[js]
                        r.wait("tensor", usem, u_ld)
                        gwait("tensor", pkey)
                        mm = r.op("tensor", "matmul", pslot[:],
                                  qsl[:, t * 128:(t + 1) * 128],
                                  qsl[:, j * 512:(j + 1) * 512], start=True,
                                  stop=True, sem=sT)
                        if j == G - 1:
                            r.wait("vector", sT, mm)
                            r.wait("vector", sL, mask_ld)
                            r.wait("vector", sA, sA.n)
                            bv = r.op("vector", "tensor_tensor", bnd[par][:],
                                      pslot[:],
                                      maskq[:, rr * 512:(rr + 1) * 512],
                                      ALU.add, sem=sV)
                            gset(pkey, (sV, bv))
                            r.op("vector", "tensor_reduce",
                                 nmstrip[par][:, j:j + 1], bnd[par][:],
                                 AX.X, ALU.max, negate=True, sem=sV)
                            r.wait("scalar", sV, sV.n)
                            ex = r.op("scalar", "activation",
                                      edump[:, (t % 4) * 1024:
                                            (t % 4) * 1024 + 512],
                                      bnd[par][:], AF.Exp,
                                      bias=nmstrip[par][:, j:j + 1],
                                      accum_out=sstrip[par][:, j:j + 1],
                                      sem=sA)
                        else:
                            r.wait("vector", sT, mm)
                            rd = r.op("vector", "tensor_reduce",
                                      nmstrip[par][:, j:j + 1], pslot[:],
                                      AX.X, ALU.max, negate=True, sem=sV)
                            r.wait("scalar", sV, rd)
                            ex = r.op("scalar", "activation",
                                      edump[:, (t % 4) * 1024 +
                                            (j % 2) * 512:
                                            (t % 4) * 1024 +
                                            (j % 2) * 512 + 512],
                                      pslot[:], AF.Exp,
                                      bias=nmstrip[par][:, j:j + 1],
                                      accum_out=sstrip[par][:, j:j + 1],
                                      sem=sA)
                            gset(pkey, (sA, ex))
                    exp_done = sA.n
                    if pend_tail is not None:
                        do_tail(*pend_tail)
                    pend_tail = (G, t % 2, t, exp_done)
                if pend_tail is not None:
                    do_tail(*pend_tail)
                    pend_tail = None
                bpl = [bplain, bplain2]
                bmk = [bmask, bmask2]
                bev_as = [0, 0]
                bev_vs = [0, 0]

                def build_b(g):
                    gwait("tensor", "q0")
                    for rr2 in range(4):
                        j = 4 * g + rr2
                        r.wait("tensor", sV, nmcol_done[j])
                        if rr2:
                            r.wait("tensor", sA, sA.n)
                        tr = r.op("tensor", "transpose", nmps,
                                  nmcols[:, j:j + 1], identf[:], sem=sT)
                        r.wait("scalar", sT, tr)
                        r.op("scalar", "copy",
                             nmev[0:1, rr2 * 128:(rr2 + 1) * 128], nmps,
                             sem=sA)
                    r.wait("tensor", sA, sA.n)
                    for rr2 in range(4):
                        r.op("tensor", "matmul",
                             bps[:, rr2 * 128:(rr2 + 1) * 128],
                             onesf,
                             nmev[0:1, rr2 * 128:(rr2 + 1) * 128],
                             start=True, stop=True,
                             sem=(sT if rr2 == 3 else None))
                    bmm = sT.n
                    r.wait("scalar", sT, bmm)
                    ba = r.op("scalar", "copy", bpl[g % 2][:], bps[:],
                              sem=sA)
                    r.wait("vector", sT, bmm)
                    r.wait("vector", sL, mask_ld)
                    r.wait("vector", sA, ba)
                    for rr2 in range(4):
                        r.op("vector", "tensor_tensor", bmk[g % 2][rr2][:],
                             bps[:], maskt[:, rr2 * 512:(rr2 + 1) * 512],
                             ALU.add, sem=(sV if rr2 == 3 else None))
                    bev_as[g % 2] = ba
                    bev_vs[g % 2] = sV.n
                    gset("q0", (sV, sV.n))

                build_b(0)
                for g in range(NQG):
                    nk = 4 * (g + 1)
                    gav = []
                    bev_a = bev_as[g % 2]
                    bev_v = bev_vs[g % 2]
                    bplain_g = bpl[g % 2]
                    bmask_g = bmk[g % 2]
                    sub_hist = []
                    exp_hist = []
                    stl = [stps[0], stps[1], qkps[1]]
                    for jk in range(nk):
                        p3 = jk % 3
                        if jk >= 3:
                            r.wait("tensor", sV, sub_hist[jk - 3])
                        elif g == 0:
                            gwait("tensor", ["s0", "s1", "q1"][jk])
                        qkmm = r.op("tensor", "matmul", stl[p3][:],
                                    qsl[:, jk * 128:(jk + 1) * 128],
                                    qsl[:, g * 512:(g + 1) * 512],
                                    start=True, stop=True, sem=sT)
                        if jk >= 2:
                            r.wait("tensor", sA, exp_hist[jk - 2])
                            if jk == 2:
                                gwait("tensor", "av")
                            av = r.op("tensor", "matmul", avps[:],
                                      vsb[:, (jk - 2) * 128:
                                          (jk - 1) * 128],
                                      pTb[(jk - 2) % 3][:],
                                      start=(jk - 2 == 0), stop=False,
                                      skip_group_check=True, sem=sT2)
                            gav.append(av)
                        r.wait("vector", sT, qkmm)
                        if jk < 4 * g:
                            r.wait("vector", sA, bev_a)
                            bsrc = bplain_g
                        else:
                            r.wait("vector", sV, bev_v)
                            bsrc = bmask_g[jk - 4 * g]
                        if jk >= 3:
                            r.wait("vector", sA, exp_hist[jk - 3])
                        sb = r.op("vector", "tensor_tensor", sadj[p3][:],
                                  stl[p3][:], bsrc[:], ALU.add, sem=sV)
                        sub_hist.append(sb)
                        r.wait("scalar", sV, sb)
                        if jk >= 3:
                            r.wait("scalar", sT2,
                                   gav[jk - 3] if jk - 3 < len(gav)
                                   else sT2.n)
                        ex = r.op("scalar", "activation", pTb[p3][:],
                                  sadj[p3][:], AF.Exp, sem=sA)
                        exp_hist.append(ex)
                    for jj in (nk - 2, nk - 1):
                        if jj < 0:
                            continue
                        r.wait("tensor", sA, exp_hist[jj])
                        gwait("tensor", "av")
                        av = r.op("tensor", "matmul", avps[:],
                                  vsb[:, jj * 128:(jj + 1) * 128],
                                  pTb[jj % 3][:],
                                  start=(jj == 0), stop=(jj == nk - 1),
                                  skip_group_check=True, sem=sT2)
                        gav.append(av)
                    stv = (sV, sub_hist[-1] if nk >= 3 else sV.n)
                    gset("q1", stv)
                    gset("s0", stv)
                    gset("s1", stv)
                    if True:
                        pass
                    if g + 1 < NQG:
                        build_b(g + 1)
                    r.wait("scalar", sT2, gav[-1])
                    gs = g % 2
                    r.wait("scalar", sSpO[gs], oTg_free[gs])
                    oe = r.op("scalar", "activation", oTg[gs][:], avps[:],
                              AF.Copy, sem=sA2)
                    gset("av", (sA2, oe))
                    r.wait("tensor", sA2, oe)
                    r.wait("sync", sA2, oe)
                    r.op("sync", "dma_start",
                         o_cc_in[h * 128:(h + 1) * 128,
                                 g * 512:(g + 1) * 512],
                         oTg[gs][:], sem=sSpO[gs], inc=16)
                    oTg_free[gs] = sSpO[gs].n
                    pump(1)
                    q_w2f(8 * h + g)
                    if g >= 6:
                        q_wo(2 * h + (g - 6))
                pump(99)
                unit_done.append(sA2.n)
                prev_unit_av = sT2.n

            for h in range(NH):
                emit_unit(h)

            r.wait("gpsimd", sSpO[0], sSpO[0].n)
            r.wait("gpsimd", sSpO[1], sSpO[1].n)

            def do_cc(gp):
                gp.collective_compute(
                    "AllGather", ALU.bypass,
                    replica_groups=[[0, 1], [2, 3], [4, 5], [6, 7]],
                    ins=[o_cc_in.ap().opt()], outs=[o_cc_out.ap().opt()],
                ).then_inc(sCC.h, 1)
            sCC.n += 1
            r.call("gpsimd", do_cc)
            for s_ in (sSp, sSpS1, sSpW2, sSpWo, sL, sLxn, sLwf, sLw2m,
                       sLwom, sLu[0], sLu[1], sSpO[0], sSpO[1]):
                r.wait("sync", s_, s_.n)
            r.flush(nc)
        b5_spill_state = [(s_, s_.n) for s_ in
                          (sSp, sSpS1, sSpW2, sSpWo, sSpO[0], sSpO[1])]

        if "attn" in dbg:
            o = dbg_out("dbg_oT", [NH * 128, S], BF16)
            r = Rec()
            r.wait("sync", sSp, sSp.n)
            r.op("sync", "dma_start", o[:], o_cc_in[:], sem=sOut, inc=16)
            r.wait("sync", sOut, sOut.n)
            r.flush(nc)
        if upto == "B4":
            return nc, dbg_outs

        if "ffnup" in dbg:
            o = dbg_out("dbg_s1", [FH * 128, TOWN], BF16)
            o2 = dbg_out("dbg_w2f", [ND * FH * 128, 128], BF16)
            r = Rec()
            r.wait("sync", sSp, sSp.n)
            for hi in range(FH):
                r.op("sync", "dma_start", o[hi * 128:(hi + 1) * 128, :],
                     s1_dram[hi], sem=sOut, inc=16)
            r.op("sync", "dma_start", o2.ap(),
                 w2f_dram.ap().rearrange("m h p d -> (m h p) d"), sem=sOut,
                 inc=16)
            r.wait("sync", sOut, sOut.n)
            r.flush(nc)
        if upto == "B5":
            return nc, dbg_outs

        with ExitStack() as c6:
            e6 = c6.enter_context
            acc = e6(nc.sbuf_tensor("acc", [128, ND * 512], F32))
            ogst = e6(nc.sbuf_tensor("ogst", [128, ND * 512], BF16))
            s1t = e6(nc.sbuf_tensor("s1t", [128, FH * 512], BF16))
            w2fst2 = [e6(nc.sbuf_tensor(f"w2fst{i}", [128, FH * 128], BF16))
                      for i in range(2)]
            wost2 = [e6(nc.sbuf_tensor(f"wost{i}", [128, D], BF16))
                     for i in range(2)]
            xnst = e6(nc.sbuf_tensor("xnst", [128, ND * 512], BF16))
            ops_ = [e6(nc.psum_tensor(f"ops{i}", [128, 512], F32))
                    for i in range(2)]
            dps = [e6(nc.psum_tensor(f"dps{i}", [128, 512], F32))
                   for i in range(2)]

            r = Rec()
            for s_, v_ in b5_spill_state:
                r.wait("sync", s_, v_)
            out_hist = 0
            oev_hist = []
            dev_hist = []
            wo_mms = []
            w2_mms = []
            s1_lds = []
            init_done = 0

            def load_s1(t4):
                nonlocal s1_lds
                if w2_mms:
                    r.wait("sync", sT2, w2_mms[-1])
                s1_lds = []
                for c in range(4):
                    r.op("sync", "dma_start",
                         bass.AP(s1t, c * 16 * 512,
                                 [[FH * 512, 128], [512, 16], [1, 512]]),
                         bass.AP(s1_dram, c * 16 * 128 * TOWN + t4 * 512,
                                 [[TOWN, 128], [128 * TOWN, 16], [1, 512]]),
                         sem=sLs1c[c], inc=16)
                    s1_lds.append(sLs1c[c].n)

            load_s1(0)
            for t4 in range(TOWN // 512):
                if t4 > 0:
                    r.wait("sync", sOut, out_hist)
                r.op("sync", "dma_start",
                     bass.AP(acc, 0, [[ND * 512, 128], [512, ND], [1, 512]]),
                     bass.AP(x_own_tr, t4 * 512,
                             [[TOWN, 128], [128 * TOWN, ND], [1, 512]]),
                     sem=sLb6, inc=16)
                if t4 > 0:
                    r.wait("sync", sV, init_done)
                r.op("sync", "dma_start",
                     bass.AP(xnst, 0, [[ND * 512, 128], [512, ND], [1, 512]]),
                     bass.AP(xnT_sp, t4 * 512,
                             [[ND * TOWN, 128], [TOWN, ND], [1, 512]]),
                     sem=sLb6, inc=16)
                base_ld = sLb6.n

                def og_load(gp, t4=t4):
                    with gp.register(f"rgo{t4}") as rg:
                        gp.reg_load(rg, coloff)
                        if t4:
                            gp.reg_add(rg, rg, t4 * 512)
                        gp.dma_start(
                            bass.AP(ogst, 0, [[ND * 512, 128], [512, ND],
                                              [1, 512]]),
                            bass.AP(o_cc_out, rg,
                                    [[S, 128], [128 * S, ND], [1, 512]])
                        ).then_inc(sLog.h, 16)
                if t4 > 0:
                    r.wait("gpsimd", sT, wo_mms[-1])
                r.wait("gpsimd", sCC, sCC.n)
                r.call("gpsimd", og_load)
                sLog.n += 16
                og_ld = sLog.n

                r.wait("vector", sLb6, base_ld)
                for k in range(ND):
                    r.op("vector", "tensor_tensor",
                         acc[:, k * 512:(k + 1) * 512],
                         acc[:, k * 512:(k + 1) * 512],
                         bcast(wobias, 512, k), ALU.add)
                init_done = r.op("vector", "tensor_tensor", acc[:], acc[:],
                                 xnst[:], ALU.add, sem=sV)

                my_s1 = list(s1_lds)
                for m in range(ND):
                    sl = m % 2
                    if len(w2_mms) >= 2:
                        r.wait("sync", sT2, w2_mms[-2])
                    r.op("sync", "dma_start",
                         bass.AP(w2fst2[sl], 0, [[FH * 128, 128], [128, FH],
                                                 [1, 128]]),
                         bass.AP(w2f_dram, m * FH * 128 * 128,
                                 [[128, 128], [128 * 128, FH], [1, 128]]),
                         sem=sLw26[sl], inc=16)
                    w2_ld = sLw26[sl].n
                    slot = dps[m % 2]
                    r.wait("tensor", sLw26[sl], w2_ld)
                    if len(dev_hist) >= 2:
                        r.wait("tensor", sV2, dev_hist[-2])
                    for hi in range(FH):
                        if m == 0 and hi % 16 == 0:
                            r.wait("tensor", sLs1c[hi // 16],
                                   my_s1[hi // 16])
                        r.op("tensor", "matmul", slot[:],
                             w2fst2[sl][:, hi * 128:(hi + 1) * 128],
                             s1t[:, hi * 512:(hi + 1) * 512],
                             start=(hi == 0), stop=(hi == FH - 1),
                             sem=(sT2 if hi == FH - 1 else None))
                    w2_mms.append(sT2.n)
                    r.wait("vector", sT2, sT2.n)
                    if m == 0:
                        r.wait("vector", sV, init_done)
                    dv_ = r.op("vector", "tensor_tensor",
                               acc[:, m * 512:(m + 1) * 512], slot[:],
                               acc[:, m * 512:(m + 1) * 512], ALU.add,
                               sem=sV2)
                    dev_hist.append(dv_)

                if t4 + 1 < TOWN // 512:
                    load_s1(t4 + 1)

                for m in range(ND):
                    sl = m % 2
                    if len(wo_mms) >= 2:
                        r.wait("sync", sT, wo_mms[-2])
                    r.op("sync", "dma_start", wost2[sl][:],
                         bass.AP(wob_dram, m * 128,
                                 [[D, 128], [128 * D, ND], [1, 128]]),
                         sem=sLwo6[sl], inc=16)
                    wo_ld = sLwo6[sl].n
                    slot = ops_[m % 2]
                    r.wait("tensor", sLwo6[sl], wo_ld)
                    if m == 0:
                        r.wait("tensor", sLog, og_ld)
                    if len(oev_hist) >= 2:
                        r.wait("tensor", sV, oev_hist[-2])
                    for k in range(ND):
                        r.op("tensor", "matmul", slot[:],
                             wost2[sl][:, k * 128:(k + 1) * 128],
                             ogst[:, k * 512:(k + 1) * 512],
                             start=(k == 0), stop=(k == ND - 1),
                             sem=(sT if k == ND - 1 else None))
                    wo_mms.append(sT.n)
                    r.wait("vector", sT, sT.n)
                    ov_ = r.op("vector", "tensor_tensor",
                               acc[:, m * 512:(m + 1) * 512], slot[:],
                               acc[:, m * 512:(m + 1) * 512], ALU.add, sem=sV)
                    oev_hist.append(ov_)

                r.wait("gpsimd", sV, sV.n)
                r.wait("gpsimd", sV2, sV2.n)
                r.op("gpsimd", "dma_start",
                     bass.AP(out_t, t4 * 512,
                             [[TOWN, 128], [128 * TOWN, ND], [1, 512]]),
                     bass.AP(acc, 0, [[ND * 512, 128], [512, ND], [1, 512]]),
                     sem=sOut, inc=16)
                out_hist = sOut.n
            r.wait("sync", sOut, sOut.n)
            r.flush(nc)

    return nc, dbg_outs


def _hadamard(n):
    h = np.array([[1.0]])
    while h.shape[0] < n:
        h = np.block([[h, h], [h, -h]])
    return h


def make_core_inputs(inputs, core):
    X = np.asarray(inputs["L0_attn_X"])
    Wq = np.asarray(inputs["L0_attn_Wq_w"])
    Wqb = np.asarray(inputs["L0_attn_Wq_b"])
    Wo = np.asarray(inputs["L0_attn_Wo_w"])
    Wob = np.asarray(inputs["L0_attn_Wo_b"])
    g = np.asarray(inputs["L0_attn_rms_w"])
    W1 = np.asarray(inputs["L0_ffn_W1"])
    W2 = np.asarray(inputs["L0_ffn_W2"])
    W3 = np.asarray(inputs["L0_ffn_W3"])

    s, half = core // 2, core % 2
    ho = half * NH
    xs = np.ascontiguousarray(X[s], dtype=np.float32)
    x_own = np.ascontiguousarray(xs[half * TOWN:(half + 1) * TOWN])
    d = {
        "x_full": xs,
        "x_own": x_own,
        "x_own_tr": np.ascontiguousarray(x_own.T),
        "wq_t": np.ascontiguousarray(Wq[ho * DK:(ho + NH) * DK, :].T),
        "wq_bh": np.ascontiguousarray(
            Wqb[ho * DK:(ho + NH) * DK].reshape(NH, 128).T),
        "rms_gc": np.ascontiguousarray(g.reshape(ND, 128).T),
        "wo_t": np.ascontiguousarray(Wo.T),
        "wo_bc": np.ascontiguousarray(Wob.reshape(ND, 128).T),
        "w1_t": np.ascontiguousarray(W1.T),
        "w3_t": np.ascontiguousarray(W3.T),
        "coloff_i": np.array([[half * TOWN]], dtype=np.int32),
    }
    T = np.ascontiguousarray(W2.T).reshape(FH, 128, D)
    T2 = T.reshape(FH, 128 // GP, GP, D)
    W2s = np.ascontiguousarray(T2.transpose(2, 0, 1, 3)).reshape(
        128, (128 // GP) * D)
    d["w2_s"] = W2s

    h128 = _hadamard(128)
    h64 = _hadamard(FH)
    bd = np.zeros((128, 128), dtype=np.float64)
    for gi in range(GP):
        bd[gi * FH:(gi + 1) * FH, gi * FH:(gi + 1) * FH] = h64
    d["h128_i"] = h128.astype(BF)
    d["bd_i"] = bd.astype(BF)
    d["identb_i"] = np.eye(128).astype(BF)
    d["identf_i"] = np.eye(128, dtype=np.float32)
    d["onesf_i"] = np.ones((1, 128), dtype=np.float32)
    qi = np.arange(128)[:, None]
    ck = np.arange(512)[None, :]
    mq = np.stack([np.where(ck <= 128 * rr + qi, 0.0, NEG_BIG)
                   for rr in range(4)]).astype(np.float32)
    mt = np.stack([np.where(ck >= 128 * rr + qi, 0.0, NEG_BIG)
                   for rr in range(4)]).astype(np.float32)
    d["maskq_i"] = np.ascontiguousarray(mq.reshape(4 * 128, 512))
    d["maskt_i"] = np.ascontiguousarray(mt.reshape(4 * 128, 512))
    return d


_BUILD_CACHE = {}


def kernel(**inputs):
    if "full" not in _BUILD_CACHE:
        _BUILD_CACHE["full"] = build_program()
    nc, _ = _BUILD_CACHE["full"]
    in_maps = [make_core_inputs(inputs, c) for c in range(8)]
    res = run_bass_kernel_spmd(nc, in_maps, core_ids=list(range(8)))
    out = np.empty((4, S, D), dtype=np.float32)
    for c in range(8):
        s, half = c // 2, c % 2
        ot = np.asarray(res.results[c]["out_t"])
        out[s, half * TOWN:(half + 1) * TOWN, :] = ot.T
    return out
